# version 05
# baseline (speedup 1.0000x reference)
import numpy as np
import ml_dtypes

import concourse.bass as bass
import concourse.mybir as mybir
from concourse.bass_utils import run_bass_kernel_spmd
from concourse.aot_env import allow_ici_all_to_all

F32 = mybir.dt.float32
BF16 = mybir.dt.bfloat16
I32 = mybir.dt.int32
AF = mybir.ActivationFunctionType
ALU = mybir.AluOpType
AX = mybir.AxisListType

B, S, D = 4, 4096, 1024
H, HD = 16, 64
INNER = 4096
TOK = 2048
NEG = -1.0e30
N_BLOCKS = 4


class Ctr:
    def __init__(self, sem):
        self.sem = sem
        self.v = 0

    def inc(self, n=1):
        self.v += n
        return self.v


class DmaStream:

    def __init__(self, a, b):
        self.sems = [a, b]
        self.k = 0

    def issue(self, dma):
        s = self.sems[self.k % 2]
        dma.then_inc(s.sem, 16)
        s.v += 16
        tok = (self.k % 2, s.v)
        self.k += 1
        return tok

    def wait(self, eng, tok):
        par, val = tok
        eng.wait_ge(self.sems[par].sem, val)


def build_nc(n_blocks=N_BLOCKS):
    nc = bass.Bass("TRN2", target_bir_lowering=False, debug=False)

    x_in = nc.dram_tensor("x", [TOK, D], F32, kind="ExternalInput").ap()
    wq_h = nc.dram_tensor("wq", [D, D], F32, kind="ExternalInput").ap()
    wk_h = nc.dram_tensor("wk", [D, D], F32, kind="ExternalInput").ap()
    wv_h = nc.dram_tensor("wv", [D, D], F32, kind="ExternalInput").ap()
    wp_h = nc.dram_tensor("wp", [D, D], F32, kind="ExternalInput").ap()
    bp_h = nc.dram_tensor("bp", [1, D], F32, kind="ExternalInput").ap()
    wkqv_h = nc.dram_tensor("wkqv", [D, 3 * D], F32, kind="ExternalInput").ap()
    m_h = nc.dram_tensor("m", [1, H], F32, kind="ExternalInput").ap()
    mmy_h = nc.dram_tensor("m_my", [1, 2], F32, kind="ExternalInput").ap()
    wg_h = nc.dram_tensor("wg", [INNER, D], F32, kind="ExternalInput").ap()
    wu_h = nc.dram_tensor("wu", [INNER, D], F32, kind="ExternalInput").ap()
    wd_h = nc.dram_tensor("wd", [D, INNER], F32, kind="ExternalInput").ap()
    ident_h = nc.dram_tensor("ident", [128, 128], BF16, kind="ExternalInput").ap()
    identf_h = nc.dram_tensor("identf_in", [128, 128], F32, kind="ExternalInput").ap()
    iotaq_h = nc.dram_tensor("iota_q", [1, S], F32, kind="ExternalInput").ap()
    iotak_h = nc.dram_tensor("iota_k", [128, 32], F32, kind="ExternalInput").ap()
    masks_h = nc.dram_tensor("masks", [4, 128, 512], BF16, kind="ExternalInput").ap()
    y_out = nc.dram_tensor("y", [TOK, D], F32, kind="ExternalOutput").ap()

    resid_h = nc.dram_tensor("resid", [TOK, D], F32)
    resid2_h = nc.dram_tensor("resid2", [TOK, D], F32)
    kv_ar_in = nc.dram_tensor("kv_ar_in", [D, D], F32)
    kv_ar_out = nc.dram_tensor("kv_ar_out", [D, D], F32)
    wgT_hbm = nc.dram_tensor("wgT_hbm", [8, 128, INNER], BF16)
    wuT_hbm = nc.dram_tensor("wuT_hbm", [8, 128, INNER], BF16)
    wdT_hbm = nc.dram_tensor("wdT_hbm", [32, 128, D], BF16)
    gu_hbm = nc.dram_tensor("gu_hbm", [32, 128, TOK], F32)
    kqvb_hbm = nc.dram_tensor("kqvb_hbm", [D, 3 * D], BF16)
    a2a_in = nc.dram_tensor("a2a_in", [8, 128, 6144], BF16)
    a2a_out = nc.dram_tensor("a2a_out", [8, 128, 6144], BF16)
    oa_in = nc.dram_tensor("oa_in", [8, 128, 2048], BF16)
    oa_out = nc.dram_tensor("oa_out", [8, 128, 2048], BF16)
    o_scr = nc.dram_tensor("o_scr", [64, 64, 512], F32)
    row_hbm = nc.dram_tensor("row_hbm", [16, 128], F32)

    from contextlib import ExitStack
    with ExitStack() as stx:
        sbt = lambda name, shape, dt: stx.enter_context(nc.sbuf_tensor(name, shape, dt))
        pst = lambda name, shape, dt=F32: stx.enter_context(nc.psum_tensor(name, shape, dt))
        mksem = lambda name: Ctr(stx.enter_context(nc.semaphore(name)))

        A0 = sbt("A0", [128, 16384], BF16)
        A1 = sbt("A1", [128, 16384], BF16)
        A2 = sbt("A2", [128, 16640], BF16)
        A3 = sbt("A3", [128, 16384], BF16)
        FB = sbt("FB", [128, 8192], F32)
        WKB = sbt("WKB", [128, 2048], BF16)
        ROWF = sbt("ROWF", [128, 2560], F32)
        ROWB = sbt("ROWB", [128, 8192], BF16)
        masksb = sbt("masksb", [128, 2048], BF16)
        ptb = sbt("ptb", [128, 2048], BF16)
        nrows = sbt("nrows", [64, 512], F32)
        rtmp = sbt("rtmp", [16, 128], F32)
        stat = sbt("stat", [128, 96], F32)
        identb = sbt("identb", [128, 128], BF16)
        identf = sbt("identf", [128, 128], F32)
        iotak = sbt("iotak", [128, 32], F32)
        mjc = sbt("mjc", [128, 64], F32)
        onesbf = sbt("onesbf", [1, 128], BF16)
        onesff = sbt("onesff", [1, 128], F32)
        mrow = sbt("mrow", [1, 32], F32)
        qi32 = sbt("qi32", [128, 1024], I32)

        p0 = pst("p0", [128, 512])
        p1 = pst("p1", [128, 512])
        p2 = pst("p2", [128, 512])
        p3 = pst("p3", [128, 512])
        p4 = pst("p4", [128, 512])
        p5 = pst("p5", [128, 512])
        p6 = pst("p6", [128, 512])
        p7 = pst("p7", [128, 512])
        PSUMS = [p0, p1, p2, p3, p4, p5, p6, p7]

        s_ld = mksem("s_ld")
        s_la = mksem("s_la")
        s_lb = mksem("s_lb")
        s_tr = mksem("s_tr")
        s_pe = mksem("s_pe")
        s_ve = mksem("s_ve")
        s_se = mksem("s_se")
        s_sa = mksem("s_sa")
        s_sb = mksem("s_sb")
        s_sd = mksem("s_sd")
        s_gd = mksem("s_gd")
        s_ga = mksem("s_ga")
        s_gb = mksem("s_gb")
        s_cc = mksem("s_cc")
        s_out = mksem("s_out")

        ld_stream = DmaStream(s_la, s_lb)
        st_stream = DmaStream(s_sa, s_sb)
        gd_stream = DmaStream(s_ga, s_gb)

        prog = {k: [] for k in ("sync", "vector", "scalar", "tensor", "gpsimd")}

        def em(eng, fn):
            prog[eng].append(fn)

        def ve(fn):
            def wrap(vector, fn=fn):
                fn(vector)
                vector.drain()
                vector.nop().then_inc(s_ve.sem, 1)
            em("vector", wrap)
            return s_ve.inc()

        def se(fn):
            def wrap(scalar, fn=fn):
                fn(scalar)
                scalar.drain()
                scalar.nop().then_inc(s_se.sem, 1)
            em("scalar", wrap)
            return s_se.inc()

        def mm_phase(groups, pool, pe_waits=(), width=512):
            npool = len(pool)
            pe_base = s_pe.v
            ev_base = s_ve.v
            waits = list(pe_waits)

            def t_ops(tensor, groups=groups, pool=pool, pe_base=pe_base,
                      ev_base=ev_base, waits=waits, npool=npool, width=width):
                for semh, val in waits:
                    tensor.wait_ge(semh, val)
                for g, (pieces, _e) in enumerate(groups):
                    pp = pool[g % npool]
                    if g >= npool:
                        tensor.wait_ge(s_ve.sem, ev_base + g - npool + 1)
                    n = len(pieces)
                    for i, (lt, rr) in enumerate(pieces):
                        mm = tensor.matmul(pp[:, 0:width], lt, rr,
                                           start=(i == 0), stop=(i == n - 1))
                        if i == n - 1:
                            mm.then_inc(s_pe.sem, 1)
            em("tensor", t_ops)
            s_pe.v += len(groups)

            def v_ops(vector, groups=groups, pool=pool, pe_base=pe_base, npool=npool):
                for g, (_p, evf) in enumerate(groups):
                    pp = pool[g % npool]
                    vector.wait_ge(s_pe.sem, pe_base + g + 1)
                    evf(vector, pp)
                    vector.drain()
                    vector.nop().then_inc(s_ve.sem, 1)
            em("vector", v_ops)
            s_ve.v += len(groups)
            return s_ve.v

        def copy_evac(dst, width=512):
            def evf(vector, pp, dst=dst, width=width):
                vector.tensor_copy(out=dst, in_=pp[:, 0:width])
            return evf

        def phase_gate(toks=()):
            vev, sev = s_ve.v, s_se.v
            tks = list(toks)

            def g(sync, vev=vev, sev=sev, tks=tks):
                sync.wait_ge(s_ve.sem, vev)
                sync.wait_ge(s_se.sem, sev)
                for t in tks:
                    swait(sync, st_stream, t)
            em("sync", g)

        def c_ld(sync):
            sync.dma_start(out=identb[:], in_=ident_h[:, :]).then_inc(s_ld.sem, 16)
            sync.dma_start(out=identf[:], in_=identf_h[:, :]).then_inc(s_ld.sem, 16)
            sync.dma_start(out=iotak[:], in_=iotak_h[:, :]).then_inc(s_ld.sem, 16)
            sync.dma_start(out=masksb[:].rearrange("p (a q) -> p a q", a=4), in_=masks_h.rearrange("a p q -> p a q")).then_inc(s_ld.sem, 16)
            sync.dma_start(out=mrow[0:1, 0:H], in_=m_h[:, :]).then_inc(s_ld.sem, 16)
            sync.dma_start(out=mrow[0:1, 24:26], in_=mmy_h[:, :]).then_inc(s_ld.sem, 16)
            sync.dma_start(out=ROWF[0:1, 0:D], in_=bp_h[:, :]).then_inc(s_ld.sem, 16)
        em("sync", c_ld)
        s_ld.v += 112
        consts_ld = s_ld.v

        def c_ones(vector):
            vector.wait_ge(s_ld.sem, consts_ld)
            vector.memset(onesbf[:], 1.0)
            vector.memset(onesff[:], 1.0)
            vector.drain()
            vector.tensor_copy(out=ROWB[0:1, 0:D], in_=ROWF[0:1, 0:D])
        consts_done = ve(c_ones)

        def stream_dma(emit_fn, stream):
            par = stream.k % 2
            sem = stream.sems[par]
            val = sem.v + 16

            def wrap(eng, emit_fn=emit_fn, sem=sem, val=val):
                if val > 16:
                    eng.wait_ge(sem.sem, val - 16)
                emit_fn(eng).then_inc(sem.sem, 16)
            stream.k += 1
            sem.v = val
            return wrap, (par, val)

        def swait(eng, stream, tok):
            par, val = tok
            eng.wait_ge(stream.sems[par].sem, val)

        def ln_half2(src, th, dst_ap_fn, quant=False):
            entry_ve, entry_se = s_ve.v, s_se.v
            for tc8 in range(8):
                tc = th * 8 + tc8
                buf = (tc8 % 2) * 1024
                done_prev = (s_ve.v if quant else s_se.v)

                def mk_ld(sync, tc=tc, buf=buf):
                    return sync.dma_start(out=FB[:, buf:buf + 1024],
                                          in_=src[tc * 128:(tc + 1) * 128, :])
                wrap, tok = stream_dma(mk_ld, ld_stream)

                def ldx(sync, wrap=wrap, tc8=tc8, done_prev=done_prev, quant=quant,
                        entry_ve=entry_ve, entry_se=entry_se):
                    if tc8 < 2:
                        sync.wait_ge(s_ve.sem, entry_ve)
                        sync.wait_ge(s_se.sem, entry_se)
                    else:
                        if quant:
                            sync.wait_ge(s_ve.sem, done_prev - 1)
                        else:
                            sync.wait_ge(s_se.sem, done_prev - 1)
                    wrap(sync)
                em("sync", ldx)

                def lsq(scalar, tok=tok, buf=buf):
                    swait(scalar, ld_stream, tok)
                    scalar.activation(out=FB[:, 2048:3072], in_=FB[:, buf:buf + 1024],
                                      func=AF.Square, accum_out=stat[:, 1:2])
                sqd = se(lsq)

                def lstats(vector, tok=tok, buf=buf, sqd=sqd):
                    swait(vector, ld_stream, tok)
                    xt = FB[:, buf:buf + 1024]
                    vector.reduce_sum(out=stat[:, 0:1], in_=xt, axis=AX.X)
                    vector.drain()
                    vector.wait_ge(s_se.sem, sqd)
                    vector.tensor_scalar(out=stat[:, 2:3], in0=stat[:, 0:1],
                                         scalar1=1.0 / D, scalar2=None, op0=ALU.mult)
                    vector.drain()
                    vector.scalar_tensor_tensor(out=stat[:, 3:4], in0=stat[:, 2:3],
                                                scalar=-1.0, in1=stat[:, 2:3],
                                                op0=ALU.mult, op1=ALU.mult)
                    vector.drain()
                    vector.scalar_tensor_tensor(out=stat[:, 3:4], in0=stat[:, 1:2],
                                                scalar=1.0 / D, in1=stat[:, 3:4],
                                                op0=ALU.mult, op1=ALU.add)
                    vector.drain()
                    vector.tensor_scalar(out=stat[:, 3:4], in0=stat[:, 3:4],
                                         scalar1=1e-5, scalar2=None, op0=ALU.add)
                    vector.drain()
                    vector.reciprocal(out=stat[:, 3:4], in_=stat[:, 3:4])
                vst = ve(lstats)

                def lsqrt(scalar, vst=vst):
                    scalar.wait_ge(s_ve.sem, vst)
                    scalar.activation(out=stat[:, 4:5], in_=stat[:, 3:4], func=AF.Sqrt)
                sst = se(lsqrt)

                def lbias(vector, sst=sst):
                    vector.wait_ge(s_se.sem, sst)
                    vector.scalar_tensor_tensor(out=stat[:, 5:6], in0=stat[:, 2:3],
                                                scalar=-1.0, in1=stat[:, 4:5],
                                                op0=ALU.mult, op1=ALU.mult)
                vbs = ve(lbias)

                if not quant:
                    def lap(scalar, tc8=tc8, buf=buf, vbs=vbs):
                        scalar.wait_ge(s_ve.sem, vbs)
                        scalar.activation(out=dst_ap_fn(tc8), in_=FB[:, buf:buf + 1024],
                                          func=AF.Identity,
                                          scale=stat[:, 4:5], bias=stat[:, 5:6])
                    se(lap)
                else:
                    def lap(scalar, tc8=tc8, buf=buf, vbs=vbs):
                        scalar.wait_ge(s_ve.sem, vbs)
                        scalar.activation(out=FB[:, 3072:4096], in_=FB[:, buf:buf + 1024],
                                          func=AF.Identity,
                                          scale=stat[:, 4:5], bias=stat[:, 5:6])
                    sap = se(lap)

                    def qsteps(vector, tc8=tc8, sap=sap):
                        vector.wait_ge(s_se.sem, sap)
                        xln = FB[:, 3072:4096]
                        vector.reduce_max(out=stat[:, 6:7], in_=xln, axis=AX.X,
                                          apply_absolute_value=True)
                        vector.drain()
                        vector.tensor_scalar(out=stat[:, 6:7], in0=stat[:, 6:7],
                                             scalar1=1e-5, scalar2=None, op0=ALU.max)
                        vector.drain()
                        vector.tensor_scalar(out=stat[:, 80 + tc8:81 + tc8],
                                             in0=stat[:, 6:7], scalar1=1.0 / 127.0,
                                             scalar2=None, op0=ALU.mult)
                        vector.drain()
                        vector.reciprocal(out=stat[:, 7:8], in_=stat[:, 6:7])
                        vector.drain()
                        vector.tensor_scalar(out=stat[:, 7:8], in0=stat[:, 7:8],
                                             scalar1=127.0, scalar2=None, op0=ALU.mult)
                        vector.drain()
                        vector.tensor_scalar(out=qi32[:, 0:1024], in0=xln,
                                             scalar1=stat[:, 7:8], scalar2=128.5,
                                             op0=ALU.mult, op1=ALU.add)
                        vector.drain()
                        vector.tensor_scalar(out=dst_ap_fn(tc8), in0=qi32[:, 0:1024],
                                             scalar1=-128.0, scalar2=None, op0=ALU.add)
                    ve(qsteps)
            return s_se.v, s_ve.v

        def prep_wT(whbm, slot_ap_fn):
            for slab in range(4):
                r0 = slab * 256
                trb = s_tr.v

                def mk_ld(gpsimd, whbm=whbm, r0=r0):
                    return gpsimd.dma_start(
                        out=WKB[:, 0:2048].rearrange("p (s d) -> p s d", s=2),
                        in_=whbm[r0:r0 + 256, :].rearrange("(s p) d -> p s d", p=128))
                wrap, tok = stream_dma(mk_ld, gd_stream)

                def ldw(gpsimd, wrap=wrap, trb=trb):
                    gpsimd.wait_ge(s_tr.sem, trb)
                    wrap(gpsimd)
                em("gpsimd", ldw)

                for ss in range(2):
                    e0 = r0 + ss * 128

                    def trw(sync, ss=ss, e0=e0, tok=tok):
                        if ss == 0:
                            swait(sync, gd_stream, tok)
                        sync.dma_start_transpose(
                            slot_ap_fn().rearrange("p (c e) -> p c e", c=8)[:, :, e0:e0 + 128],
                            WKB[:, ss * D:(ss + 1) * D],
                        ).then_inc(s_tr.sem, 16)
                    em("sync", trw)
                    s_tr.inc(16)
            return s_tr.v

        def transpose16(src_ap_fn, dst, gates=()):
            def trg(sync, gates=gates):
                for g in gates:
                    g(sync)
            em("sync", trg)
            for tc in range(16):
                def trx(sync, tc=tc):
                    sync.dma_start_transpose(
                        dst[:].rearrange("p (c t) -> p c t", c=8)[:, :, tc * 128:(tc + 1) * 128],
                        src_ap_fn(tc),
                    ).then_inc(s_tr.sem, 16)
                em("sync", trx)
                s_tr.inc(16)
            return s_tr.v

        def block1():
            A3lo = lambda: A3[:, 0:8192]
            A3hi = lambda: A3[:, 8192:16384]
            prep_wT(wq_h, A3hi)
            wkT_done = prep_wT(wk_h, A3lo)

            lnse0, _ = ln_half2(x_in, 0, lambda tc8: A2[:, tc8 * D:(tc8 + 1) * D])
            lnse1, _ = ln_half2(x_in, 1, lambda tc8: A2[:, 8192 + tc8 * D: 8192 + (tc8 + 1) * D])

            kv_last = None
            for th in range(2):
                kvprev = kv_last
                lnse = (lnse0, lnse1)[th]
                a2base = th * 8192

                def trg(sync, lnse=lnse, wk=wkT_done, th=th, kvprev=kvprev):
                    sync.wait_ge(s_se.sem, lnse)
                    if th == 0:
                        sync.wait_ge(s_tr.sem, wk)
                    else:
                        sync.wait_ge(s_ve.sem, kvprev)
                em("sync", trg)
                for tc8 in range(8):
                    def trx(sync, tc8=tc8, a2base=a2base):
                        sync.dma_start_transpose(
                            A0[:, 0:8192].rearrange("p (c t) -> p c t", c=8)[:, :, tc8 * 128:(tc8 + 1) * 128],
                            A2[:, a2base + tc8 * D: a2base + (tc8 + 1) * D],
                        ).then_inc(s_tr.sem, 16)
                    em("sync", trx)
                    s_tr.inc(16)
                xT_done = s_tr.v

                groups = []
                for ec in range(8):
                    for tq in range(2):
                        pieces = [(A3[:, 8192 + c * D + ec * 128: 8192 + c * D + ec * 128 + 128],
                                   A0[:, c * 1024 + tq * 512: c * 1024 + tq * 512 + 512])
                                  for c in range(8)]
                        dst = A1[:, ec * 2048 + th * 1024 + tq * 512:
                                 ec * 2048 + th * 1024 + tq * 512 + 512]
                        groups.append((pieces, copy_evac(dst)))
                q_done = mm_phase(groups, [p0, p1],
                                  pe_waits=[(s_tr.sem, xT_done), (s_ve.sem, consts_done)])

                groups = []
                for tc8 in range(8):
                    for half in range(2):
                        pieces = [(A0[:, c * 1024 + tc8 * 128: c * 1024 + tc8 * 128 + 128],
                                   A3[:, c * D + half * 512: c * D + half * 512 + 512])
                                  for c in range(8)]
                        dst = A0[:, 8192 + tc8 * 1024 + half * 512:
                                 8192 + tc8 * 1024 + half * 512 + 512]
                        groups.append((pieces, copy_evac(dst)))
                k_done = mm_phase(groups, [p2, p3])

                def wv_gate(sync, q_done=q_done):
                    sync.wait_ge(s_ve.sem, q_done)
                em("sync", wv_gate)
                wvT_done = prep_wT(wv_h, A3hi)

                groups = []
                for tc8 in range(8):
                    for half in range(2):
                        pieces = [(A0[:, c * 1024 + tc8 * 128: c * 1024 + tc8 * 128 + 128],
                                   A3[:, 8192 + c * D + half * 512: 8192 + c * D + half * 512 + 512])
                                  for c in range(8)]
                        dst = A2[:, tc8 * 1024 + half * 512: tc8 * 1024 + half * 512 + 512]
                        groups.append((pieces, copy_evac(dst)))
                v_done = mm_phase(groups, [p4, p5], pe_waits=[(s_tr.sem, wvT_done)])

                groups = []
                for ec in range(8):
                    for fh in range(2):
                        pieces = [(A0[:, 8192 + tc8 * 1024 + ec * 128:
                                      8192 + tc8 * 1024 + ec * 128 + 128],
                                   A2[:, tc8 * 1024 + fh * 512: tc8 * 1024 + fh * 512 + 512])
                                  for tc8 in range(8)]
                        dst = FB[:, ec * 1024 + fh * 512: ec * 1024 + fh * 512 + 512]
                        if th == 0:
                            evf = copy_evac(dst)
                        else:
                            def evf(vector, pp, dst=dst):
                                vector.tensor_tensor(out=dst, in0=pp[:, 0:512],
                                                     in1=dst, op=ALU.add)
                        groups.append((pieces, evf))
                kvw = [(s_ve.sem, v_done)]
                if th == 0:
                    kvw.append((s_se.sem, lnse1))
                kv_last = mm_phase(groups, [p6, p7], pe_waits=kvw)

                if th == 0:
                    def wq_gate(sync, kv_last=kv_last):
                        sync.wait_ge(s_ve.sem, kv_last)
                    em("sync", wq_gate)
                    prep_wT(wq_h, A3hi)

            gd0, cc0 = s_gd.v, s_cc.v

            def e1(gpsimd, gd0=gd0, cc0=cc0, kvdone=kv_last):
                gpsimd.wait_ge(s_ve.sem, kvdone)
                gpsimd.dma_start(out=kv_ar_in.rearrange("(c p) f -> p c f", p=128),
                                 in_=FB[:, 0:8192].rearrange("p (c f) -> p c f", c=8)
                                 ).then_inc(s_gd.sem, 16)
                gpsimd.wait_ge(s_gd.sem, gd0 + 16)
                gpsimd.collective_compute(
                    "AllReduce", ALU.add,
                    replica_groups=[[0, 1], [2, 3], [4, 5], [6, 7]],
                    ins=[kv_ar_in.ap().opt()], outs=[kv_ar_out.ap().opt()],
                ).then_inc(s_cc.sem, 1)
                gpsimd.wait_ge(s_cc.sem, cc0 + 1)
                gpsimd.dma_start(out=A2[:, 8192:16384].rearrange("p (c f) -> p c f", c=8),
                                 in_=kv_ar_out.rearrange("(c p) f -> p c f", p=128)
                                 ).then_inc(s_gd.sem, 16)
            em("gpsimd", e1)
            s_gd.v += 32
            s_cc.v += 1
            kvfull_done = s_gd.v

            wpT_done = prep_wT(wp_h, A3hi)

            groups = []
            for fc in range(8):
                for tt in range(4):
                    pieces = [(A2[:, 8192 + ec * 1024 + fc * 128:
                                  8192 + ec * 1024 + fc * 128 + 128],
                               A1[:, ec * 2048 + tt * 512: ec * 2048 + tt * 512 + 512])
                              for ec in range(8)]
                    dst = A0[:, fc * 2048 + tt * 512: fc * 2048 + tt * 512 + 512]
                    groups.append((pieces, copy_evac(dst)))
            z_done = mm_phase(groups, [p0, p1, p2, p3],
                              pe_waits=[(s_gd.sem, kvfull_done)])

            pe_base = s_pe.v
            ev_base = s_ve.v
            pool = [p4, p5, p6, p7]

            def prj_t(tensor, pe_base=pe_base, ev_base=ev_base,
                      z_done=z_done, wpT_done=wpT_done):
                tensor.wait_ge(s_ve.sem, z_done)
                tensor.wait_ge(s_tr.sem, wpT_done)
                g = 0
                for tc in range(16):
                    for gh in range(2):
                        pp = pool[g % 4]
                        if g >= 4:
                            tensor.wait_ge(s_ve.sem, ev_base + g - 3)
                        for fc in range(8):
                            tensor.matmul(pp[:, 0:512],
                                          A0[:, fc * 2048 + tc * 128: fc * 2048 + tc * 128 + 128],
                                          A3[:, 8192 + fc * D + gh * 512:
                                             8192 + fc * D + gh * 512 + 512],
                                          start=(fc == 0), stop=False)
                        tensor.matmul(pp[:, 0:512], onesbf[0:1, 0:128],
                                      ROWB[0:1, gh * 512: gh * 512 + 512],
                                      start=False, stop=True).then_inc(s_pe.sem, 1)
                        g += 1
            em("tensor", prj_t)
            s_pe.v += 32

            st_wraps = []
            rtoks = []
            for tc in range(16):
                def mk_st(scalar, tc=tc):
                    return scalar.dma_start(out=resid_h[tc * 128:(tc + 1) * 128, :],
                                            in_=FB[:, (tc % 2) * 1024:(tc % 2) * 1024 + 1024])
                wrap, tok = stream_dma(mk_st, st_stream)
                st_wraps.append(wrap)
                rtoks.append((tc, tok))

            xtoks = []
            for tc in range(16):
                def mk_ld(sync, tc=tc):
                    return sync.dma_start(out=FB[:, (tc % 2) * 1024:(tc % 2) * 1024 + 1024],
                                          in_=x_in[tc * 128:(tc + 1) * 128, :])
                wrap, tok = stream_dma(mk_ld, ld_stream)
                xtoks.append(tok)
                pst_tok = rtoks[tc - 2][1] if tc >= 2 else None

                def ldx2(sync, wrap=wrap, tc=tc, ev_base=ev_base, pst_tok=pst_tok,
                         kv_read_done=gd0 + 16):
                    if tc < 2:
                        sync.wait_ge(s_gd.sem, kv_read_done)
                    else:
                        sync.wait_ge(s_ve.sem, ev_base + (tc - 2) * 2 + 2)
                        swait(sync, st_stream, pst_tok)
                    wrap(sync)
                em("sync", ldx2)

            def prj_v(vector, pe_base=pe_base, xtoks=xtoks):
                for tc in range(16):
                    swait(vector, ld_stream, xtoks[tc])
                    for gh in range(2):
                        g = tc * 2 + gh
                        pp = pool[g % 4]
                        vector.wait_ge(s_pe.sem, pe_base + g + 1)
                        sl = slice((tc % 2) * 1024 + gh * 512, (tc % 2) * 1024 + gh * 512 + 512)
                        vector.tensor_tensor(out=FB[:, sl], in0=pp[:, 0:512],
                                             in1=FB[:, sl], op=ALU.add)
                        vector.drain()
                        vector.nop().then_inc(s_ve.sem, 1)
            em("vector", prj_v)
            s_ve.v += 32

            for tc in range(16):
                tgt = ev_base + tc * 2 + 2

                def stx(scalar, wrap=st_wraps[tc], tgt=tgt):
                    scalar.wait_ge(s_ve.sem, tgt)
                    wrap(scalar)
                em("scalar", stx)
            return rtoks

        def ffn_wprep(entry_toks):
            phase_gate(entry_toks)
            for wi, (whbm, thbm, rows, cols) in enumerate(
                    ((wg_h, wgT_hbm, INNER, D), (wu_h, wuT_hbm, INNER, D),
                     (wd_h, wdT_hbm, D, INNER))):
                nslab = rows // 128
                vprev = [None, None]
                for sl in range(nslab):
                    def mk_ld(sync, whbm=whbm, sl=sl, cols=cols):
                        return sync.dma_start(
                            out=FB[:, (sl % 2) * 4096:(sl % 2) * 4096 + cols],
                            in_=whbm[sl * 128:(sl + 1) * 128, :])
                    wrap, tok = stream_dma(mk_ld, ld_stream)

                    def ld1(sync, wrap=wrap, sl=sl, pv=vprev[sl % 2]):
                        if pv is not None:
                            sync.wait_ge(s_ve.sem, pv)
                        wrap(sync)
                    em("sync", ld1)

                    def red1(vector, sl=sl, cols=cols, tok=tok):
                        swait(vector, ld_stream, tok)
                        vector.reduce_max(out=stat[:, 9:10],
                                          in_=FB[:, (sl % 2) * 4096:(sl % 2) * 4096 + cols],
                                          axis=AX.X) if False else None
                        vector.reduce_sum(out=stat[:, 9:10],
                                          in_=FB[:, (sl % 2) * 4096:(sl % 2) * 4096 + cols],
                                          axis=AX.X, apply_absolute_value=True)
                        vector.drain()
                        if sl == 0:
                            vector.tensor_copy(out=stat[:, 8:9], in_=stat[:, 9:10])
                        else:
                            vector.tensor_tensor(out=stat[:, 8:9], in0=stat[:, 8:9],
                                                 in1=stat[:, 9:10], op=ALU.add)
                    vprev[sl % 2] = ve(red1)
                sum_done = s_ve.v

                def tsum(tensor, sum_done=sum_done):
                    tensor.wait_ge(s_ve.sem, sum_done)
                    tensor.transpose(p0[0:1, 0:128], stat[:, 8:9], identf[:]
                                     ).then_inc(s_pe.sem, 1)
                em("tensor", tsum)
                pe_t = s_pe.inc()

                def fin(vector, pe_t=pe_t, wi=wi, rows=rows, cols=cols):
                    vector.wait_ge(s_pe.sem, pe_t)
                    vector.reduce_sum(out=stat[0:1, 10:11], in_=p0[0:1, 0:128], axis=AX.X)
                    vector.drain()
                    vector.tensor_scalar(out=stat[0:1, 13 + wi:14 + wi],
                                         in0=stat[0:1, 10:11],
                                         scalar1=1.0 / (rows * cols), scalar2=1e-5,
                                         op0=ALU.mult, op1=ALU.max)
                    vector.drain()
                    vector.reciprocal(out=stat[0:1, 10:11], in_=stat[0:1, 13 + wi:14 + wi])
                sw_done = ve(fin)

                def bcast(tensor, sw_done=sw_done):
                    tensor.wait_ge(s_ve.sem, sw_done)
                    tensor.matmul(p1[:, 0:1], onesff[0:1, 0:128],
                                  stat[0:1, 10:11], start=True, stop=True
                                  ).then_inc(s_pe.sem, 1)
                em("tensor", bcast)
                pe_b = s_pe.inc()

                def bc_ev(vector, pe_b=pe_b):
                    vector.wait_ge(s_pe.sem, pe_b)
                    vector.tensor_copy(out=stat[:, 12:13], in_=p1[:, 0:1])
                bc_done = ve(bc_ev)

                npieces = rows * cols // (128 * 1024)
                cpr = cols // 1024
                qz_prev = [None, None]
                for pc in range(npieces):
                    r0 = (pc // cpr) * 128
                    c0 = (pc % cpr) * 1024
                    trb = s_tr.v

                    def mk_ld(sync, whbm=whbm, r0=r0, c0=c0, pc=pc):
                        return sync.dma_start(
                            out=FB[:, (pc % 2) * 1024:(pc % 2) * 1024 + 1024],
                            in_=whbm[r0:r0 + 128, c0:c0 + 1024])
                    wrap, tok = stream_dma(mk_ld, ld_stream)

                    def ld2(sync, wrap=wrap, pc=pc, trb=trb, bc_done=bc_done,
                            pv=qz_prev[pc % 2]):
                        if pc == 0:
                            sync.wait_ge(s_ve.sem, bc_done)
                        if pv is not None:
                            sync.wait_ge(s_ve.sem, pv)
                        wrap(sync)
                    em("sync", ld2)

                    def qz(vector, pc=pc, tok=tok, trb=trb):
                        swait(vector, ld_stream, tok)
                        vector.wait_ge(s_tr.sem, trb)
                        src = FB[:, (pc % 2) * 1024:(pc % 2) * 1024 + 1024]
                        vector.tensor_scalar(out=qi32[:, 0:1024], in0=src,
                                             scalar1=stat[:, 12:13], scalar2=128.5,
                                             op0=ALU.mult, op1=ALU.add)
                        vector.drain()
                        vector.tensor_scalar(out=qi32[:, 0:1024], in0=qi32[:, 0:1024],
                                             scalar1=129.0, scalar2=127.0,
                                             op0=ALU.min, op1=ALU.max)
                        vector.drain()
                        vector.tensor_scalar(out=WKB[:, 0:1024], in0=qi32[:, 0:1024],
                                             scalar1=-128.0, scalar2=None, op0=ALU.add)
                    qzv = ve(qz)
                    qz_prev[pc % 2] = qzv

                    def trq(sync, qzv=qzv):
                        sync.wait_ge(s_ve.sem, qzv)
                        sync.dma_start_transpose(
                            WKB[:, 1024:2048].rearrange("p (c e) -> p c e", c=8),
                            WKB[:, 0:1024],
                        ).then_inc(s_tr.sem, 16)
                    em("sync", trq)
                    t2 = s_tr.inc(16)

                    def mk_st(scalar, thbm=thbm, r0=r0, c0=c0, cols=cols):
                        if cols == D:
                            dstp = thbm[:, :, r0:r0 + 128].rearrange("c p e -> p c e")
                        else:
                            dstp = thbm[c0 // 128:c0 // 128 + 8, :, r0:r0 + 128].rearrange("c p e -> p c e")
                        return scalar.dma_start(out=dstp,
                                                in_=WKB[:, 1024:2048].rearrange("p (c e) -> p c e", c=8))
                    wrap2, tok2 = stream_dma(mk_st, st_stream)

                    def st2(scalar, wrap2=wrap2, t2=t2):
                        scalar.wait_ge(s_tr.sem, t2)
                        wrap2(scalar)
                    em("scalar", st2)
                    last_store_tok = tok2

                    def trq_guard(sync, last_store_tok=last_store_tok):
                        swait(sync, st_stream, last_store_tok)
                    em("sync", trq_guard)
            for wi in range(3):
                def bch(tensor, wi=wi):
                    tensor.matmul(p1[:, 1 + wi:2 + wi], onesff[0:1, 0:128],
                                  stat[0:1, 13 + wi:14 + wi], start=True, stop=True
                                  ).then_inc(s_pe.sem, 1)
                em("tensor", bch)
                pe_b = s_pe.inc()

                def bce(vector, pe_b=pe_b, wi=wi):
                    vector.wait_ge(s_pe.sem, pe_b)
                    vector.tensor_copy(out=stat[:, 17 + wi:18 + wi], in_=p1[:, 1 + wi:2 + wi])
                ve(bce)
            means_done = s_ve.v

            ck_prev = [None, None]
            for sl in range(8):
                def mk_ld(sync, sl=sl):
                    return sync.dma_start(
                        out=FB[:, (sl % 2) * 4096:(sl % 2) * 4096 + 3072],
                        in_=wkqv_h[sl * 128:(sl + 1) * 128, :])
                wrap, tok = stream_dma(mk_ld, ld_stream)

                def ldk(sync, wrap=wrap, pv=ck_prev[sl % 2]):
                    if pv is not None:
                        swait(sync, st_stream, pv)
                    wrap(sync)
                em("sync", ldk)

                def ck(vector, sl=sl, tok=tok):
                    swait(vector, ld_stream, tok)
                    vector.tensor_copy(out=A3[:, (sl % 2) * 3072:(sl % 2) * 3072 + 3072],
                                       in_=FB[:, (sl % 2) * 4096:(sl % 2) * 4096 + 3072])
                cv = ve(ck)

                def mk_st(scalar, sl=sl):
                    return scalar.dma_start(out=kqvb_hbm[sl * 128:(sl + 1) * 128, :],
                                            in_=A3[:, (sl % 2) * 3072:(sl % 2) * 3072 + 3072])
                wrap2, tok2 = stream_dma(mk_st, st_stream)

                def stk(scalar, wrap2=wrap2, cv=cv):
                    scalar.wait_ge(s_ve.sem, cv)
                    wrap2(scalar)
                em("scalar", stk)
                ck_prev[sl % 2] = tok2
            kqvb_tok = tok2
            return means_done, kqvb_tok

        def ffn_block(src_resid, dst_resid, means_done, wprep_guard_toks):
            phase_gate(wprep_guard_toks)
            srow_last = None
            for th in range(2):
                _, vq = ln_half2(src_resid, th,
                                 lambda tc8, th=th: A0[:, (th * 8 + tc8) * 1024:(th * 8 + tc8) * 1024 + 1024],
                                 quant=True)

                def strow(tensor, vq=vq):
                    tensor.wait_ge(s_ve.sem, vq)
                    tensor.transpose(p7[0:8, 0:128], stat[:, 80:88], identf[:]
                                     ).then_inc(s_pe.sem, 1)
                em("tensor", strow)
                pe1 = s_pe.inc()

                def srow_ev(vector, pe1=pe1):
                    vector.wait_ge(s_pe.sem, pe1)
                    vector.tensor_copy(out=rtmp[0:8, 0:128], in_=p7[0:8, 0:128])
                srow_cp = ve(srow_ev)

                def mk_srd1(sync):
                    return sync.dma_start(out=row_hbm[0:8, :], in_=rtmp[0:8, 0:128])
                srw1, srt1 = stream_dma(mk_srd1, ld_stream)

                def mk_srdma(sync, th=th):
                    return sync.dma_start(
                        out=ROWF[0:1, th * 1024: th * 1024 + 1024],
                        in_=row_hbm[0:8, :])
                srwrap, srtok = stream_dma(mk_srdma, ld_stream)

                def srdma(sync, srw1=srw1, srwrap=srwrap, srow_cp=srow_cp, srt1=srt1):
                    sync.wait_ge(s_ve.sem, srow_cp)
                    srw1(sync)
                    swait(sync, ld_stream, srt1)
                    srwrap(sync)
                em("sync", srdma)
                srow_last = srtok
            xq_ve = s_ve.v

            xqT_done = transpose16(lambda tc: A0[:, tc * 1024:(tc + 1) * 1024], A1,
                                   gates=[lambda sync, v=xq_ve: sync.wait_ge(s_ve.sem, v)])

            groups = []
            for tt in range(4):
                pieces = [(onesff[0:1, 0:128], ROWF[0:1, tt * 512:(tt + 1) * 512])]
                groups.append((pieces, copy_evac(FB[:, tt * 512:(tt + 1) * 512])))
            bcast0_done = mm_phase(groups, [p4, p5],
                                   pe_waits=[(ld_stream.sems[srow_last[0]].sem, srow_last[1])])

            def macc0(vector, bcast0_done=bcast0_done):
                vector.wait_ge(s_ve.sem, bcast0_done)
                vector.memset(FB[:, 4096:6144], 0.0)
            macc_z = ve(macc0)

            peA = s_pe.v
            guev_last = {}
            gst_toks = {}
            for oc in range(32):
                bsl = (oc % 2) * 1024
                prev_pe = peA + (oc - 1) * 8

                def mk_ldg(sync, oc=oc, bsl=bsl):
                    return sync.dma_start(
                        out=WKB[:, bsl:bsl + 1024].rearrange("p (c e) -> p c e", c=8),
                        in_=wgT_hbm[:, :, oc * 128:(oc + 1) * 128])
                wrapg, tokg = stream_dma(mk_ldg, ld_stream)

                def mk_ldu(sync, oc=oc, bsl=bsl):
                    return sync.dma_start(
                        out=A3[:, 14336 + bsl:14336 + bsl + 1024].rearrange("p (c e) -> p c e", c=8),
                        in_=wuT_hbm[:, :, oc * 128:(oc + 1) * 128])
                wrapu, toku = stream_dma(mk_ldu, ld_stream)

                def ldw(sync, wrapg=wrapg, wrapu=wrapu, oc=oc, prev_pe=prev_pe,
                        xqT_done=xqT_done, macc_z=macc_z):
                    if oc == 0:
                        sync.wait_ge(s_tr.sem, xqT_done)
                        sync.wait_ge(s_ve.sem, macc_z)
                        for t in wprep_guard_toks:
                            swait(sync, st_stream, t)
                    if oc >= 2:
                        sync.wait_ge(s_pe.sem, prev_pe)
                    wrapg(sync)
                    wrapu(sync)
                em("sync", ldw)

                def gumm(tensor, oc=oc, bsl=bsl, tokg=tokg, toku=toku, peA=peA):
                    swait(tensor, ld_stream, tokg)
                    swait(tensor, ld_stream, toku)
                    for tt in range(4):
                        gidx = oc * 4 + tt
                        if gidx >= 2:
                            tensor.wait_ge(s_ve.sem, guev_last[gidx - 2])
                        pg = (p0, p2)[tt % 2]
                        pu = (p1, p3)[tt % 2]
                        for c in range(8):
                            mm = tensor.matmul(pg[:, 0:512],
                                               WKB[:, bsl + c * 128: bsl + c * 128 + 128],
                                               A1[:, c * 2048 + tt * 512: c * 2048 + tt * 512 + 512],
                                               start=(c == 0), stop=(c == 7))
                            if c == 7:
                                mm.then_inc(s_pe.sem, 1)
                        for c in range(8):
                            mm = tensor.matmul(pu[:, 0:512],
                                               A3[:, 14336 + bsl + c * 128: 14336 + bsl + c * 128 + 128],
                                               A1[:, c * 2048 + tt * 512: c * 2048 + tt * 512 + 512],
                                               start=(c == 0), stop=(c == 7))
                            if c == 7:
                                mm.then_inc(s_pe.sem, 1)
                em("tensor", gumm)
                s_pe.v += 8

                for tt in range(4):
                    gidx = oc * 4 + tt
                    pe_u = peA + oc * 8 + tt * 2 + 2

                    def guev1(vector, tt=tt, pe_u=pe_u):
                        pg = (p0, p2)[tt % 2]
                        vector.wait_ge(s_pe.sem, pe_u - 1)
                        vector.tensor_tensor(
                            out=FB[:, 6144 + (tt % 2) * 512: 6144 + (tt % 2) * 512 + 512],
                            in0=pg[:, 0:512],
                            in1=FB[:, tt * 512:(tt + 1) * 512], op=ALU.mult)
                    v1 = ve(guev1)

                    def gusil(scalar, tt=tt, v1=v1):
                        scalar.wait_ge(s_ve.sem, v1)
                        scalar.activation(
                            out=FB[:, 7168 + (tt % 2) * 512: 7168 + (tt % 2) * 512 + 512],
                            in_=FB[:, 6144 + (tt % 2) * 512: 6144 + (tt % 2) * 512 + 512],
                            func=AF.Silu, scale=stat[:, 17:18])
                    sl1 = se(gusil)

                    gprev = gst_toks.get(gidx - 2)

                    def guev2(vector, tt=tt, sl1=sl1, pe_u=pe_u, gprev=gprev):
                        pu = (p1, p3)[tt % 2]
                        vector.wait_ge(s_se.sem, sl1)
                        vector.wait_ge(s_pe.sem, pe_u)
                        if gprev is not None:
                            swait(vector, st_stream, gprev)
                        vector.tensor_tensor(
                            out=FB[:, 6144 + (tt % 2) * 512: 6144 + (tt % 2) * 512 + 512],
                            in0=pu[:, 0:512],
                            in1=FB[:, tt * 512:(tt + 1) * 512], op=ALU.mult)
                        vector.drain()
                        vector.tensor_tensor(
                            out=FB[:, 2048 + (tt % 2) * 512: 2048 + (tt % 2) * 512 + 512],
                            in0=FB[:, 7168 + (tt % 2) * 512: 7168 + (tt % 2) * 512 + 512],
                            in1=FB[:, 6144 + (tt % 2) * 512: 6144 + (tt % 2) * 512 + 512],
                            op=ALU.mult)
                    v2a = ve(guev2)

                    def gusq(scalar, tt=tt, v2a=v2a):
                        scalar.wait_ge(s_ve.sem, v2a)
                        scalar.activation(
                            out=FB[:, 7168 + (tt % 2) * 512: 7168 + (tt % 2) * 512 + 512],
                            in_=FB[:, 2048 + (tt % 2) * 512: 2048 + (tt % 2) * 512 + 512],
                            func=AF.Square)
                    sq1 = se(gusq)

                    def guevm(vector, tt=tt, sq1=sq1):
                        vector.wait_ge(s_se.sem, sq1)
                        vector.tensor_tensor(
                            out=FB[:, 4096 + tt * 512: 4096 + tt * 512 + 512],
                            in0=FB[:, 4096 + tt * 512: 4096 + tt * 512 + 512],
                            in1=FB[:, 7168 + (tt % 2) * 512: 7168 + (tt % 2) * 512 + 512],
                            op=ALU.max)
                    v2 = ve(guevm)
                    guev_last[gidx] = v2

                    def mk_stg(scalar, oc=oc, tt=tt):
                        return scalar.dma_start(
                            out=gu_hbm[oc, :, tt * 512:(tt + 1) * 512],
                            in_=FB[:, 2048 + (tt % 2) * 512: 2048 + (tt % 2) * 512 + 512])
                    wraps, toks = stream_dma(mk_stg, st_stream)
                    gst_toks[gidx] = toks

                    def gust(scalar, wraps=wraps, v2=v2):
                        scalar.wait_ge(s_ve.sem, v2)
                        wraps(scalar)
                    em("scalar", gust)
            phaseA_ve = s_ve.v
            gu_last_toks = [gst_toks[126], gst_toks[127]]

            peT = s_pe.v
            s2c_base = s_ve.v

            def maccT(tensor, phaseA_ve=phaseA_ve, s2c_base=s2c_base):
                tensor.wait_ge(s_ve.sem, phaseA_ve)
                for j in range(16):
                    if j >= 2:
                        tensor.wait_ge(s_ve.sem, s2c_base + j - 1)
                    tensor.transpose((p4, p5)[j % 2][:, 0:128],
                                     FB[:, 4096 + j * 128: 4096 + j * 128 + 128],
                                     identf[:]).then_inc(s_pe.sem, 1)
            em("tensor", maccT)
            s_pe.v += 16

            def s2cols(vector, peT=peT):
                for j in range(16):
                    vector.wait_ge(s_pe.sem, peT + j + 1)
                    vector.reduce_max(out=stat[:, 64 + j:65 + j],
                                      in_=(p4, p5)[j % 2][:, 0:128], axis=AX.X)
                    vector.drain()
                    vector.nop().then_inc(s_ve.sem, 1)
            em("vector", s2cols)
            s_ve.v += 16
            s2c_done = s_ve.v

            def s2sqrt(scalar, s2c_done=s2c_done):
                scalar.wait_ge(s_ve.sem, s2c_done)
                scalar.activation(out=stat[:, 64:80], in_=stat[:, 64:80], func=AF.Sqrt)
            s2sq = se(s2sqrt)

            def s2fin(vector, s2sq=s2sq):
                vector.wait_ge(s_se.sem, s2sq)
                vector.tensor_scalar(out=stat[:, 64:80], in0=stat[:, 64:80],
                                     scalar1=1e-5, scalar2=None, op0=ALU.max)
                vector.drain()
                vector.tensor_scalar(out=stat[:, 32:48], in0=stat[:, 64:80],
                                     scalar1=stat[:, 18:19], scalar2=None, op0=ALU.mult)
                vector.drain()
                vector.tensor_scalar(out=stat[:, 32:48], in0=stat[:, 32:48],
                                     scalar1=stat[:, 19:20], scalar2=1.0 / 127.0,
                                     op0=ALU.mult, op1=ALU.mult)
                vector.drain()
                vector.reciprocal(out=stat[:, 64:80], in_=stat[:, 64:80])
                vector.drain()
                vector.tensor_scalar(out=stat[:, 64:80], in0=stat[:, 64:80],
                                     scalar1=127.0, scalar2=None, op0=ALU.mult)
            s2_done = ve(s2fin)

            def s2row_t(tensor, s2_done=s2_done):
                tensor.wait_ge(s_ve.sem, s2_done)
                tensor.transpose(p7[0:16, 0:128], stat[:, 64:80], identf[:]
                                 ).then_inc(s_pe.sem, 1)
            em("tensor", s2row_t)
            peR = s_pe.inc()

            def s2row_ev(vector, peR=peR):
                vector.wait_ge(s_pe.sem, peR)
                vector.tensor_copy(out=rtmp[0:16, 0:128], in_=p7[0:16, 0:128])
            s2row_cp = ve(s2row_ev)

            def mk_s2d1(sync):
                return sync.dma_start(out=row_hbm[:, :], in_=rtmp[0:16, 0:128])
            s2w1, s2t1 = stream_dma(mk_s2d1, ld_stream)

            def mk_s2dma(sync):
                return sync.dma_start(
                    out=ROWF[0:1, 0:2048],
                    in_=row_hbm[:, :])
            s2wrap, s2tok = stream_dma(mk_s2dma, ld_stream)

            def s2dma(sync, s2w1=s2w1, s2wrap=s2wrap, s2row_cp=s2row_cp, s2t1=s2t1):
                sync.wait_ge(s_ve.sem, s2row_cp)
                s2w1(sync)
                swait(sync, ld_stream, s2t1)
                s2wrap(sync)
            em("sync", s2dma)

            groups = []
            for tt in range(4):
                pieces = [(onesff[0:1, 0:128], ROWF[0:1, tt * 512:(tt + 1) * 512])]
                groups.append((pieces, copy_evac(FB[:, tt * 512:(tt + 1) * 512])))
            bcast2_done = mm_phase(groups, [p4, p5],
                                   pe_waits=[(ld_stream.sems[s2tok[0]].sem, s2tok[1])])

            res_toks = []
            dev_last_tt = None
            for tt in range(4):
                veB0 = s_ve.v
                qg_prev = [None, None]
                for oc in range(32):
                    def mk_ldgu(sync, oc=oc, tt=tt):
                        return sync.dma_start(
                            out=FB[:, 2048 + (oc % 2) * 512: 2048 + (oc % 2) * 512 + 512],
                            in_=gu_hbm[oc, :, tt * 512:(tt + 1) * 512])
                    wrap, tok = stream_dma(mk_ldgu, ld_stream)

                    def ldgu(sync, wrap=wrap, oc=oc, tt=tt, pv=qg_prev[oc % 2],
                             bcast2_done=bcast2_done, gu_last_toks=gu_last_toks):
                        if oc == 0 and tt == 0:
                            for t_ in gu_last_toks:
                                swait(sync, st_stream, t_)
                            sync.wait_ge(s_ve.sem, bcast2_done)
                        if pv is not None:
                            sync.wait_ge(s_ve.sem, pv)
                        wrap(sync)
                    em("sync", ldgu)

                    def qgu(vector, oc=oc, tt=tt, tok=tok):
                        swait(vector, ld_stream, tok)
                        src = FB[:, 2048 + (oc % 2) * 512: 2048 + (oc % 2) * 512 + 512]
                        vector.tensor_tensor(out=FB[:, 3072:3584], in0=src,
                                             in1=FB[:, tt * 512:(tt + 1) * 512],
                                             op=ALU.mult)
                        vector.drain()
                        vector.tensor_scalar(out=qi32[:, 0:512], in0=FB[:, 3072:3584],
                                             scalar1=128.5, scalar2=None, op0=ALU.add)
                        vector.drain()
                        vector.tensor_scalar(out=A0[:, oc * 512:(oc + 1) * 512],
                                             in0=qi32[:, 0:512],
                                             scalar1=-128.0, scalar2=None, op0=ALU.add)
                    qv = ve(qgu)
                    qg_prev[oc % 2] = qv
                q_done = s_ve.v

                peD = s_pe.v
                dtoks = {}
                for oc in range(32):
                    def mk_ldd(sync, oc=oc):
                        return sync.dma_start(
                            out=WKB[:, (oc % 2) * 1024:(oc % 2) * 1024 + 1024],
                            in_=wdT_hbm[oc, :, :])
                    wrap, tok = stream_dma(mk_ldd, ld_stream)
                    dtoks[oc] = tok

                    def ldd(sync, wrap=wrap, oc=oc, q_done=q_done, peD=peD):
                        if oc == 0:
                            sync.wait_ge(s_ve.sem, q_done)
                        if oc >= 2:
                            sync.wait_ge(s_pe.sem, peD + (oc - 1))
                        wrap(sync)
                    em("sync", ldd)

                    def dmm(tensor, oc=oc, tok=tok, peD=peD, dprev=dev_last_tt):
                        swait(tensor, ld_stream, tok)
                        if oc == 0 and dprev is not None:
                            tensor.wait_ge(s_ve.sem, dprev)
                        for tc2 in range(4):
                            for hh in range(2):
                                pp = PSUMS[tc2 * 2 + hh]
                                mm = tensor.matmul(
                                    pp[:, 0:512],
                                    A0[:, oc * 512 + tc2 * 128: oc * 512 + tc2 * 128 + 128],
                                    WKB[:, (oc % 2) * 1024 + hh * 512:(oc % 2) * 1024 + hh * 512 + 512],
                                    start=(oc == 0), stop=(oc == 31))
                                if tc2 == 3 and hh == 1:
                                    mm.then_inc(s_pe.sem, 1)
                    em("tensor", dmm)
                    s_pe.inc()
                down_pe = s_pe.v

                for tc2 in range(4):
                    tc = tt * 4 + tc2

                    def mk_ldr(sync, tc=tc, tc2=tc2):
                        return sync.dma_start(
                            out=FB[:, 4096 + (tc2 % 2) * 1024: 4096 + (tc2 % 2) * 1024 + 1024],
                            in_=src_resid[tc * 128:(tc + 1) * 128, :])
                    wrap, tok = stream_dma(mk_ldr, ld_stream)
                    prev_res_tok = res_toks[-2][1] if len(res_toks) >= 2 else None

                    def ldr(sync, wrap=wrap, down_pe=down_pe, prev=prev_res_tok):
                        sync.wait_ge(s_pe.sem, down_pe)
                        if prev is not None:
                            swait(sync, st_stream, prev)
                        wrap(sync)
                    em("sync", ldr)

                    def dev(vector, tc2=tc2, tt=tt, tok=tok, down_pe=down_pe):
                        swait(vector, ld_stream, tok)
                        vector.wait_ge(s_pe.sem, down_pe)
                        for hh in range(2):
                            pp = PSUMS[tc2 * 2 + hh]
                            sl = slice(4096 + (tc2 % 2) * 1024 + hh * 512,
                                       4096 + (tc2 % 2) * 1024 + hh * 512 + 512)
                            vector.scalar_tensor_tensor(
                                out=FB[:, sl], in0=pp[:, 0:512],
                                scalar=stat[:, 32 + tt * 4 + tc2:33 + tt * 4 + tc2],
                                in1=FB[:, sl], op0=ALU.mult, op1=ALU.add)
                            vector.drain()
                        vector.nop().then_inc(s_ve.sem, 1)
                    em("vector", dev)
                    dv = s_ve.inc()

                    def mk_str(scalar, tc=tc, tc2=tc2):
                        return scalar.dma_start(
                            out=dst_resid[tc * 128:(tc + 1) * 128, :],
                            in_=FB[:, 4096 + (tc2 % 2) * 1024: 4096 + (tc2 % 2) * 1024 + 1024])
                    wrap2, tok2 = stream_dma(mk_str, st_stream)
                    res_toks.append((tc, tok2))

                    def strx(scalar, wrap2=wrap2, dv=dv):
                        scalar.wait_ge(s_ve.sem, dv)
                        wrap2(scalar)
                    em("scalar", strx)
                dev_last_tt = s_ve.v
            return res_toks

        def block3(kqvb_tok, b2_res_toks):
            phase_gate([t for _, t in b2_res_toks])
            def mk_liq(sync):
                return sync.dma_start(out=FB[0:1, 0:4096], in_=iotaq_h[:, :])
            wrap, liq_tok = stream_dma(mk_liq, ld_stream)

            def ldiq(sync, wrap=wrap):
                wrap(sync)
            em("sync", ldiq)

            def mvals(vector, liq_tok=liq_tok):
                swait(vector, ld_stream, liq_tok)
                for hh in range(2):
                    vector.tensor_copy(out=stat[0:1, 20 + hh:21 + hh],
                                       in_=mrow[0:1, 24 + hh:25 + hh])
                    vector.drain()
                    vector.tensor_scalar(out=stat[0:1, 22 + hh:23 + hh],
                                         in0=stat[0:1, 20 + hh:21 + hh],
                                         scalar1=0.0, scalar2=-32.0,
                                         op0=ALU.max, op1=ALU.mult)
                    vector.drain()
                    vector.tensor_scalar(out=ROWB[0:1, hh * 4096:(hh + 1) * 4096],
                                         in0=FB[0:1, 0:4096],
                                         scalar1=stat[0:1, 22 + hh:23 + hh],
                                         scalar2=None, op0=ALU.mult)
                    vector.drain()
            mv_done = ve(mvals)

            mjc_done = mv_done
            for hh in range(2):
                def mbc(tensor, hh=hh, mv_done=mv_done):
                    tensor.wait_ge(s_ve.sem, mv_done)
                    tensor.matmul(p1[:, hh:hh + 1], onesff[0:1, 0:128],
                                  stat[0:1, 20 + hh:21 + hh], start=True, stop=True
                                  ).then_inc(s_pe.sem, 1)
                em("tensor", mbc)
                pe_b = s_pe.inc()

                def mje(vector, hh=hh, pe_b=pe_b):
                    vector.wait_ge(s_pe.sem, pe_b)
                    vector.tensor_copy(out=stat[:, 24 + hh:25 + hh], in_=p1[:, hh:hh + 1])
                    vector.drain()
                    vector.tensor_scalar(out=mjc[:, hh * 32:(hh + 1) * 32], in0=iotak[:],
                                         scalar1=stat[:, 24 + hh:25 + hh],
                                         scalar2=None, op0=ALU.mult)
                mjc_done = ve(mje)

            for th in range(2):
                lnse, _ = ln_half2(
                    resid_h, th,
                    lambda tc8, th=th: A0[:, (th * 8 + tc8) * 1024:(th * 8 + tc8) * 1024 + 1024])
            ln3_se = s_se.v
            ln3T_done = transpose16(lambda tc: A0[:, tc * 1024:(tc + 1) * 1024], A3,
                                    gates=[lambda sync, v=ln3_se: sync.wait_ge(s_se.sem, v)])

            peALL = None

            def kq_part2(colbase, dstbuf, first, part_off, mjc_done=mjc_done):
                nonlocal peALL
                if peALL is None:
                    peALL = s_pe.v
                peK0 = peALL - part_off
                for tt in range(4):
                    peK = s_pe.v
                    for c in range(8):
                        def mk_ldc(sync, c=c, colbase=colbase):
                            return sync.dma_start(
                                out=WKB[:, (c % 2) * 1024:(c % 2) * 1024 + 1024],
                                in_=kqvb_hbm[c * 128:(c + 1) * 128, colbase:colbase + 1024])
                        wrap, tok = stream_dma(mk_ldc, ld_stream)

                        def ldc(sync, wrap=wrap, c=c, tt=tt, first=first,
                                peALLv=peALL, part_off=part_off,
                                ln3T_done=ln3T_done, kqvb_tok=kqvb_tok):
                            if c == 0 and tt == 0 and first:
                                sync.wait_ge(s_tr.sem, ln3T_done)
                                swait(sync, st_stream, kqvb_tok)
                            gi = part_off + tt * 8 + c
                            if gi >= 2:
                                sync.wait_ge(s_pe.sem, peALLv + gi - 1)
                            wrap(sync)
                        em("sync", ldc)

                        def kmm(tensor, c=c, tt=tt, tok=tok, first=first,
                                mjc_done=mjc_done):
                            if c == 0 and tt == 0 and first:
                                tensor.wait_ge(s_ve.sem, mjc_done)
                            swait(tensor, ld_stream, tok)
                            for slot in range(8):
                                pp = PSUMS[slot]
                                mm = tensor.matmul(
                                    pp[:, 0:512],
                                    WKB[:, (c % 2) * 1024 + slot * 128:(c % 2) * 1024 + slot * 128 + 128],
                                    A3[:, c * 2048 + tt * 512: c * 2048 + tt * 512 + 512],
                                    start=(c == 0), stop=(c == 7))
                                if slot == 7:
                                    mm.then_inc(s_pe.sem, 1)
                        em("tensor", kmm)
                        s_pe.inc()

                    def kev(vector, tt=tt, dstbuf=dstbuf, peK=peK):
                        vector.wait_ge(s_pe.sem, peK + 8)
                        for slot in range(8):
                            pp = PSUMS[slot]
                            vector.tensor_copy(
                                out=dstbuf[:, slot * 2048 + tt * 512: slot * 2048 + tt * 512 + 512],
                                in_=pp[:, 0:512])
                        vector.drain()
                        vector.nop().then_inc(s_ve.sem, 1)
                    em("vector", kev)
                    s_ve.inc()
                    kev_v = s_ve.v

                    def tt_guard(tensor, kev_v=kev_v):
                        tensor.wait_ge(s_ve.sem, kev_v)
                    em("tensor", tt_guard)

            kq_part2(0, A0, True, 0)
            kq_part2(D, A1, False, 32)
            for tcg in range(8):
                peV = s_pe.v
                for c in range(8):
                    def mk_ldc(sync, c=c):
                        return sync.dma_start(
                            out=WKB[:, (c % 2) * 1024:(c % 2) * 1024 + 1024],
                            in_=kqvb_hbm[c * 128:(c + 1) * 128, 2 * D:2 * D + 1024])
                    wrap, tok = stream_dma(mk_ldc, ld_stream)

                    def ldc(sync, wrap=wrap, c=c, tcg=tcg, peALLv=peALL):
                        gi = 64 + tcg * 8 + c
                        if gi >= 2:
                            sync.wait_ge(s_pe.sem, peALLv + gi - 1)
                        wrap(sync)
                    em("sync", ldc)

                    def vmm(tensor, c=c, tcg=tcg, tok=tok):
                        swait(tensor, ld_stream, tok)
                        for tc2 in range(2):
                            for vh in range(2):
                                pp = PSUMS[tc2 * 2 + vh]
                                tc = tcg * 2 + tc2
                                mm = tensor.matmul(
                                    pp[:, 0:512],
                                    A3[:, c * 2048 + tc * 128: c * 2048 + tc * 128 + 128],
                                    WKB[:, (c % 2) * 1024 + vh * 512:(c % 2) * 1024 + vh * 512 + 512],
                                    start=(c == 0), stop=(c == 7))
                                if tc2 == 1 and vh == 1:
                                    mm.then_inc(s_pe.sem, 1)
                    em("tensor", vmm)
                    s_pe.inc()

                def vev(vector, tcg=tcg, peV=peV):
                    vector.wait_ge(s_pe.sem, peV + 8)
                    for tc2 in range(2):
                        for vh in range(2):
                            pp = PSUMS[tc2 * 2 + vh]
                            tc = tcg * 2 + tc2
                            for sl4 in range(4):
                                slot = vh * 4 + sl4
                                vector.tensor_copy(
                                    out=A2[:, slot * 2048 + tc * 128: slot * 2048 + tc * 128 + 128],
                                    in_=pp[:, sl4 * 128:(sl4 + 1) * 128])
                    vector.drain()
                    vector.nop().then_inc(s_ve.sem, 1)
                em("vector", vev)
                s_ve.inc()
                vev_v = s_ve.v

                def v_guard(tensor, vev_v=vev_v):
                    tensor.wait_ge(s_ve.sem, vev_v)
                em("tensor", v_guard)
            kqv_ve = s_ve.v

            gd0 = s_gd.v
            cc_e2 = s_cc.v + 1

            def e2(gpsimd, kqv_ve=kqv_ve, gd0=gd0, cc_e2=cc_e2):
                gpsimd.wait_ge(s_ve.sem, kqv_ve)
                for slot in range(8):
                    gpsimd.dma_start(out=a2a_in[slot, :, 0:2048],
                                     in_=A0[:, slot * 2048:(slot + 1) * 2048]).then_inc(s_gd.sem, 16)
                    gpsimd.dma_start(out=a2a_in[slot, :, 2048:4096],
                                     in_=A1[:, slot * 2048:(slot + 1) * 2048]).then_inc(s_gd.sem, 16)
                    gpsimd.dma_start(out=a2a_in[slot, :, 4096:6144],
                                     in_=A2[:, slot * 2048:(slot + 1) * 2048]).then_inc(s_gd.sem, 16)
                gpsimd.wait_ge(s_gd.sem, gd0 + 24 * 16)
                with allow_ici_all_to_all():
                    gpsimd.collective_compute(
                        "AllToAll", ALU.bypass,
                        replica_groups=[list(range(8))],
                        ins=[a2a_in.ap().opt()], outs=[a2a_out.ap().opt()],
                    ).then_inc(s_cc.sem, 1)
                gpsimd.wait_ge(s_cc.sem, cc_e2)
                for i in range(8):
                    gpsimd.dma_start(out=A1[:, i * 2048:(i + 1) * 2048],
                                     in_=a2a_out[i, :, 0:2048]).then_inc(s_gd.sem, 16)
                    gpsimd.dma_start(out=A0[:, i * 2048:(i + 1) * 2048],
                                     in_=a2a_out[i, :, 2048:4096]).then_inc(s_gd.sem, 16)
                    gpsimd.dma_start(
                        out=A2[:].rearrange("p (k w) -> p k w", w=130)[:, i * 16:(i + 1) * 16, 0:64],
                        in_=a2a_out[i, :, 4096:6144].rearrange("p (k v) -> p k v", v=128)[:, :, 0:64]
                    ).then_inc(s_gd.sem, 16)
                    gpsimd.dma_start(
                        out=A2[:].rearrange("p (k w) -> p k w", w=130)[:, i * 16:(i + 1) * 16, 65:129],
                        in_=a2a_out[i, :, 4096:6144].rearrange("p (k v) -> p k v", v=128)[:, :, 64:128]
                    ).then_inc(s_gd.sem, 16)
            em("gpsimd", e2)
            s_gd.v += 24 * 16
            s_cc.v += 1
            s_gd.v += 32 * 16
            e2_recv = s_gd.v

            def vones(vector, e2_recv=e2_recv, mjc_done=mjc_done):
                vector.wait_ge(s_gd.sem, e2_recv)
                vector.wait_ge(s_ve.sem, mjc_done)
                vector.memset(A2[:].rearrange("p (k w) -> p k w", w=130)[:, :, 64:65], 1.0)
                vector.memset(A2[:].rearrange("p (k w) -> p k w", w=130)[:, :, 129:130], 1.0)
            vpad_done = ve(vones)

            sched = []
            exp_of = {}
            av_of = {}
            fin_ve_of = {}
            pe_v = s_pe.v
            se_v = s_se.v
            ve_v = s_ve.v
            ktg = 0
            tile = 0
            for beta in range(4):
                for qi in range(8):
                    nkt = 4 * qi + 4
                    for kt in range(nkt):
                        ping = ktg % 2
                        sched.append(dict(op="S", beta=beta, qi=qi, kt=kt, ktg=ktg,
                                          ping=ping,
                                          first=(kt == 0), last=(kt == nkt - 1)))
                        pe_v += 2
                        sgrp = (pe_v - 1, pe_v)
                        sched[-1]["sgrp"] = sgrp
                        sched.append(dict(op="E", ktg=ktg, ping=ping, beta=beta,
                                          qi=qi, kt=kt, sgrp=sgrp))
                        se_v += 2
                        exp_of[ktg] = (se_v - 1, se_v)
                        sched.append(dict(op="A", ktg=ktg, ping=ping, beta=beta,
                                          qi=qi, kt=kt, first=(kt == 0),
                                          last=(kt == nkt - 1), tile=tile))
                        pe_v += 2
                        av_of[ktg] = (pe_v - 1, pe_v)
                        ktg += 1
                    sched.append(dict(op="F", beta=beta, qi=qi, tile=tile,
                                      av=av_of[ktg - 1]))
                    ve_v += 1
                    fin_ve_of[tile] = ve_v
                    tile += 1

            pe_b3 = s_pe.v
            se_b3 = s_se.v
            ve_b3 = s_ve.v

            f_toks = {}
            for step in [s for s in sched if s["op"] == "F"]:
                t = step["tile"]
            def b3_t(tensor, sched=sched, vpad_done=vpad_done):
                for step in sched:
                    if step["op"] == "S":
                        beta, qi, kt, ktg = step["beta"], step["qi"], step["kt"], step["ktg"]
                        ping = step["ping"]
                        pS = ((p0, p1), (p4, p5))[ping]
                        if ktg == 0:
                            tensor.wait_ge(s_ve.sem, vpad_done)
                        if ktg >= 2:
                            tensor.wait_ge(s_se.sem, exp_of[ktg - 2][1])
                        kcol = beta * 4096 + kt * 128
                        qcol = beta * 4096 + qi * 512
                        diag = kt - 4 * qi
                        for hh in range(2):
                            hb = hh * 64
                            tensor.matmul(pS[hh][:, 0:512],
                                          A1[hb:hb + 64, kcol:kcol + 128],
                                          A0[hb:hb + 64, qcol:qcol + 512],
                                          start=True, stop=False)
                            mm = tensor.matmul(
                                pS[hh][:, 0:512], onesbf[0:1, 0:128],
                                ROWB[0:1, hh * 4096 + qi * 512: hh * 4096 + qi * 512 + 512],
                                start=False, stop=(diag < 0))
                            if diag < 0:
                                mm.then_inc(s_pe.sem, 1)
                            else:
                                tensor.matmul(pS[hh][:, 0:512], identb[:, 0:128],
                                              masksb[:, diag * 512:(diag + 1) * 512],
                                              start=False, stop=True).then_inc(s_pe.sem, 1)
                    elif step["op"] == "A":
                        ktg, ping = step["ktg"], step["ping"]
                        beta, kt, tile = step["beta"], step["kt"], step["tile"]
                        kc = beta * 32 + kt
                        if step["first"] and tile >= 1:
                            tensor.wait_ge(s_ve.sem, fin_ve_of[tile - 1])
                        for hh in range(2):
                            tensor.wait_ge(s_se.sem, exp_of[ktg][hh])
                            pO = (p2, p3)[hh]
                            tensor.matmul(pO[0:65, 0:512],
                                          A2[:, kc * 130 + hh * 65: kc * 130 + hh * 65 + 65],
                                          ptb[:, (ping * 2 + hh) * 512:(ping * 2 + hh) * 512 + 512],
                                          start=step["first"], stop=step["last"]
                                          ).then_inc(s_pe.sem, 1)
            em("tensor", b3_t)
            s_pe.v = pe_b3 + sum(4 for s in sched if s["op"] == "S")

            fstage_toks = []
            f_sd_tokens = {}
            for step in sched:
                if step["op"] == "E":
                    def b3_e(scalar, step=step):
                        ktg, ping, kt = step["ktg"], step["ping"], step["kt"]
                        for hh in range(2):
                            scalar.wait_ge(s_pe.sem, step["sgrp"][hh])
                            if ktg >= 2:
                                scalar.wait_ge(s_pe.sem, av_of[ktg - 2][hh])
                            pS = ((p0, p1), (p4, p5))[ping]
                            scalar.activation(
                                out=ptb[:, (ping * 2 + hh) * 512:(ping * 2 + hh) * 512 + 512],
                                in_=pS[hh][:, 0:512], func=AF.Exp,
                                scale=1.0 / 32.0,
                                bias=mjc[:, hh * 32 + kt:hh * 32 + kt + 1])
                            scalar.drain()
                            scalar.nop().then_inc(s_se.sem, 1)
                    em("scalar", b3_e)
                    s_se.inc(2)
                elif step["op"] == "F":
                    t = step["tile"]

                    def fev(vector, step=step, t=t):
                        vector.wait_ge(s_pe.sem, step["av"][1])
                        if t >= 2:
                            for hh2 in range(2):
                                swait(vector, st_stream, f_sd_tokens[(t - 2, hh2, "o")])
                                swait(vector, ld_stream, f_sd_tokens[(t - 2, hh2, "n")])
                        for hh in range(2):
                            pO = (p2, p3)[hh]
                            vector.tensor_copy(
                                out=FB[0:64, (t % 2) * 1024 + hh * 512:(t % 2) * 1024 + hh * 512 + 512],
                                in_=pO[0:64, 0:512])
                            vector.tensor_copy(
                                out=FB[64:65, 2048 + ((t % 2) * 2 + hh) * 512:
                                       2048 + ((t % 2) * 2 + hh) * 512 + 512],
                                in_=pO[64:65, 0:512])
                        vector.drain()
                        vector.nop().then_inc(s_ve.sem, 1)
                    em("vector", fev)
                    fv = s_ve.inc()
                    assert fv == fin_ve_of[t], (fv, fin_ve_of[t], t)

                    for hh in range(2):
                        def mk_sto(scalar, t=t, hh=hh):
                            return scalar.dma_start(
                                out=o_scr[2 * t + hh, :, :],
                                in_=FB[0:64, (t % 2) * 1024 + hh * 512:(t % 2) * 1024 + hh * 512 + 512])
                        wrap, tok = stream_dma(mk_sto, st_stream)
                        f_sd_tokens[(t, hh, "o")] = tok

                        def sto(scalar, wrap=wrap, fv=fv):
                            scalar.wait_ge(s_ve.sem, fv)
                            wrap(scalar)
                        em("scalar", sto)

                        def mk_stn(sync, t=t, hh=hh):
                            return sync.dma_start(
                                out=nrows[2 * t + hh:2 * t + hh + 1, :],
                                in_=FB[64:65, 2048 + ((t % 2) * 2 + hh) * 512:
                                       2048 + ((t % 2) * 2 + hh) * 512 + 512])
                        wrapn, tokn = stream_dma(mk_stn, ld_stream)
                        f_sd_tokens[(t, hh, "n")] = tokn

                        def stn(sync, wrapn=wrapn, fv=fv):
                            sync.wait_ge(s_ve.sem, fv)
                            wrapn(sync)
                        em("sync", stn)
                    fstage_toks.append(fv)
            attn_done_ve = s_ve.v
            last_o_toks = [f_sd_tokens[(31, 0, "o")], f_sd_tokens[(31, 1, "o")]]
            last_n_toks = [f_sd_tokens[(31, 0, "n")], f_sd_tokens[(31, 1, "n")]]

            def nrecip(vector, attn_done_ve=attn_done_ve, last_n_toks=last_n_toks):
                vector.wait_ge(s_ve.sem, attn_done_ve)
                for t_ in last_n_toks:
                    swait(vector, ld_stream, t_)
                vector.reciprocal(out=nrows[:, :], in_=nrows[:, :])
            nrec_done = ve(nrecip)

            norm_ve = {}
            for t64 in range(64):
                def mk_ntmp(sync, t64=t64):
                    return sync.dma_start(out=ROWF[0:1, 2048:2560],
                                          in_=nrows[t64:t64 + 1, :])
                wrapn, tokn = stream_dma(mk_ntmp, ld_stream)

                def ldn(sync, wrapn=wrapn, t64=t64, nrec_done=nrec_done):
                    if t64 == 0:
                        sync.wait_ge(s_ve.sem, nrec_done)
                    else:
                        sync.wait_ge(s_pe.sem, nb_pe[t64 - 1])
                    wrapn(sync)
                em("sync", ldn)

                def mk_ldo(sync, t64=t64):
                    return sync.dma_start(
                        out=FB[0:64, (t64 % 2) * 512:(t64 % 2) * 512 + 512],
                        in_=o_scr[t64, :, :])
                wrapo, toko = stream_dma(mk_ldo, ld_stream)

                def ldo(sync, wrapo=wrapo, t64=t64, last_o_toks=last_o_toks):
                    if t64 == 0:
                        for t_ in last_o_toks:
                            swait(sync, st_stream, t_)
                    if t64 >= 2:
                        sync.wait_ge(s_ve.sem, norm_ve[t64 - 2])
                    wrapo(sync)
                em("sync", ldo)

                nprev = norm_ve.get(t64 - 1) if t64 >= 1 else None

                def nb(tensor, tokn=tokn, t64=t64, nprev=nprev):
                    swait(tensor, ld_stream, tokn)
                    if nprev is not None:
                        tensor.wait_ge(s_ve.sem, nprev)
                    tensor.matmul(p6[0:64, 0:512], onesff[0:1, 0:64],
                                  ROWF[0:1, 2048:2560], start=True, stop=True
                                  ).then_inc(s_pe.sem, 1)
                em("tensor", nb)
                nb_pe_t = s_pe.inc()
                if t64 == 0:
                    nb_pe = {}
                nb_pe[t64] = nb_pe_t

                def nmul(vector, t64=t64, toko=toko, nb_pe_t=nb_pe_t):
                    swait(vector, ld_stream, toko)
                    vector.wait_ge(s_pe.sem, nb_pe_t)
                    qtglob = t64 // 2
                    hh = t64 % 2
                    beta, qi = qtglob // 8, qtglob % 8
                    slot = 2 * beta + (1 if qi >= 4 else 0)
                    col = slot * 2048 + (qi % 4) * 512
                    vector.tensor_tensor(out=A3[hh * 64:hh * 64 + 64, col:col + 512],
                                         in0=FB[0:64, (t64 % 2) * 512:(t64 % 2) * 512 + 512],
                                         in1=p6[0:64, 0:512], op=ALU.mult)
                norm_ve[t64] = ve(nmul)
            e3_stage_ve = s_ve.v

            gd1 = s_gd.v
            cc_e3 = s_cc.v + 1

            def e3(gpsimd, e3_stage_ve=e3_stage_ve, gd1=gd1, cc_e3=cc_e3):
                gpsimd.wait_ge(s_ve.sem, e3_stage_ve)
                for slot in range(8):
                    gpsimd.dma_start(out=oa_in[slot, :, :],
                                     in_=A3[:, slot * 2048:(slot + 1) * 2048]).then_inc(s_gd.sem, 16)
                gpsimd.wait_ge(s_gd.sem, gd1 + 8 * 16)
                with allow_ici_all_to_all():
                    gpsimd.collective_compute(
                        "AllToAll", ALU.bypass,
                        replica_groups=[list(range(8))],
                        ins=[oa_in.ap().opt()], outs=[oa_out.ap().opt()],
                    ).then_inc(s_cc.sem, 1)
                gpsimd.wait_ge(s_cc.sem, cc_e3)
                for i in range(8):
                    gpsimd.dma_start(out=A0[:, i * 2048:(i + 1) * 2048],
                                     in_=oa_out[i, :, :]).then_inc(s_gd.sem, 16)
            em("gpsimd", e3)
            s_gd.v += 8 * 16
            s_cc.v += 1
            s_gd.v += 8 * 16
            e3_recv = s_gd.v

            def e3t_gate(sync, e3_recv=e3_recv):
                sync.wait_ge(s_gd.sem, e3_recv)
            em("sync", e3t_gate)
            for si in range(8):
                def trs(sync, si=si):
                    sync.dma_start_transpose(
                        A1[:].rearrange("p (tc s f) -> p tc s f", s=8, f=128)[:, :, si, :],
                        A0[:, si * 2048:(si + 1) * 2048],
                    ).then_inc(s_tr.sem, 16)
                em("sync", trs)
                s_tr.inc(16)
            e3T_done = s_tr.v

            fin_toks = []
            for tc in range(16):
                def mk_ldr(sync, tc=tc):
                    return sync.dma_start(
                        out=FB[:, (tc % 2) * 1024:(tc % 2) * 1024 + 1024],
                        in_=resid_h[tc * 128:(tc + 1) * 128, :])
                wrap, tok = stream_dma(mk_ldr, ld_stream)
                prev = fin_toks[-2][1] if len(fin_toks) >= 2 else None

                def ldr(sync, wrap=wrap, e3T_done=e3T_done, prev=prev, tc=tc):
                    if tc == 0:
                        sync.wait_ge(s_tr.sem, e3T_done)
                    if prev is not None:
                        swait(sync, st_stream, prev)
                    wrap(sync)
                em("sync", ldr)

                def fadd(vector, tc=tc, tok=tok):
                    swait(vector, ld_stream, tok)
                    vector.tensor_tensor(out=FB[:, (tc % 2) * 1024:(tc % 2) * 1024 + 1024],
                                         in0=FB[:, (tc % 2) * 1024:(tc % 2) * 1024 + 1024],
                                         in1=A1[:, tc * 1024:(tc + 1) * 1024], op=ALU.add)
                fa = ve(fadd)

                def mk_str(scalar, tc=tc):
                    return scalar.dma_start(
                        out=resid2_h[tc * 128:(tc + 1) * 128, :],
                        in_=FB[:, (tc % 2) * 1024:(tc % 2) * 1024 + 1024])
                wrap2, tok2 = stream_dma(mk_str, st_stream)
                fin_toks.append((tc, tok2))

                def strx(scalar, wrap2=wrap2, fa=fa):
                    scalar.wait_ge(s_ve.sem, fa)
                    wrap2(scalar)
                em("scalar", strx)
            return fin_toks

        b1_toks = block1()
        if n_blocks >= 2:
            means_done, kqvb_tok = ffn_wprep([t for _, t in b1_toks])
            b2_toks = ffn_block(resid_h, resid_h if n_blocks >= 3 else resid2_h,
                                means_done, [t for _, t in b1_toks])
        if n_blocks >= 3:
            b3_toks = block3(kqvb_tok, b2_toks)
        if n_blocks >= 4:
            b4_toks = ffn_block(resid2_h, y_out, means_done,
                                [t for _, t in b3_toks])

        if n_blocks == 1:
            def yc(gpsimd, toks=b1_toks):
                for _, t in toks:
                    swait(gpsimd, st_stream, t)
                gpsimd.dma_start(out=y_out[:, :], in_=resid_h[:, :]).then_inc(s_out.sem, 16)
                gpsimd.wait_ge(s_out.sem, 16)
            em("gpsimd", yc)
            s_out.v += 16
        elif n_blocks == 2:
            def yc(gpsimd, toks=b2_toks):
                for _, t in toks:
                    swait(gpsimd, st_stream, t)
                gpsimd.dma_start(out=y_out[:, :], in_=resid2_h[:, :]).then_inc(s_out.sem, 16)
                gpsimd.wait_ge(s_out.sem, 16)
            em("gpsimd", yc)
            s_out.v += 16
        elif n_blocks == 3:
            def yc(gpsimd, toks=b3_toks):
                for _, t in toks:
                    swait(gpsimd, st_stream, t)
                gpsimd.dma_start(out=y_out[:, :], in_=resid2_h[:, :]).then_inc(s_out.sem, 16)
                gpsimd.wait_ge(s_out.sem, 16)
            em("gpsimd", yc)
            s_out.v += 16
        else:
            def yc(gpsimd, toks=b4_toks):
                for _, t in toks:
                    swait(gpsimd, st_stream, t)
            em("gpsimd", yc)

        with nc.Block() as block:
            @block.sync
            def _(sync):
                for fn in prog["sync"]:
                    fn(sync)

            @block.vector
            def _(vector):
                for fn in prog["vector"]:
                    fn(vector)

            @block.scalar
            def _(scalar):
                for fn in prog["scalar"]:
                    fn(scalar)

            @block.tensor
            def _(tensor):
                for fn in prog["tensor"]:
                    fn(tensor)

            @block.gpsimd
            def _(gpsimd):
                for fn in prog["gpsimd"]:
                    fn(gpsimd)

    return nc


def make_host_constants():
    ident = np.eye(128, dtype=ml_dtypes.bfloat16)
    identf = np.eye(128, dtype=np.float32)
    iota_k = (np.arange(32)[None, :] * 128 + np.arange(128)[:, None]).astype(np.float32)
    iota_q = np.arange(S, dtype=np.float32)[None, :]
    masks = np.zeros((4, 128, 512), dtype=np.float32)
    for a in range(4):
        kpos = a * 128 + np.arange(128)[:, None]
        qpos = np.arange(512)[None, :]
        masks[a][kpos > qpos] = NEG
    return ident, identf, iota_k, iota_q, masks.astype(ml_dtypes.bfloat16)


def kernel(L0_attn_a_x, L0_attn_a_Wq, L0_attn_a_Wk, L0_attn_a_Wv, L0_attn_a_Wp,
           L0_attn_a_bp, L0_attn_b_W_kqv, L0_attn_b_m, L0_ffn_w_gate,
           L0_ffn_w_up, L0_ffn_w_down):
    x = np.ascontiguousarray(np.asarray(L0_attn_a_x, dtype=np.float32))
    ident, identf, iota_k, iota_q, masks = make_host_constants()
    nc = build_nc(N_BLOCKS)
    common = {
        "wq": np.ascontiguousarray(L0_attn_a_Wq), "wk": np.ascontiguousarray(L0_attn_a_Wk),
        "wv": np.ascontiguousarray(L0_attn_a_Wv), "wp": np.ascontiguousarray(L0_attn_a_Wp),
        "bp": np.ascontiguousarray(np.asarray(L0_attn_a_bp).reshape(1, D)),
        "wkqv": np.ascontiguousarray(L0_attn_b_W_kqv),
        "m": np.ascontiguousarray(np.asarray(L0_attn_b_m).reshape(1, H)),
        "wg": np.ascontiguousarray(L0_ffn_w_gate), "wu": np.ascontiguousarray(L0_ffn_w_up),
        "wd": np.ascontiguousarray(L0_ffn_w_down),
        "ident": ident, "identf_in": identf, "iota_q": iota_q, "iota_k": iota_k,
        "masks": masks,
    }
    in_maps = []
    for c in range(8):
        bb, hh = c >> 1, c & 1
        mm = dict(common)
        mm["x"] = np.ascontiguousarray(x[bb, hh * TOK:(hh + 1) * TOK, :])
        mfull = np.asarray(L0_attn_b_m, dtype=np.float32).reshape(H)
        mm["m_my"] = np.ascontiguousarray(mfull[2 * c:2 * c + 2].reshape(1, 2))
        in_maps.append(mm)
    res = run_bass_kernel_spmd(nc, in_maps, core_ids=list(range(8)))
    out = np.empty((B, S, D), dtype=np.float32)
    for c in range(8):
        bb, hh = c >> 1, c & 1
        out[bb, hh * TOK:(hh + 1) * TOK, :] = res.results[c]["y"].reshape(TOK, D)
    return out


# version 06
# speedup vs baseline: 1.1022x; 1.1022x over previous
import numpy as np
import ml_dtypes

import concourse.bass as bass
import concourse.mybir as mybir
from concourse.bass_utils import run_bass_kernel_spmd
from concourse.aot_env import allow_ici_all_to_all

F32 = mybir.dt.float32
BF16 = mybir.dt.bfloat16
I32 = mybir.dt.int32
AF = mybir.ActivationFunctionType
ALU = mybir.AluOpType
AX = mybir.AxisListType

B, S, D = 4, 4096, 1024
H, HD = 16, 64
INNER = 4096
TOK = 2048
NEG = -1.0e30
N_BLOCKS = 4


class Ctr:
    def __init__(self, sem):
        self.sem = sem
        self.v = 0

    def inc(self, n=1):
        self.v += n
        return self.v


class DmaStream:

    def __init__(self, a, b):
        self.sems = [a, b]
        self.k = 0

    def issue(self, dma):
        s = self.sems[self.k % 2]
        dma.then_inc(s.sem, 16)
        s.v += 16
        tok = (self.k % 2, s.v)
        self.k += 1
        return tok

    def wait(self, eng, tok):
        par, val = tok
        eng.wait_ge(self.sems[par].sem, val)


def build_nc(n_blocks=N_BLOCKS):
    nc = bass.Bass("TRN2", target_bir_lowering=False, debug=False)

    x_in = nc.dram_tensor("x", [TOK, D], F32, kind="ExternalInput").ap()
    wq_h = nc.dram_tensor("wq", [D, D], F32, kind="ExternalInput").ap()
    wk_h = nc.dram_tensor("wk", [D, D], F32, kind="ExternalInput").ap()
    wv_h = nc.dram_tensor("wv", [D, D], F32, kind="ExternalInput").ap()
    wp_h = nc.dram_tensor("wp", [D, D], F32, kind="ExternalInput").ap()
    bp_h = nc.dram_tensor("bp", [1, D], F32, kind="ExternalInput").ap()
    wkqv_h = nc.dram_tensor("wkqv", [D, 3 * D], F32, kind="ExternalInput").ap()
    m_h = nc.dram_tensor("m", [1, H], F32, kind="ExternalInput").ap()
    mmy_h = nc.dram_tensor("m_my", [1, 2], F32, kind="ExternalInput").ap()
    wg_h = nc.dram_tensor("wg", [INNER, D], F32, kind="ExternalInput").ap()
    wu_h = nc.dram_tensor("wu", [INNER, D], F32, kind="ExternalInput").ap()
    wd_h = nc.dram_tensor("wd", [D, INNER], F32, kind="ExternalInput").ap()
    ident_h = nc.dram_tensor("ident", [128, 128], BF16, kind="ExternalInput").ap()
    identf_h = nc.dram_tensor("identf_in", [128, 128], F32, kind="ExternalInput").ap()
    iotaq_h = nc.dram_tensor("iota_q", [1, S], F32, kind="ExternalInput").ap()
    iotak_h = nc.dram_tensor("iota_k", [128, 32], F32, kind="ExternalInput").ap()
    masks_h = nc.dram_tensor("masks", [4, 128, 512], BF16, kind="ExternalInput").ap()
    y_out = nc.dram_tensor("y", [TOK, D], F32, kind="ExternalOutput").ap()

    resid_h = nc.dram_tensor("resid", [TOK, D], F32)
    resid2_h = nc.dram_tensor("resid2", [TOK, D], F32)
    kv_ar_in = nc.dram_tensor("kv_ar_in", [D, D], F32)
    kv_ar_out = nc.dram_tensor("kv_ar_out", [D, D], F32)
    wgT_hbm = nc.dram_tensor("wgT_hbm", [32, 128, D], BF16)
    wuT_hbm = nc.dram_tensor("wuT_hbm", [32, 128, D], BF16)
    wdT_hbm = nc.dram_tensor("wdT_hbm", [32, 128, D], BF16)
    gu_hbm = nc.dram_tensor("gu_hbm", [32, 128, TOK], F32)
    kqvb_hbm = nc.dram_tensor("kqvb_hbm", [D, 3 * D], BF16)
    a2a_in = nc.dram_tensor("a2a_in", [8, 128, 6144], BF16)
    a2a_out = nc.dram_tensor("a2a_out", [8, 128, 6144], BF16)
    oa_in = nc.dram_tensor("oa_in", [8, 128, 2048], BF16)
    oa_out = nc.dram_tensor("oa_out", [8, 128, 2048], BF16)
    o_scr = nc.dram_tensor("o_scr", [64, 64, 512], F32)
    row_hbm = nc.dram_tensor("row_hbm", [16, 128], F32)

    from contextlib import ExitStack
    with ExitStack() as stx:
        sbt = lambda name, shape, dt: stx.enter_context(nc.sbuf_tensor(name, shape, dt))
        pst = lambda name, shape, dt=F32: stx.enter_context(nc.psum_tensor(name, shape, dt))
        mksem = lambda name: Ctr(stx.enter_context(nc.semaphore(name)))

        A0 = sbt("A0", [128, 16384], BF16)
        A1 = sbt("A1", [128, 16384], BF16)
        A2 = sbt("A2", [128, 16640], BF16)
        A3 = sbt("A3", [128, 16384], BF16)
        FB = sbt("FB", [128, 8192], F32)
        WKB = sbt("WKB", [128, 2048], BF16)
        ROWF = sbt("ROWF", [128, 2560], F32)
        ROWB = sbt("ROWB", [128, 8192], BF16)
        masksb = sbt("masksb", [128, 2048], BF16)
        ptb = sbt("ptb", [128, 2048], BF16)
        nrows = sbt("nrows", [64, 512], F32)
        rtmp = sbt("rtmp", [16, 128], F32)
        stat = sbt("stat", [128, 96], F32)
        identb = sbt("identb", [128, 128], BF16)
        identf = sbt("identf", [128, 128], F32)
        iotak = sbt("iotak", [128, 32], F32)
        mjc = sbt("mjc", [128, 64], F32)
        onesbf = sbt("onesbf", [1, 128], BF16)
        onesff = sbt("onesff", [1, 128], F32)
        mrow = sbt("mrow", [1, 32], F32)
        qi32 = sbt("qi32", [128, 1024], I32)

        p0 = pst("p0", [128, 512])
        p1 = pst("p1", [128, 512])
        p2 = pst("p2", [128, 512])
        p3 = pst("p3", [128, 512])
        p4 = pst("p4", [128, 512])
        p5 = pst("p5", [128, 512])
        p6 = pst("p6", [128, 512])
        p7 = pst("p7", [128, 512])
        PSUMS = [p0, p1, p2, p3, p4, p5, p6, p7]

        s_ld = mksem("s_ld")
        s_la = mksem("s_la")
        s_lb = mksem("s_lb")
        s_tr = mksem("s_tr")
        s_pe = mksem("s_pe")
        s_ve = mksem("s_ve")
        s_se = mksem("s_se")
        s_sa = mksem("s_sa")
        s_sb = mksem("s_sb")
        s_sd = mksem("s_sd")
        s_gd = mksem("s_gd")
        s_ga = mksem("s_ga")
        s_gb = mksem("s_gb")
        s_cc = mksem("s_cc")
        s_out = mksem("s_out")
        s_va = mksem("s_va")
        s_vb = mksem("s_vb")

        ld_stream = DmaStream(s_la, s_lb)
        st_stream = DmaStream(s_sa, s_sb)
        gd_stream = DmaStream(s_ga, s_gb)
        wl_stream = DmaStream(s_va, s_vb)

        prog = {k: [] for k in ("sync", "vector", "scalar", "tensor", "gpsimd")}

        def em(eng, fn):
            prog[eng].append(fn)

        def ve(fn):
            def wrap(vector, fn=fn):
                fn(vector)
                vector.drain()
                vector.nop().then_inc(s_ve.sem, 1)
            em("vector", wrap)
            return s_ve.inc()

        def se(fn):
            def wrap(scalar, fn=fn):
                fn(scalar)
                scalar.drain()
                scalar.nop().then_inc(s_se.sem, 1)
            em("scalar", wrap)
            return s_se.inc()

        def mm_phase(groups, pool, pe_waits=(), width=512):
            npool = len(pool)
            pe_base = s_pe.v
            ev_base = s_ve.v
            waits = list(pe_waits)

            def t_ops(tensor, groups=groups, pool=pool, pe_base=pe_base,
                      ev_base=ev_base, waits=waits, npool=npool, width=width):
                for semh, val in waits:
                    tensor.wait_ge(semh, val)
                for g, (pieces, _e) in enumerate(groups):
                    pp = pool[g % npool]
                    if g >= npool:
                        tensor.wait_ge(s_ve.sem, ev_base + g - npool + 1)
                    n = len(pieces)
                    for i, (lt, rr) in enumerate(pieces):
                        mm = tensor.matmul(pp[:, 0:width], lt, rr,
                                           start=(i == 0), stop=(i == n - 1))
                        if i == n - 1:
                            mm.then_inc(s_pe.sem, 1)
            em("tensor", t_ops)
            s_pe.v += len(groups)

            def v_ops(vector, groups=groups, pool=pool, pe_base=pe_base, npool=npool):
                for g, (_p, evf) in enumerate(groups):
                    pp = pool[g % npool]
                    vector.wait_ge(s_pe.sem, pe_base + g + 1)
                    evf(vector, pp)
                    vector.drain()
                    vector.nop().then_inc(s_ve.sem, 1)
            em("vector", v_ops)
            s_ve.v += len(groups)
            return s_ve.v

        def copy_evac(dst, width=512):
            def evf(vector, pp, dst=dst, width=width):
                vector.tensor_copy(out=dst, in_=pp[:, 0:width])
            return evf

        def phase_gate(toks=()):
            vev, sev = s_ve.v, s_se.v
            tks = list(toks)

            def g(sync, vev=vev, sev=sev, tks=tks):
                sync.wait_ge(s_ve.sem, vev)
                sync.wait_ge(s_se.sem, sev)
                for t in tks:
                    swait(sync, st_stream, t)
            em("sync", g)

        def c_ld(sync):
            sync.dma_start(out=identb[:], in_=ident_h[:, :]).then_inc(s_ld.sem, 16)
            sync.dma_start(out=identf[:], in_=identf_h[:, :]).then_inc(s_ld.sem, 16)
            sync.dma_start(out=iotak[:], in_=iotak_h[:, :]).then_inc(s_ld.sem, 16)
            sync.dma_start(out=masksb[:].rearrange("p (a q) -> p a q", a=4), in_=masks_h.rearrange("a p q -> p a q")).then_inc(s_ld.sem, 16)
            sync.dma_start(out=mrow[0:1, 0:H], in_=m_h[:, :]).then_inc(s_ld.sem, 16)
            sync.dma_start(out=mrow[0:1, 24:26], in_=mmy_h[:, :]).then_inc(s_ld.sem, 16)
            sync.dma_start(out=ROWF[0:1, 0:D], in_=bp_h[:, :]).then_inc(s_ld.sem, 16)
        em("sync", c_ld)
        s_ld.v += 112
        consts_ld = s_ld.v

        def c_ones(vector):
            vector.wait_ge(s_ld.sem, consts_ld)
            vector.memset(onesbf[:], 1.0)
            vector.memset(onesff[:], 1.0)
            vector.drain()
            vector.tensor_copy(out=ROWB[0:1, 0:D], in_=ROWF[0:1, 0:D])
        consts_done = ve(c_ones)

        def stream_dma(emit_fn, stream):
            par = stream.k % 2
            sem = stream.sems[par]
            val = sem.v + 16

            def wrap(eng, emit_fn=emit_fn, sem=sem, val=val):
                if val > 16:
                    eng.wait_ge(sem.sem, val - 16)
                emit_fn(eng).then_inc(sem.sem, 16)
            stream.k += 1
            sem.v = val
            return wrap, (par, val)

        def swait(eng, stream, tok):
            par, val = tok
            eng.wait_ge(stream.sems[par].sem, val)

        def ln_half2(src, th, dst_ap_fn, quant=False, entry=None, entry_toks=()):
            entry_ve, entry_se = entry if entry is not None else (s_ve.v, s_se.v)
            entry_toks = list(entry_toks)
            for tc8 in range(8):
                tc = th * 8 + tc8
                buf = (tc8 % 2) * 1024
                done_prev = (s_ve.v if quant else s_se.v)

                def mk_ld(sync, tc=tc, buf=buf):
                    return sync.dma_start(out=FB[:, buf:buf + 1024],
                                          in_=src[tc * 128:(tc + 1) * 128, :])
                wrap, tok = stream_dma(mk_ld, ld_stream)

                def ldx(sync, wrap=wrap, tc8=tc8, done_prev=done_prev, quant=quant,
                        entry_ve=entry_ve, entry_se=entry_se, entry_toks=entry_toks):
                    if tc8 < 2:
                        sync.wait_ge(s_ve.sem, entry_ve)
                        sync.wait_ge(s_se.sem, entry_se)
                        for t_ in entry_toks:
                            swait(sync, st_stream, t_)
                    else:
                        if quant:
                            sync.wait_ge(s_ve.sem, done_prev - 1)
                        else:
                            sync.wait_ge(s_se.sem, done_prev - 1)
                    wrap(sync)
                em("sync", ldx)

                def lsq(scalar, tok=tok, buf=buf):
                    swait(scalar, ld_stream, tok)
                    scalar.activation(out=FB[:, 2048:3072], in_=FB[:, buf:buf + 1024],
                                      func=AF.Square, accum_out=stat[:, 1:2])
                sqd = se(lsq)

                def lstats(vector, tok=tok, buf=buf, sqd=sqd):
                    swait(vector, ld_stream, tok)
                    xt = FB[:, buf:buf + 1024]
                    vector.reduce_sum(out=stat[:, 0:1], in_=xt, axis=AX.X)
                    vector.drain()
                    vector.wait_ge(s_se.sem, sqd)
                    vector.tensor_scalar(out=stat[:, 2:3], in0=stat[:, 0:1],
                                         scalar1=1.0 / D, scalar2=None, op0=ALU.mult)
                    vector.drain()
                    vector.scalar_tensor_tensor(out=stat[:, 3:4], in0=stat[:, 2:3],
                                                scalar=-1.0, in1=stat[:, 2:3],
                                                op0=ALU.mult, op1=ALU.mult)
                    vector.drain()
                    vector.scalar_tensor_tensor(out=stat[:, 3:4], in0=stat[:, 1:2],
                                                scalar=1.0 / D, in1=stat[:, 3:4],
                                                op0=ALU.mult, op1=ALU.add)
                    vector.drain()
                    vector.tensor_scalar(out=stat[:, 3:4], in0=stat[:, 3:4],
                                         scalar1=1e-5, scalar2=None, op0=ALU.add)
                    vector.drain()
                    vector.reciprocal(out=stat[:, 3:4], in_=stat[:, 3:4])
                vst = ve(lstats)

                def lsqrt(scalar, vst=vst):
                    scalar.wait_ge(s_ve.sem, vst)
                    scalar.activation(out=stat[:, 4:5], in_=stat[:, 3:4], func=AF.Sqrt)
                sst = se(lsqrt)

                def lbias(vector, sst=sst):
                    vector.wait_ge(s_se.sem, sst)
                    vector.scalar_tensor_tensor(out=stat[:, 5:6], in0=stat[:, 2:3],
                                                scalar=-1.0, in1=stat[:, 4:5],
                                                op0=ALU.mult, op1=ALU.mult)
                vbs = ve(lbias)

                if not quant:
                    def lap(scalar, tc8=tc8, buf=buf, vbs=vbs):
                        scalar.wait_ge(s_ve.sem, vbs)
                        scalar.activation(out=dst_ap_fn(tc8), in_=FB[:, buf:buf + 1024],
                                          func=AF.Identity,
                                          scale=stat[:, 4:5], bias=stat[:, 5:6])
                    se(lap)
                else:
                    def lap(scalar, tc8=tc8, buf=buf, vbs=vbs):
                        scalar.wait_ge(s_ve.sem, vbs)
                        scalar.activation(out=FB[:, 3072:4096], in_=FB[:, buf:buf + 1024],
                                          func=AF.Identity,
                                          scale=stat[:, 4:5], bias=stat[:, 5:6])
                    sap = se(lap)

                    def qsteps(vector, tc8=tc8, sap=sap):
                        vector.wait_ge(s_se.sem, sap)
                        xln = FB[:, 3072:4096]
                        vector.reduce_max(out=stat[:, 6:7], in_=xln, axis=AX.X,
                                          apply_absolute_value=True)
                        vector.drain()
                        vector.tensor_scalar(out=stat[:, 6:7], in0=stat[:, 6:7],
                                             scalar1=1e-5, scalar2=None, op0=ALU.max)
                        vector.drain()
                        vector.tensor_scalar(out=stat[:, 80 + tc8:81 + tc8],
                                             in0=stat[:, 6:7], scalar1=1.0 / 127.0,
                                             scalar2=None, op0=ALU.mult)
                        vector.drain()
                        vector.reciprocal(out=stat[:, 7:8], in_=stat[:, 6:7])
                        vector.drain()
                        vector.tensor_scalar(out=stat[:, 7:8], in0=stat[:, 7:8],
                                             scalar1=127.0, scalar2=None, op0=ALU.mult)
                        vector.drain()
                        vector.tensor_scalar(out=qi32[:, 0:1024], in0=xln,
                                             scalar1=stat[:, 7:8], scalar2=128.5,
                                             op0=ALU.mult, op1=ALU.add)
                        vector.drain()
                        vector.tensor_scalar(out=dst_ap_fn(tc8), in0=qi32[:, 0:1024],
                                             scalar1=-128.0, scalar2=None, op0=ALU.add)
                    ve(qsteps)
            return s_se.v, s_ve.v

        def prep_wT(whbm, slot_ap_fn):
            for slab in range(4):
                r0 = slab * 256
                trb = s_tr.v

                def mk_ld(gpsimd, whbm=whbm, r0=r0):
                    return gpsimd.dma_start(
                        out=WKB[:, 0:2048].rearrange("p (s d) -> p s d", s=2),
                        in_=whbm[r0:r0 + 256, :].rearrange("(s p) d -> p s d", p=128))
                wrap, tok = stream_dma(mk_ld, gd_stream)

                def ldw(gpsimd, wrap=wrap, trb=trb):
                    gpsimd.wait_ge(s_tr.sem, trb)
                    wrap(gpsimd)
                em("gpsimd", ldw)

                for ss in range(2):
                    e0 = r0 + ss * 128

                    def trw(sync, ss=ss, e0=e0, tok=tok):
                        if ss == 0:
                            swait(sync, gd_stream, tok)
                        sync.dma_start_transpose(
                            slot_ap_fn().rearrange("p (c e) -> p c e", c=8)[:, :, e0:e0 + 128],
                            WKB[:, ss * D:(ss + 1) * D],
                        ).then_inc(s_tr.sem, 16)
                    em("sync", trw)
                    s_tr.inc(16)
            return s_tr.v

        def transpose16(src_ap_fn, dst, gates=()):
            def trg(sync, gates=gates):
                for g in gates:
                    g(sync)
            em("sync", trg)
            for tc in range(16):
                def trx(sync, tc=tc):
                    sync.dma_start_transpose(
                        dst[:].rearrange("p (c t) -> p c t", c=8)[:, :, tc * 128:(tc + 1) * 128],
                        src_ap_fn(tc),
                    ).then_inc(s_tr.sem, 16)
                em("sync", trx)
                s_tr.inc(16)
            return s_tr.v

        def block1(mid_emit=None):
            A3lo = lambda: A3[:, 0:8192]
            A3hi = lambda: A3[:, 8192:16384]
            prep_wT(wq_h, A3hi)
            wkT_done = prep_wT(wk_h, A3lo)

            lnse0, _ = ln_half2(x_in, 0, lambda tc8: A2[:, tc8 * D:(tc8 + 1) * D])
            lnse1, _ = ln_half2(x_in, 1, lambda tc8: A2[:, 8192 + tc8 * D: 8192 + (tc8 + 1) * D])

            kv_last = None
            for th in range(2):
                kvprev = kv_last
                lnse = (lnse0, lnse1)[th]
                a2base = th * 8192

                def trg(sync, lnse=lnse, wk=wkT_done, th=th, kvprev=kvprev):
                    sync.wait_ge(s_se.sem, lnse)
                    if th == 0:
                        sync.wait_ge(s_tr.sem, wk)
                    else:
                        sync.wait_ge(s_ve.sem, kvprev)
                em("sync", trg)
                for tc8 in range(8):
                    def trx(sync, tc8=tc8, a2base=a2base):
                        sync.dma_start_transpose(
                            A0[:, 0:8192].rearrange("p (c t) -> p c t", c=8)[:, :, tc8 * 128:(tc8 + 1) * 128],
                            A2[:, a2base + tc8 * D: a2base + (tc8 + 1) * D],
                        ).then_inc(s_tr.sem, 16)
                    em("sync", trx)
                    s_tr.inc(16)
                xT_done = s_tr.v

                groups = []
                for ec in range(8):
                    for tq in range(2):
                        pieces = [(A3[:, 8192 + c * D + ec * 128: 8192 + c * D + ec * 128 + 128],
                                   A0[:, c * 1024 + tq * 512: c * 1024 + tq * 512 + 512])
                                  for c in range(8)]
                        dst = A1[:, ec * 2048 + th * 1024 + tq * 512:
                                 ec * 2048 + th * 1024 + tq * 512 + 512]
                        groups.append((pieces, copy_evac(dst)))
                q_done = mm_phase(groups, [p0, p1],
                                  pe_waits=[(s_tr.sem, xT_done), (s_ve.sem, consts_done)])

                groups = []
                for tc8 in range(8):
                    for half in range(2):
                        pieces = [(A0[:, c * 1024 + tc8 * 128: c * 1024 + tc8 * 128 + 128],
                                   A3[:, c * D + half * 512: c * D + half * 512 + 512])
                                  for c in range(8)]
                        dst = A0[:, 8192 + tc8 * 1024 + half * 512:
                                 8192 + tc8 * 1024 + half * 512 + 512]
                        groups.append((pieces, copy_evac(dst)))
                k_done = mm_phase(groups, [p2, p3])

                def wv_gate(sync, q_done=q_done):
                    sync.wait_ge(s_ve.sem, q_done)
                em("sync", wv_gate)
                wvT_done = prep_wT(wv_h, A3hi)

                groups = []
                for tc8 in range(8):
                    for half in range(2):
                        pieces = [(A0[:, c * 1024 + tc8 * 128: c * 1024 + tc8 * 128 + 128],
                                   A3[:, 8192 + c * D + half * 512: 8192 + c * D + half * 512 + 512])
                                  for c in range(8)]
                        dst = A2[:, tc8 * 1024 + half * 512: tc8 * 1024 + half * 512 + 512]
                        groups.append((pieces, copy_evac(dst)))
                v_done = mm_phase(groups, [p4, p5], pe_waits=[(s_tr.sem, wvT_done)])

                groups = []
                for ec in range(8):
                    for fh in range(2):
                        pieces = [(A0[:, 8192 + tc8 * 1024 + ec * 128:
                                      8192 + tc8 * 1024 + ec * 128 + 128],
                                   A2[:, tc8 * 1024 + fh * 512: tc8 * 1024 + fh * 512 + 512])
                                  for tc8 in range(8)]
                        dst = FB[:, ec * 1024 + fh * 512: ec * 1024 + fh * 512 + 512]
                        if th == 0:
                            evf = copy_evac(dst)
                        else:
                            def evf(vector, pp, dst=dst):
                                vector.tensor_tensor(out=dst, in0=pp[:, 0:512],
                                                     in1=dst, op=ALU.add)
                        groups.append((pieces, evf))
                kvw = [(s_ve.sem, v_done)]
                if th == 0:
                    kvw.append((s_se.sem, lnse1))
                kv_last = mm_phase(groups, [p6, p7], pe_waits=kvw)

                if th == 0:
                    def wq_gate(sync, kv_last=kv_last):
                        sync.wait_ge(s_ve.sem, kv_last)
                    em("sync", wq_gate)
                    prep_wT(wq_h, A3hi)

            gd0, cc0 = s_gd.v, s_cc.v

            def e1(gpsimd, gd0=gd0, cc0=cc0, kvdone=kv_last):
                gpsimd.wait_ge(s_ve.sem, kvdone)
                gpsimd.dma_start(out=kv_ar_in.rearrange("(c p) f -> p c f", p=128),
                                 in_=FB[:, 0:8192].rearrange("p (c f) -> p c f", c=8)
                                 ).then_inc(s_gd.sem, 16)
                gpsimd.wait_ge(s_gd.sem, gd0 + 16)
                gpsimd.collective_compute(
                    "AllReduce", ALU.add,
                    replica_groups=[[0, 1], [2, 3], [4, 5], [6, 7]],
                    ins=[kv_ar_in.ap().opt()], outs=[kv_ar_out.ap().opt()],
                ).then_inc(s_cc.sem, 1)
                gpsimd.wait_ge(s_cc.sem, cc0 + 1)
                gpsimd.dma_start(out=A2[:, 8192:16384].rearrange("p (c f) -> p c f", c=8),
                                 in_=kv_ar_out.rearrange("(c p) f -> p c f", p=128)
                                 ).then_inc(s_gd.sem, 16)
            em("gpsimd", e1)
            s_gd.v += 32
            s_cc.v += 1
            kvfull_done = s_gd.v

            mid_out = mid_emit(gd0 + 16) if mid_emit is not None else None

            wpT_done = prep_wT(wp_h, A3hi)

            groups = []
            for fc in range(8):
                for tt in range(4):
                    pieces = [(A2[:, 8192 + ec * 1024 + fc * 128:
                                  8192 + ec * 1024 + fc * 128 + 128],
                               A1[:, ec * 2048 + tt * 512: ec * 2048 + tt * 512 + 512])
                              for ec in range(8)]
                    dst = A0[:, fc * 2048 + tt * 512: fc * 2048 + tt * 512 + 512]
                    groups.append((pieces, copy_evac(dst)))
            z_done = mm_phase(groups, [p0, p1, p2, p3],
                              pe_waits=[(s_gd.sem, kvfull_done)])

            pe_base = s_pe.v
            ev_base = s_ve.v
            pool = [p4, p5, p6, p7]

            def prj_t(tensor, pe_base=pe_base, ev_base=ev_base,
                      z_done=z_done, wpT_done=wpT_done):
                tensor.wait_ge(s_ve.sem, z_done)
                tensor.wait_ge(s_tr.sem, wpT_done)
                g = 0
                for tc in range(16):
                    for gh in range(2):
                        pp = pool[g % 4]
                        if g >= 4:
                            tensor.wait_ge(s_ve.sem, ev_base + g - 3)
                        for fc in range(8):
                            tensor.matmul(pp[:, 0:512],
                                          A0[:, fc * 2048 + tc * 128: fc * 2048 + tc * 128 + 128],
                                          A3[:, 8192 + fc * D + gh * 512:
                                             8192 + fc * D + gh * 512 + 512],
                                          start=(fc == 0), stop=False)
                        tensor.matmul(pp[:, 0:512], onesbf[0:1, 0:128],
                                      ROWB[0:1, gh * 512: gh * 512 + 512],
                                      start=False, stop=True).then_inc(s_pe.sem, 1)
                        g += 1
            em("tensor", prj_t)
            s_pe.v += 32

            st_wraps = []
            rtoks = []
            for tc in range(16):
                def mk_st(scalar, tc=tc):
                    return scalar.dma_start(out=resid_h[tc * 128:(tc + 1) * 128, :],
                                            in_=FB[:, (tc % 2) * 1024:(tc % 2) * 1024 + 1024])
                wrap, tok = stream_dma(mk_st, st_stream)
                st_wraps.append(wrap)
                rtoks.append((tc, tok))

            xtoks = []
            for tc in range(16):
                def mk_ld(sync, tc=tc):
                    return sync.dma_start(out=FB[:, (tc % 2) * 1024:(tc % 2) * 1024 + 1024],
                                          in_=x_in[tc * 128:(tc + 1) * 128, :])
                wrap, tok = stream_dma(mk_ld, ld_stream)
                xtoks.append(tok)
                pst_tok = rtoks[tc - 2][1] if tc >= 2 else None

                def ldx2(sync, wrap=wrap, tc=tc, ev_base=ev_base, pst_tok=pst_tok,
                         kv_read_done=gd0 + 16):
                    if tc < 2:
                        sync.wait_ge(s_gd.sem, kv_read_done)
                    else:
                        sync.wait_ge(s_ve.sem, ev_base + (tc - 2) * 2 + 2)
                        swait(sync, st_stream, pst_tok)
                    wrap(sync)
                em("sync", ldx2)

            def prj_v(vector, pe_base=pe_base, xtoks=xtoks):
                for tc in range(16):
                    swait(vector, ld_stream, xtoks[tc])
                    for gh in range(2):
                        g = tc * 2 + gh
                        pp = pool[g % 4]
                        vector.wait_ge(s_pe.sem, pe_base + g + 1)
                        sl = slice((tc % 2) * 1024 + gh * 512, (tc % 2) * 1024 + gh * 512 + 512)
                        vector.tensor_tensor(out=FB[:, sl], in0=pp[:, 0:512],
                                             in1=FB[:, sl], op=ALU.add)
                        vector.drain()
                        vector.nop().then_inc(s_ve.sem, 1)
            em("vector", prj_v)
            s_ve.v += 32

            for tc in range(16):
                tgt = ev_base + tc * 2 + 2

                def stx(scalar, wrap=st_wraps[tc], tgt=tgt):
                    scalar.wait_ge(s_ve.sem, tgt)
                    wrap(scalar)
                em("scalar", stx)
            return rtoks, mid_out

        def ffn_wprep_pass1(e1_gate_gd):
            slabs = []
            for wi, (whbm, cols) in enumerate(((wg_h, D), (wu_h, D), (wd_h, INNER))):
                for sl in range(16):
                    if cols == D:
                        src = whbm[sl * 256:(sl + 1) * 256, :].rearrange(
                            "(s p) d -> p s d", p=128)
                        is3d = True
                    else:
                        src = whbm[(sl // 2) * 128:(sl // 2) * 128 + 128,
                                   (sl % 2) * 2048:(sl % 2) * 2048 + 2048]
                        is3d = False
                    slabs.append((wi, sl, src, is3d))
            wraps, toks = [], []
            for i, (wi, sl, src, is3d) in enumerate(slabs):
                def mk_ld(scalar, src=src, is3d=is3d, i=i):
                    dst = FB[:, 2048 + (i % 2) * 2048: 2048 + (i % 2) * 2048 + 2048]
                    if is3d:
                        dst = dst.rearrange("p (s d) -> p s d", s=2)
                    return scalar.dma_start(out=dst, in_=src)
                wrap, tok = stream_dma(mk_ld, wl_stream)
                wraps.append(wrap)
                toks.append(tok)

            fins = []
            redv = []
            for i, (wi, sl, src, is3d) in enumerate(slabs):
                def red(vector, i=i, sl=sl, tok=toks[i]):
                    swait(vector, wl_stream, tok)
                    vector.reduce_sum(
                        out=stat[:, 40 + sl:41 + sl],
                        in_=FB[:, 2048 + (i % 2) * 2048: 2048 + (i % 2) * 2048 + 2048],
                        axis=AX.X, apply_absolute_value=True)
                redv.append(ve(red))
                if sl == 15:
                    def fin1(vector, wi=wi):
                        vector.reduce_sum(out=stat[:, 20 + wi:21 + wi],
                                          in_=stat[:, 40:56], axis=AX.X)
                    fins.append(ve(fin1))

            for i in range(len(slabs)):
                def ld(scalar, wrap=wraps[i], i=i, g=e1_gate_gd,
                       rprev=(redv[i - 2] if i >= 2 else None)):
                    if i == 0:
                        scalar.wait_ge(s_gd.sem, g)
                    if rprev is not None:
                        scalar.wait_ge(s_ve.sem, rprev)
                    wrap(scalar)
                em("scalar", ld)
            return fins

        def ffn_wprep_pass1_fin(fins):
            prev = None
            for wi in range(3):
                n = INNER * D

                def tsum(tensor, wi=wi, f1=fins[wi], prev=prev):
                    tensor.wait_ge(s_ve.sem, f1)
                    if prev is not None:
                        tensor.wait_ge(s_ve.sem, prev)
                    tensor.transpose(p0[0:1, 0:128], stat[:, 20 + wi:21 + wi],
                                     identf[:]).then_inc(s_pe.sem, 1)
                em("tensor", tsum)
                pe_t = s_pe.inc()

                def fin2(vector, pe_t=pe_t, wi=wi, n=n):
                    vector.wait_ge(s_pe.sem, pe_t)
                    vector.reduce_sum(out=stat[0:1, 10:11], in_=p0[0:1, 0:128],
                                      axis=AX.X)
                    vector.drain()
                    vector.tensor_scalar(out=stat[0:1, 13 + wi:14 + wi],
                                         in0=stat[0:1, 10:11],
                                         scalar1=1.0 / n, scalar2=1e-5,
                                         op0=ALU.mult, op1=ALU.max)
                prev = ve(fin2)
            return prev

        def ffn_wprep_rest(mean_fin):
            for wi in range(3):
                def bch(tensor, wi=wi, mean_fin=mean_fin):
                    tensor.wait_ge(s_ve.sem, mean_fin)
                    tensor.matmul(p1[:, 1 + wi:2 + wi], onesff[0:1, 0:128],
                                  stat[0:1, 13 + wi:14 + wi], start=True, stop=True
                                  ).then_inc(s_pe.sem, 1)
                em("tensor", bch)
                pe_b = s_pe.inc()

                def bce(vector, pe_b=pe_b, wi=wi):
                    vector.wait_ge(s_pe.sem, pe_b)
                    vector.tensor_copy(out=stat[:, 17 + wi:18 + wi],
                                       in_=p1[:, 1 + wi:2 + wi])
                means_done = ve(bce)

            def emit_scale(wi):
                def swb(vector, wi=wi):
                    vector.reciprocal(out=stat[0:1, 10:11],
                                      in_=stat[0:1, 13 + wi:14 + wi])
                sw_done = ve(swb)

                def bcast(tensor, sw_done=sw_done):
                    tensor.wait_ge(s_ve.sem, sw_done)
                    tensor.matmul(p1[:, 0:1], onesff[0:1, 0:128],
                                  stat[0:1, 10:11], start=True, stop=True
                                  ).then_inc(s_pe.sem, 1)
                em("tensor", bcast)
                pe_b = s_pe.inc()

                def bc_ev(vector, pe_b=pe_b):
                    vector.wait_ge(s_pe.sem, pe_b)
                    vector.tensor_copy(out=stat[:, 12:13], in_=p1[:, 0:1])
                return ve(bc_ev)

            def quant3(vector, src, dst):
                vector.tensor_scalar(out=qi32[:, 0:1024], in0=src,
                                     scalar1=stat[:, 12:13], scalar2=128.5,
                                     op0=ALU.mult, op1=ALU.add)
                vector.drain()
                vector.tensor_scalar(out=qi32[:, 0:1024], in0=qi32[:, 0:1024],
                                     scalar1=129.0, scalar2=127.0,
                                     op0=ALU.min, op1=ALU.max)
                vector.drain()
                vector.tensor_scalar(out=dst, in0=qi32[:, 0:1024],
                                     scalar1=-128.0, scalar2=None, op0=ALU.add)
                vector.drain()

            carry_ev = {}
            carry_tr = {}

            emit_scale(2)
            wraps, toks = [], []
            for ph in range(32):
                def mk_ld(gpsimd, ph=ph):
                    ds, iq = ph // 4, ph % 4
                    return gpsimd.dma_start(
                        out=FB[:, 2048 + (ph % 2) * 1024: 2048 + (ph % 2) * 1024 + 1024],
                        in_=wd_h[ds * 128:(ds + 1) * 128, iq * 1024:(iq + 1) * 1024])
                wrap, tok = stream_dma(mk_ld, gd_stream)
                wraps.append(wrap)
                toks.append(tok)

            def kickd(gpsimd, w0=wraps[0], w1=wraps[1], mf=mean_fin):
                gpsimd.wait_ge(s_ve.sem, mf)
                w0(gpsimd)
                w1(gpsimd)
            em("gpsimd", kickd)

            tr_pe, ev_se = {}, {}
            for ph in range(32):
                ds, iq = ph // 4, ph % 4

                def qzd(vector, ph=ph, tok=toks[ph], tprev=tr_pe.get(ph - 2)):
                    swait(vector, gd_stream, tok)
                    if tprev is not None:
                        vector.wait_ge(s_pe.sem, tprev)
                    quant3(vector,
                           FB[:, 2048 + (ph % 2) * 1024: 2048 + (ph % 2) * 1024 + 1024],
                           FB[:, 6144 + (ph % 2) * 1024: 6144 + (ph % 2) * 1024 + 1024])
                qv = ve(qzd)
                if ph + 2 < 32:
                    def ldn(gpsimd, wrap=wraps[ph + 2], qv=qv):
                        gpsimd.wait_ge(s_ve.sem, qv)
                        wrap(gpsimd)
                    em("gpsimd", ldn)

                def trd(tensor, ph=ph, qv=qv, eprev=ev_se.get(ph - 2)):
                    tensor.wait_ge(s_ve.sem, qv)
                    if eprev is not None:
                        tensor.wait_ge(s_se.sem, eprev)
                    bA, bB = ((p4, p5), (p6, p7))[ph % 2]
                    for j in range(8):
                        bk = bA if j < 4 else bB
                        mm = tensor.transpose(
                            bk[:, (j % 4) * 128:(j % 4) * 128 + 128],
                            FB[:, 6144 + (ph % 2) * 1024 + j * 128:
                               6144 + (ph % 2) * 1024 + j * 128 + 128],
                            identf[:])
                        if j == 7:
                            mm.then_inc(s_pe.sem, 1)
                em("tensor", trd)
                tr_pe[ph] = s_pe.inc()

                def evd(scalar, ph=ph, ds=ds, iq=iq, pe_t=tr_pe[ph]):
                    scalar.wait_ge(s_pe.sem, pe_t)
                    bA, bB = ((p4, p5), (p6, p7))[ph % 2]
                    for g in range(2):
                        oc0 = iq * 8 + g * 4
                        base = A2 if oc0 < 16 else A1
                        b0 = (oc0 if oc0 < 16 else oc0 - 16) * 1024
                        scalar.activation(
                            out=base[:, b0:b0 + 4096].rearrange(
                                "p (s e) -> p s e", s=4)[:, :, ds * 128:(ds + 1) * 128],
                            in_=(bA, bB)[g][:, 0:512].rearrange(
                                "p (s e) -> p s e", s=4),
                            func=AF.Identity, scale=1.0)
                        scalar.drain()
                ev_se[ph] = se(evd)
            carry_tr = {0: tr_pe[30], 1: tr_pe[31]}
            carry_ev = {0: ev_se[30], 1: ev_se[31]}

            wd_toks = []
            for oc in range(32):
                def mk_st(scalar, oc=oc):
                    base = A2 if oc < 16 else A1
                    b0 = (oc if oc < 16 else oc - 16) * 1024
                    return scalar.dma_start(out=wdT_hbm[oc, :, :],
                                            in_=base[:, b0:b0 + 1024])
                wrap2, tok2 = stream_dma(mk_st, st_stream)
                wd_toks.append(tok2)

                def std(scalar, wrap2=wrap2):
                    wrap2(scalar)
                em("scalar", std)
            wd_last = [wd_toks[30], wd_toks[31]]

            wgwu_last = None
            prev_qv = None
            for wi, (whbm, thbm) in enumerate(((wg_h, wgT_hbm), (wu_h, wuT_hbm))):
                emit_scale(wi)
                wraps, toks = [], []
                for pc in range(32):
                    def mk_ld(gpsimd, whbm=whbm, pc=pc):
                        return gpsimd.dma_start(
                            out=FB[:, 4096 + (pc % 2) * 1024: 4096 + (pc % 2) * 1024 + 1024],
                            in_=whbm[pc * 128:(pc + 1) * 128, :])
                    wrap, tok = stream_dma(mk_ld, gd_stream)
                    wraps.append(wrap)
                    toks.append(tok)

                def kickg(gpsimd, w0=wraps[0], w1=wraps[1], prev_qv=prev_qv):
                    if prev_qv is not None:
                        gpsimd.wait_ge(s_ve.sem, prev_qv)
                    w0(gpsimd)
                    w1(gpsimd)
                em("gpsimd", kickg)

                tr_pe, ev_se, st_toks = {-2: carry_tr[0], -1: carry_tr[1]}, \
                    {-2: carry_ev[0], -1: carry_ev[1]}, {}
                for pc in range(32):
                    def qz(vector, pc=pc, tok=toks[pc], tprev=tr_pe.get(pc - 2)):
                        swait(vector, gd_stream, tok)
                        vector.wait_ge(s_pe.sem, tprev)
                        quant3(vector,
                               FB[:, 4096 + (pc % 2) * 1024: 4096 + (pc % 2) * 1024 + 1024],
                               FB[:, 6144 + (pc % 2) * 1024: 6144 + (pc % 2) * 1024 + 1024])
                    qv = ve(qz)
                    last_qv = qv
                    if pc + 2 < 32:
                        def ldn(gpsimd, wrap=wraps[pc + 2], qv=qv):
                            gpsimd.wait_ge(s_ve.sem, qv)
                            wrap(gpsimd)
                        em("gpsimd", ldn)

                    def trq(tensor, pc=pc, qv=qv, eprev=ev_se.get(pc - 2)):
                        tensor.wait_ge(s_ve.sem, qv)
                        tensor.wait_ge(s_se.sem, eprev)
                        bA, bB = ((p4, p5), (p6, p7))[pc % 2]
                        for j in range(8):
                            bk = bA if j < 4 else bB
                            mm = tensor.transpose(
                                bk[:, (j % 4) * 128:(j % 4) * 128 + 128],
                                FB[:, 6144 + (pc % 2) * 1024 + j * 128:
                                   6144 + (pc % 2) * 1024 + j * 128 + 128],
                                identf[:])
                            if j == 7:
                                mm.then_inc(s_pe.sem, 1)
                    em("tensor", trq)
                    tr_pe[pc] = s_pe.inc()

                    def evq(scalar, pc=pc, pe_t=tr_pe[pc],
                            sprev=st_toks.get(pc - 4)):
                        scalar.wait_ge(s_pe.sem, pe_t)
                        if sprev is not None:
                            swait(scalar, st_stream, sprev)
                        bA, bB = ((p4, p5), (p6, p7))[pc % 2]
                        slot = 8192 + (pc % 4) * 1024
                        scalar.activation(out=A3[:, slot:slot + 512],
                                          in_=bA[:, 0:512],
                                          func=AF.Identity, scale=1.0)
                        scalar.drain()
                        scalar.activation(out=A3[:, slot + 512:slot + 1024],
                                          in_=bB[:, 0:512],
                                          func=AF.Identity, scale=1.0)
                    ev_se[pc] = se(evq)

                    def mk_st(scalar, thbm=thbm, pc=pc):
                        slot = 8192 + (pc % 4) * 1024
                        return scalar.dma_start(out=thbm[pc, :, :],
                                                in_=A3[:, slot:slot + 1024])
                    wrap2, tok2 = stream_dma(mk_st, st_stream)
                    st_toks[pc] = tok2

                    def stq(scalar, wrap2=wrap2):
                        wrap2(scalar)
                    em("scalar", stq)
                carry_tr = {0: tr_pe[30], 1: tr_pe[31]}
                carry_ev = {0: ev_se[30], 1: ev_se[31]}
                wgwu_last = [st_toks[30], st_toks[31]]
                prev_qv = last_qv
            psum_free_se = s_se.v
            a3_free_pe = carry_tr[1]

            ck_prev = [None, None]
            for sl in range(8):
                def mk_ld(gpsimd, sl=sl):
                    return gpsimd.dma_start(
                        out=A3[:, (sl % 2) * 3072:(sl % 2) * 3072 + 3072],
                        in_=wkqv_h[sl * 128:(sl + 1) * 128, :])
                wrap, tok = stream_dma(mk_ld, gd_stream)

                def ldk(gpsimd, wrap=wrap, sl=sl, pv=ck_prev[sl % 2],
                        a3_free_pe=a3_free_pe):
                    if sl < 2:
                        gpsimd.wait_ge(s_pe.sem, a3_free_pe)
                    if pv is not None:
                        swait(gpsimd, st_stream, pv)
                    wrap(gpsimd)
                em("gpsimd", ldk)

                def mk_st(scalar, sl=sl):
                    return scalar.dma_start(out=kqvb_hbm[sl * 128:(sl + 1) * 128, :],
                                            in_=A3[:, (sl % 2) * 3072:(sl % 2) * 3072 + 3072])
                wrap2, tok2 = stream_dma(mk_st, st_stream)

                def stk(scalar, wrap2=wrap2, tok=tok):
                    swait(scalar, gd_stream, tok)
                    wrap2(scalar)
                em("scalar", stk)
                ck_prev[sl % 2] = tok2
            kqvb_tok = tok2
            return means_done, kqvb_tok, wd_last, wgwu_last, psum_free_se

        def ffn_block(src_resid, dst_resid, means_done, entry_toks,
                      entry=None, wgwu_toks=(), wd_toks=(), psum_free_se=None):
            if entry is None:
                entry = (s_ve.v, s_se.v)
            if psum_free_se is None:
                psum_free_se = s_se.v
            srow_last = None
            for th in range(2):
                _, vq = ln_half2(src_resid, th,
                                 lambda tc8, th=th: A0[:, (th * 8 + tc8) * 1024:(th * 8 + tc8) * 1024 + 1024],
                                 quant=True, entry=entry, entry_toks=entry_toks)

                def strow(tensor, vq=vq, psum_free_se=psum_free_se):
                    tensor.wait_ge(s_ve.sem, vq)
                    tensor.wait_ge(s_se.sem, psum_free_se)
                    tensor.transpose(p7[0:8, 0:128], stat[:, 80:88], identf[:]
                                     ).then_inc(s_pe.sem, 1)
                em("tensor", strow)
                pe1 = s_pe.inc()

                def srow_ev(vector, pe1=pe1):
                    vector.wait_ge(s_pe.sem, pe1)
                    vector.tensor_copy(out=rtmp[0:8, 0:128], in_=p7[0:8, 0:128])
                srow_cp = ve(srow_ev)

                def mk_srd1(sync):
                    return sync.dma_start(out=row_hbm[0:8, :], in_=rtmp[0:8, 0:128])
                srw1, srt1 = stream_dma(mk_srd1, ld_stream)

                def mk_srdma(sync, th=th):
                    return sync.dma_start(
                        out=ROWF[0:1, th * 1024: th * 1024 + 1024],
                        in_=row_hbm[0:8, :])
                srwrap, srtok = stream_dma(mk_srdma, ld_stream)

                def srdma(sync, srw1=srw1, srwrap=srwrap, srow_cp=srow_cp, srt1=srt1):
                    sync.wait_ge(s_ve.sem, srow_cp)
                    srw1(sync)
                    swait(sync, ld_stream, srt1)
                    srwrap(sync)
                em("sync", srdma)
                srow_last = srtok
            xq_ve = s_ve.v

            def tr_gate(sync, v=xq_ve, wd_toks=list(wd_toks)):
                sync.wait_ge(s_ve.sem, v)
                for t_ in wd_toks:
                    swait(sync, st_stream, t_)
            xqT_done = transpose16(lambda tc: A0[:, tc * 1024:(tc + 1) * 1024], A1,
                                   gates=[tr_gate])

            groups = []
            for tt in range(4):
                pieces = [(onesff[0:1, 0:128], ROWF[0:1, tt * 512:(tt + 1) * 512])]
                groups.append((pieces, copy_evac(FB[:, tt * 512:(tt + 1) * 512])))
            bcast0_done = mm_phase(groups, [p4, p5],
                                   pe_waits=[(ld_stream.sems[srow_last[0]].sem, srow_last[1]),
                                             (s_se.sem, psum_free_se)])

            def macc0(vector, bcast0_done=bcast0_done, wgwu_toks=list(wgwu_toks)):
                vector.wait_ge(s_ve.sem, bcast0_done)
                for t_ in wgwu_toks:
                    swait(vector, st_stream, t_)
                vector.memset(FB[:, 4096:6144], 0.0)
            macc_z = ve(macc0)

            peA = s_pe.v
            guev_last = {}
            gst_toks = {}
            for oc in range(32):
                bsl = (oc % 2) * 1024
                prev_pe = peA + (oc - 1) * 8

                def mk_ldg(sync, oc=oc, bsl=bsl):
                    return sync.dma_start(out=WKB[:, bsl:bsl + 1024],
                                          in_=wgT_hbm[oc, :, :])
                wrapg, tokg = stream_dma(mk_ldg, ld_stream)

                def mk_ldu(sync, oc=oc, bsl=bsl):
                    return sync.dma_start(out=A3[:, 14336 + bsl:14336 + bsl + 1024],
                                          in_=wuT_hbm[oc, :, :])
                wrapu, toku = stream_dma(mk_ldu, ld_stream)

                def ldw(sync, wrapg=wrapg, wrapu=wrapu, oc=oc, prev_pe=prev_pe,
                        xqT_done=xqT_done, macc_z=macc_z, wgwu_toks=list(wgwu_toks)):
                    if oc == 0:
                        sync.wait_ge(s_tr.sem, xqT_done)
                        sync.wait_ge(s_ve.sem, macc_z)
                        for t in wgwu_toks:
                            swait(sync, st_stream, t)
                    if oc >= 2:
                        sync.wait_ge(s_pe.sem, prev_pe)
                    wrapg(sync)
                    wrapu(sync)
                em("sync", ldw)

                def gumm(tensor, oc=oc, bsl=bsl, tokg=tokg, toku=toku, peA=peA):
                    swait(tensor, ld_stream, tokg)
                    swait(tensor, ld_stream, toku)
                    for tt in range(4):
                        gidx = oc * 4 + tt
                        if gidx >= 2:
                            tensor.wait_ge(s_ve.sem, guev_last[gidx - 2])
                        pg = (p0, p2)[tt % 2]
                        pu = (p1, p3)[tt % 2]
                        for c in range(8):
                            mm = tensor.matmul(pg[:, 0:512],
                                               WKB[:, bsl + c * 128: bsl + c * 128 + 128],
                                               A1[:, c * 2048 + tt * 512: c * 2048 + tt * 512 + 512],
                                               start=(c == 0), stop=(c == 7))
                            if c == 7:
                                mm.then_inc(s_pe.sem, 1)
                        for c in range(8):
                            mm = tensor.matmul(pu[:, 0:512],
                                               A3[:, 14336 + bsl + c * 128: 14336 + bsl + c * 128 + 128],
                                               A1[:, c * 2048 + tt * 512: c * 2048 + tt * 512 + 512],
                                               start=(c == 0), stop=(c == 7))
                            if c == 7:
                                mm.then_inc(s_pe.sem, 1)
                em("tensor", gumm)
                s_pe.v += 8

                for tt in range(4):
                    gidx = oc * 4 + tt
                    pe_u = peA + oc * 8 + tt * 2 + 2

                    def guev1(vector, tt=tt, pe_u=pe_u):
                        pg = (p0, p2)[tt % 2]
                        vector.wait_ge(s_pe.sem, pe_u - 1)
                        vector.tensor_tensor(
                            out=FB[:, 6144 + (tt % 2) * 512: 6144 + (tt % 2) * 512 + 512],
                            in0=pg[:, 0:512],
                            in1=FB[:, tt * 512:(tt + 1) * 512], op=ALU.mult)
                    v1 = ve(guev1)

                    def gusil(scalar, tt=tt, v1=v1):
                        scalar.wait_ge(s_ve.sem, v1)
                        scalar.activation(
                            out=FB[:, 7168 + (tt % 2) * 512: 7168 + (tt % 2) * 512 + 512],
                            in_=FB[:, 6144 + (tt % 2) * 512: 6144 + (tt % 2) * 512 + 512],
                            func=AF.Silu, scale=stat[:, 17:18])
                    sl1 = se(gusil)

                    gprev = gst_toks.get(gidx - 2)

                    def guev2(vector, tt=tt, sl1=sl1, pe_u=pe_u, gprev=gprev):
                        pu = (p1, p3)[tt % 2]
                        vector.wait_ge(s_se.sem, sl1)
                        vector.wait_ge(s_pe.sem, pe_u)
                        if gprev is not None:
                            swait(vector, st_stream, gprev)
                        vector.tensor_tensor(
                            out=FB[:, 6144 + (tt % 2) * 512: 6144 + (tt % 2) * 512 + 512],
                            in0=pu[:, 0:512],
                            in1=FB[:, tt * 512:(tt + 1) * 512], op=ALU.mult)
                        vector.drain()
                        vector.tensor_tensor(
                            out=FB[:, 2048 + (tt % 2) * 512: 2048 + (tt % 2) * 512 + 512],
                            in0=FB[:, 7168 + (tt % 2) * 512: 7168 + (tt % 2) * 512 + 512],
                            in1=FB[:, 6144 + (tt % 2) * 512: 6144 + (tt % 2) * 512 + 512],
                            op=ALU.mult)
                    v2a = ve(guev2)

                    def gusq(scalar, tt=tt, v2a=v2a):
                        scalar.wait_ge(s_ve.sem, v2a)
                        scalar.activation(
                            out=FB[:, 7168 + (tt % 2) * 512: 7168 + (tt % 2) * 512 + 512],
                            in_=FB[:, 2048 + (tt % 2) * 512: 2048 + (tt % 2) * 512 + 512],
                            func=AF.Square)
                    sq1 = se(gusq)

                    def guevm(vector, tt=tt, sq1=sq1):
                        vector.wait_ge(s_se.sem, sq1)
                        vector.tensor_tensor(
                            out=FB[:, 4096 + tt * 512: 4096 + tt * 512 + 512],
                            in0=FB[:, 4096 + tt * 512: 4096 + tt * 512 + 512],
                            in1=FB[:, 7168 + (tt % 2) * 512: 7168 + (tt % 2) * 512 + 512],
                            op=ALU.max)
                    v2 = ve(guevm)
                    guev_last[gidx] = v2

                    def mk_stg(scalar, oc=oc, tt=tt):
                        return scalar.dma_start(
                            out=gu_hbm[oc, :, tt * 512:(tt + 1) * 512],
                            in_=FB[:, 2048 + (tt % 2) * 512: 2048 + (tt % 2) * 512 + 512])
                    wraps, toks = stream_dma(mk_stg, st_stream)
                    gst_toks[gidx] = toks

                    def gust(scalar, wraps=wraps, v2=v2):
                        scalar.wait_ge(s_ve.sem, v2)
                        wraps(scalar)
                    em("scalar", gust)
            phaseA_ve = s_ve.v
            gu_last_toks = [gst_toks[126], gst_toks[127]]

            peT = s_pe.v
            s2c_base = s_ve.v

            def maccT(tensor, phaseA_ve=phaseA_ve, s2c_base=s2c_base):
                tensor.wait_ge(s_ve.sem, phaseA_ve)
                for j in range(16):
                    if j >= 2:
                        tensor.wait_ge(s_ve.sem, s2c_base + j - 1)
                    tensor.transpose((p4, p5)[j % 2][:, 0:128],
                                     FB[:, 4096 + j * 128: 4096 + j * 128 + 128],
                                     identf[:]).then_inc(s_pe.sem, 1)
            em("tensor", maccT)
            s_pe.v += 16

            def s2cols(vector, peT=peT):
                for j in range(16):
                    vector.wait_ge(s_pe.sem, peT + j + 1)
                    vector.reduce_max(out=stat[:, 64 + j:65 + j],
                                      in_=(p4, p5)[j % 2][:, 0:128], axis=AX.X)
                    vector.drain()
                    vector.nop().then_inc(s_ve.sem, 1)
            em("vector", s2cols)
            s_ve.v += 16
            s2c_done = s_ve.v

            def s2sqrt(scalar, s2c_done=s2c_done):
                scalar.wait_ge(s_ve.sem, s2c_done)
                scalar.activation(out=stat[:, 64:80], in_=stat[:, 64:80], func=AF.Sqrt)
            s2sq = se(s2sqrt)

            def s2fin(vector, s2sq=s2sq):
                vector.wait_ge(s_se.sem, s2sq)
                vector.tensor_scalar(out=stat[:, 64:80], in0=stat[:, 64:80],
                                     scalar1=1e-5, scalar2=None, op0=ALU.max)
                vector.drain()
                vector.tensor_scalar(out=stat[:, 32:48], in0=stat[:, 64:80],
                                     scalar1=stat[:, 18:19], scalar2=None, op0=ALU.mult)
                vector.drain()
                vector.tensor_scalar(out=stat[:, 32:48], in0=stat[:, 32:48],
                                     scalar1=stat[:, 19:20], scalar2=1.0 / 127.0,
                                     op0=ALU.mult, op1=ALU.mult)
                vector.drain()
                vector.reciprocal(out=stat[:, 64:80], in_=stat[:, 64:80])
                vector.drain()
                vector.tensor_scalar(out=stat[:, 64:80], in0=stat[:, 64:80],
                                     scalar1=127.0, scalar2=None, op0=ALU.mult)
            s2_done = ve(s2fin)

            def s2row_t(tensor, s2_done=s2_done):
                tensor.wait_ge(s_ve.sem, s2_done)
                tensor.transpose(p7[0:16, 0:128], stat[:, 64:80], identf[:]
                                 ).then_inc(s_pe.sem, 1)
            em("tensor", s2row_t)
            peR = s_pe.inc()

            def s2row_ev(vector, peR=peR):
                vector.wait_ge(s_pe.sem, peR)
                vector.tensor_copy(out=rtmp[0:16, 0:128], in_=p7[0:16, 0:128])
            s2row_cp = ve(s2row_ev)

            def mk_s2d1(sync):
                return sync.dma_start(out=row_hbm[:, :], in_=rtmp[0:16, 0:128])
            s2w1, s2t1 = stream_dma(mk_s2d1, ld_stream)

            def mk_s2dma(sync):
                return sync.dma_start(
                    out=ROWF[0:1, 0:2048],
                    in_=row_hbm[:, :])
            s2wrap, s2tok = stream_dma(mk_s2dma, ld_stream)

            def s2dma(sync, s2w1=s2w1, s2wrap=s2wrap, s2row_cp=s2row_cp, s2t1=s2t1):
                sync.wait_ge(s_ve.sem, s2row_cp)
                s2w1(sync)
                swait(sync, ld_stream, s2t1)
                s2wrap(sync)
            em("sync", s2dma)

            groups = []
            for tt in range(4):
                pieces = [(onesff[0:1, 0:128], ROWF[0:1, tt * 512:(tt + 1) * 512])]
                groups.append((pieces, copy_evac(FB[:, tt * 512:(tt + 1) * 512])))
            bcast2_done = mm_phase(groups, [p4, p5],
                                   pe_waits=[(ld_stream.sems[s2tok[0]].sem, s2tok[1])])

            res_toks = []
            dev_last_tt = None
            for tt in range(4):
                veB0 = s_ve.v
                qg_prev = [None, None]
                for oc in range(32):
                    def mk_ldgu(sync, oc=oc, tt=tt):
                        return sync.dma_start(
                            out=FB[:, 2048 + (oc % 2) * 512: 2048 + (oc % 2) * 512 + 512],
                            in_=gu_hbm[oc, :, tt * 512:(tt + 1) * 512])
                    wrap, tok = stream_dma(mk_ldgu, ld_stream)

                    def ldgu(sync, wrap=wrap, oc=oc, tt=tt, pv=qg_prev[oc % 2],
                             bcast2_done=bcast2_done, gu_last_toks=gu_last_toks):
                        if oc == 0 and tt == 0:
                            for t_ in gu_last_toks:
                                swait(sync, st_stream, t_)
                            sync.wait_ge(s_ve.sem, bcast2_done)
                        if pv is not None:
                            sync.wait_ge(s_ve.sem, pv)
                        wrap(sync)
                    em("sync", ldgu)

                    def qgu(vector, oc=oc, tt=tt, tok=tok):
                        swait(vector, ld_stream, tok)
                        src = FB[:, 2048 + (oc % 2) * 512: 2048 + (oc % 2) * 512 + 512]
                        vector.tensor_tensor(out=FB[:, 3072:3584], in0=src,
                                             in1=FB[:, tt * 512:(tt + 1) * 512],
                                             op=ALU.mult)
                        vector.drain()
                        vector.tensor_scalar(out=qi32[:, 0:512], in0=FB[:, 3072:3584],
                                             scalar1=128.5, scalar2=None, op0=ALU.add)
                        vector.drain()
                        vector.tensor_scalar(out=A0[:, oc * 512:(oc + 1) * 512],
                                             in0=qi32[:, 0:512],
                                             scalar1=-128.0, scalar2=None, op0=ALU.add)
                    qv = ve(qgu)
                    qg_prev[oc % 2] = qv
                q_done = s_ve.v

                peD = s_pe.v
                dtoks = {}
                for oc in range(32):
                    def mk_ldd(sync, oc=oc):
                        return sync.dma_start(
                            out=WKB[:, (oc % 2) * 1024:(oc % 2) * 1024 + 1024],
                            in_=wdT_hbm[oc, :, :])
                    wrap, tok = stream_dma(mk_ldd, ld_stream)
                    dtoks[oc] = tok

                    def ldd(sync, wrap=wrap, oc=oc, q_done=q_done, peD=peD):
                        if oc == 0:
                            sync.wait_ge(s_ve.sem, q_done)
                        if oc >= 2:
                            sync.wait_ge(s_pe.sem, peD + (oc - 1))
                        wrap(sync)
                    em("sync", ldd)

                    def dmm(tensor, oc=oc, tok=tok, peD=peD, dprev=dev_last_tt):
                        swait(tensor, ld_stream, tok)
                        if oc == 0 and dprev is not None:
                            tensor.wait_ge(s_ve.sem, dprev)
                        for tc2 in range(4):
                            for hh in range(2):
                                pp = PSUMS[tc2 * 2 + hh]
                                mm = tensor.matmul(
                                    pp[:, 0:512],
                                    A0[:, oc * 512 + tc2 * 128: oc * 512 + tc2 * 128 + 128],
                                    WKB[:, (oc % 2) * 1024 + hh * 512:(oc % 2) * 1024 + hh * 512 + 512],
                                    start=(oc == 0), stop=(oc == 31))
                                if tc2 == 3 and hh == 1:
                                    mm.then_inc(s_pe.sem, 1)
                    em("tensor", dmm)
                    s_pe.inc()
                down_pe = s_pe.v

                for tc2 in range(4):
                    tc = tt * 4 + tc2

                    def mk_ldr(sync, tc=tc, tc2=tc2):
                        return sync.dma_start(
                            out=FB[:, 4096 + (tc2 % 2) * 1024: 4096 + (tc2 % 2) * 1024 + 1024],
                            in_=src_resid[tc * 128:(tc + 1) * 128, :])
                    wrap, tok = stream_dma(mk_ldr, ld_stream)
                    prev_res_tok = res_toks[-2][1] if len(res_toks) >= 2 else None

                    def ldr(sync, wrap=wrap, down_pe=down_pe, prev=prev_res_tok):
                        sync.wait_ge(s_pe.sem, down_pe)
                        if prev is not None:
                            swait(sync, st_stream, prev)
                        wrap(sync)
                    em("sync", ldr)

                    def dev(vector, tc2=tc2, tt=tt, tok=tok, down_pe=down_pe):
                        swait(vector, ld_stream, tok)
                        vector.wait_ge(s_pe.sem, down_pe)
                        for hh in range(2):
                            pp = PSUMS[tc2 * 2 + hh]
                            sl = slice(4096 + (tc2 % 2) * 1024 + hh * 512,
                                       4096 + (tc2 % 2) * 1024 + hh * 512 + 512)
                            vector.scalar_tensor_tensor(
                                out=FB[:, sl], in0=pp[:, 0:512],
                                scalar=stat[:, 32 + tt * 4 + tc2:33 + tt * 4 + tc2],
                                in1=FB[:, sl], op0=ALU.mult, op1=ALU.add)
                            vector.drain()
                        vector.nop().then_inc(s_ve.sem, 1)
                    em("vector", dev)
                    dv = s_ve.inc()

                    def mk_str(scalar, tc=tc, tc2=tc2):
                        return scalar.dma_start(
                            out=dst_resid[tc * 128:(tc + 1) * 128, :],
                            in_=FB[:, 4096 + (tc2 % 2) * 1024: 4096 + (tc2 % 2) * 1024 + 1024])
                    wrap2, tok2 = stream_dma(mk_str, st_stream)
                    res_toks.append((tc, tok2))

                    def strx(scalar, wrap2=wrap2, dv=dv):
                        scalar.wait_ge(s_ve.sem, dv)
                        wrap2(scalar)
                    em("scalar", strx)
                dev_last_tt = s_ve.v
            return res_toks

        def block3(kqvb_tok, b2_res_toks):
            phase_gate([t for _, t in b2_res_toks])
            def mk_liq(sync):
                return sync.dma_start(out=FB[0:1, 0:4096], in_=iotaq_h[:, :])
            wrap, liq_tok = stream_dma(mk_liq, ld_stream)

            def ldiq(sync, wrap=wrap):
                wrap(sync)
            em("sync", ldiq)

            def mvals(vector, liq_tok=liq_tok):
                swait(vector, ld_stream, liq_tok)
                for hh in range(2):
                    vector.tensor_copy(out=stat[0:1, 20 + hh:21 + hh],
                                       in_=mrow[0:1, 24 + hh:25 + hh])
                    vector.drain()
                    vector.tensor_scalar(out=stat[0:1, 22 + hh:23 + hh],
                                         in0=stat[0:1, 20 + hh:21 + hh],
                                         scalar1=0.0, scalar2=-32.0,
                                         op0=ALU.max, op1=ALU.mult)
                    vector.drain()
                    vector.tensor_scalar(out=ROWB[0:1, hh * 4096:(hh + 1) * 4096],
                                         in0=FB[0:1, 0:4096],
                                         scalar1=stat[0:1, 22 + hh:23 + hh],
                                         scalar2=None, op0=ALU.mult)
                    vector.drain()
            mv_done = ve(mvals)

            mjc_done = mv_done
            for hh in range(2):
                def mbc(tensor, hh=hh, mv_done=mv_done):
                    tensor.wait_ge(s_ve.sem, mv_done)
                    tensor.matmul(p1[:, hh:hh + 1], onesff[0:1, 0:128],
                                  stat[0:1, 20 + hh:21 + hh], start=True, stop=True
                                  ).then_inc(s_pe.sem, 1)
                em("tensor", mbc)
                pe_b = s_pe.inc()

                def mje(vector, hh=hh, pe_b=pe_b):
                    vector.wait_ge(s_pe.sem, pe_b)
                    vector.tensor_copy(out=stat[:, 24 + hh:25 + hh], in_=p1[:, hh:hh + 1])
                    vector.drain()
                    vector.tensor_scalar(out=mjc[:, hh * 32:(hh + 1) * 32], in0=iotak[:],
                                         scalar1=stat[:, 24 + hh:25 + hh],
                                         scalar2=None, op0=ALU.mult)
                mjc_done = ve(mje)

            for th in range(2):
                lnse, _ = ln_half2(
                    resid_h, th,
                    lambda tc8, th=th: A0[:, (th * 8 + tc8) * 1024:(th * 8 + tc8) * 1024 + 1024])
            ln3_se = s_se.v
            ln3T_done = transpose16(lambda tc: A0[:, tc * 1024:(tc + 1) * 1024], A3,
                                    gates=[lambda sync, v=ln3_se: sync.wait_ge(s_se.sem, v)])

            peALL = None

            def kq_part2(colbase, dstbuf, first, part_off, mjc_done=mjc_done):
                nonlocal peALL
                if peALL is None:
                    peALL = s_pe.v
                peK0 = peALL - part_off
                for tt in range(4):
                    peK = s_pe.v
                    for c in range(8):
                        def mk_ldc(sync, c=c, colbase=colbase):
                            return sync.dma_start(
                                out=WKB[:, (c % 2) * 1024:(c % 2) * 1024 + 1024],
                                in_=kqvb_hbm[c * 128:(c + 1) * 128, colbase:colbase + 1024])
                        wrap, tok = stream_dma(mk_ldc, ld_stream)

                        def ldc(sync, wrap=wrap, c=c, tt=tt, first=first,
                                peALLv=peALL, part_off=part_off,
                                ln3T_done=ln3T_done, kqvb_tok=kqvb_tok):
                            if c == 0 and tt == 0 and first:
                                sync.wait_ge(s_tr.sem, ln3T_done)
                                swait(sync, st_stream, kqvb_tok)
                            gi = part_off + tt * 8 + c
                            if gi >= 2:
                                sync.wait_ge(s_pe.sem, peALLv + gi - 1)
                            wrap(sync)
                        em("sync", ldc)

                        def kmm(tensor, c=c, tt=tt, tok=tok, first=first,
                                mjc_done=mjc_done):
                            if c == 0 and tt == 0 and first:
                                tensor.wait_ge(s_ve.sem, mjc_done)
                            swait(tensor, ld_stream, tok)
                            for slot in range(8):
                                pp = PSUMS[slot]
                                mm = tensor.matmul(
                                    pp[:, 0:512],
                                    WKB[:, (c % 2) * 1024 + slot * 128:(c % 2) * 1024 + slot * 128 + 128],
                                    A3[:, c * 2048 + tt * 512: c * 2048 + tt * 512 + 512],
                                    start=(c == 0), stop=(c == 7))
                                if slot == 7:
                                    mm.then_inc(s_pe.sem, 1)
                        em("tensor", kmm)
                        s_pe.inc()

                    def kev(vector, tt=tt, dstbuf=dstbuf, peK=peK):
                        vector.wait_ge(s_pe.sem, peK + 8)
                        for slot in range(8):
                            pp = PSUMS[slot]
                            vector.tensor_copy(
                                out=dstbuf[:, slot * 2048 + tt * 512: slot * 2048 + tt * 512 + 512],
                                in_=pp[:, 0:512])
                        vector.drain()
                        vector.nop().then_inc(s_ve.sem, 1)
                    em("vector", kev)
                    s_ve.inc()
                    kev_v = s_ve.v

                    def tt_guard(tensor, kev_v=kev_v):
                        tensor.wait_ge(s_ve.sem, kev_v)
                    em("tensor", tt_guard)

            kq_part2(0, A0, True, 0)
            kq_part2(D, A1, False, 32)
            for tcg in range(8):
                peV = s_pe.v
                for c in range(8):
                    def mk_ldc(sync, c=c):
                        return sync.dma_start(
                            out=WKB[:, (c % 2) * 1024:(c % 2) * 1024 + 1024],
                            in_=kqvb_hbm[c * 128:(c + 1) * 128, 2 * D:2 * D + 1024])
                    wrap, tok = stream_dma(mk_ldc, ld_stream)

                    def ldc(sync, wrap=wrap, c=c, tcg=tcg, peALLv=peALL):
                        gi = 64 + tcg * 8 + c
                        if gi >= 2:
                            sync.wait_ge(s_pe.sem, peALLv + gi - 1)
                        wrap(sync)
                    em("sync", ldc)

                    def vmm(tensor, c=c, tcg=tcg, tok=tok):
                        swait(tensor, ld_stream, tok)
                        for tc2 in range(2):
                            for vh in range(2):
                                pp = PSUMS[tc2 * 2 + vh]
                                tc = tcg * 2 + tc2
                                mm = tensor.matmul(
                                    pp[:, 0:512],
                                    A3[:, c * 2048 + tc * 128: c * 2048 + tc * 128 + 128],
                                    WKB[:, (c % 2) * 1024 + vh * 512:(c % 2) * 1024 + vh * 512 + 512],
                                    start=(c == 0), stop=(c == 7))
                                if tc2 == 1 and vh == 1:
                                    mm.then_inc(s_pe.sem, 1)
                    em("tensor", vmm)
                    s_pe.inc()

                def vev(vector, tcg=tcg, peV=peV):
                    vector.wait_ge(s_pe.sem, peV + 8)
                    for tc2 in range(2):
                        for vh in range(2):
                            pp = PSUMS[tc2 * 2 + vh]
                            tc = tcg * 2 + tc2
                            for sl4 in range(4):
                                slot = vh * 4 + sl4
                                vector.tensor_copy(
                                    out=A2[:, slot * 2048 + tc * 128: slot * 2048 + tc * 128 + 128],
                                    in_=pp[:, sl4 * 128:(sl4 + 1) * 128])
                    vector.drain()
                    vector.nop().then_inc(s_ve.sem, 1)
                em("vector", vev)
                s_ve.inc()
                vev_v = s_ve.v

                def v_guard(tensor, vev_v=vev_v):
                    tensor.wait_ge(s_ve.sem, vev_v)
                em("tensor", v_guard)
            kqv_ve = s_ve.v

            gd0 = s_gd.v
            cc_e2 = s_cc.v + 1

            def e2(gpsimd, kqv_ve=kqv_ve, gd0=gd0, cc_e2=cc_e2):
                gpsimd.wait_ge(s_ve.sem, kqv_ve)
                for slot in range(8):
                    gpsimd.dma_start(out=a2a_in[slot, :, 0:2048],
                                     in_=A0[:, slot * 2048:(slot + 1) * 2048]).then_inc(s_gd.sem, 16)
                    gpsimd.dma_start(out=a2a_in[slot, :, 2048:4096],
                                     in_=A1[:, slot * 2048:(slot + 1) * 2048]).then_inc(s_gd.sem, 16)
                    gpsimd.dma_start(out=a2a_in[slot, :, 4096:6144],
                                     in_=A2[:, slot * 2048:(slot + 1) * 2048]).then_inc(s_gd.sem, 16)
                gpsimd.wait_ge(s_gd.sem, gd0 + 24 * 16)
                with allow_ici_all_to_all():
                    gpsimd.collective_compute(
                        "AllToAll", ALU.bypass,
                        replica_groups=[list(range(8))],
                        ins=[a2a_in.ap().opt()], outs=[a2a_out.ap().opt()],
                    ).then_inc(s_cc.sem, 1)
                gpsimd.wait_ge(s_cc.sem, cc_e2)
                for i in range(8):
                    gpsimd.dma_start(out=A1[:, i * 2048:(i + 1) * 2048],
                                     in_=a2a_out[i, :, 0:2048]).then_inc(s_gd.sem, 16)
                    gpsimd.dma_start(out=A0[:, i * 2048:(i + 1) * 2048],
                                     in_=a2a_out[i, :, 2048:4096]).then_inc(s_gd.sem, 16)
                    gpsimd.dma_start(
                        out=A2[:].rearrange("p (k w) -> p k w", w=130)[:, i * 16:(i + 1) * 16, 0:64],
                        in_=a2a_out[i, :, 4096:6144].rearrange("p (k v) -> p k v", v=128)[:, :, 0:64]
                    ).then_inc(s_gd.sem, 16)
                    gpsimd.dma_start(
                        out=A2[:].rearrange("p (k w) -> p k w", w=130)[:, i * 16:(i + 1) * 16, 65:129],
                        in_=a2a_out[i, :, 4096:6144].rearrange("p (k v) -> p k v", v=128)[:, :, 64:128]
                    ).then_inc(s_gd.sem, 16)
            em("gpsimd", e2)
            s_gd.v += 24 * 16
            s_cc.v += 1
            s_gd.v += 32 * 16
            e2_recv = s_gd.v

            def vones(vector, e2_recv=e2_recv, mjc_done=mjc_done):
                vector.wait_ge(s_gd.sem, e2_recv)
                vector.wait_ge(s_ve.sem, mjc_done)
                vector.memset(A2[:].rearrange("p (k w) -> p k w", w=130)[:, :, 64:65], 1.0)
                vector.memset(A2[:].rearrange("p (k w) -> p k w", w=130)[:, :, 129:130], 1.0)
            vpad_done = ve(vones)

            sched = []
            exp_of = {}
            av_of = {}
            fin_ve_of = {}
            pe_v = s_pe.v
            se_v = s_se.v
            ve_v = s_ve.v
            ktg = 0
            tile = 0
            for beta in range(4):
                for qi in range(8):
                    nkt = 4 * qi + 4
                    for kt in range(nkt):
                        ping = ktg % 2
                        sched.append(dict(op="S", beta=beta, qi=qi, kt=kt, ktg=ktg,
                                          ping=ping,
                                          first=(kt == 0), last=(kt == nkt - 1)))
                        pe_v += 2
                        sgrp = (pe_v - 1, pe_v)
                        sched[-1]["sgrp"] = sgrp
                        sched.append(dict(op="E", ktg=ktg, ping=ping, beta=beta,
                                          qi=qi, kt=kt, sgrp=sgrp))
                        se_v += 2
                        exp_of[ktg] = (se_v - 1, se_v)
                        sched.append(dict(op="A", ktg=ktg, ping=ping, beta=beta,
                                          qi=qi, kt=kt, first=(kt == 0),
                                          last=(kt == nkt - 1), tile=tile))
                        pe_v += 2
                        av_of[ktg] = (pe_v - 1, pe_v)
                        ktg += 1
                    sched.append(dict(op="F", beta=beta, qi=qi, tile=tile,
                                      av=av_of[ktg - 1]))
                    ve_v += 1
                    fin_ve_of[tile] = ve_v
                    tile += 1

            pe_b3 = s_pe.v
            se_b3 = s_se.v
            ve_b3 = s_ve.v

            f_toks = {}
            for step in [s for s in sched if s["op"] == "F"]:
                t = step["tile"]
            def b3_t(tensor, sched=sched, vpad_done=vpad_done):
                for step in sched:
                    if step["op"] == "S":
                        beta, qi, kt, ktg = step["beta"], step["qi"], step["kt"], step["ktg"]
                        ping = step["ping"]
                        pS = ((p0, p1), (p4, p5))[ping]
                        if ktg == 0:
                            tensor.wait_ge(s_ve.sem, vpad_done)
                        if ktg >= 2:
                            tensor.wait_ge(s_se.sem, exp_of[ktg - 2][1])
                        kcol = beta * 4096 + kt * 128
                        qcol = beta * 4096 + qi * 512
                        diag = kt - 4 * qi
                        for hh in range(2):
                            hb = hh * 64
                            tensor.matmul(pS[hh][:, 0:512],
                                          A1[hb:hb + 64, kcol:kcol + 128],
                                          A0[hb:hb + 64, qcol:qcol + 512],
                                          start=True, stop=False)
                            mm = tensor.matmul(
                                pS[hh][:, 0:512], onesbf[0:1, 0:128],
                                ROWB[0:1, hh * 4096 + qi * 512: hh * 4096 + qi * 512 + 512],
                                start=False, stop=(diag < 0))
                            if diag < 0:
                                mm.then_inc(s_pe.sem, 1)
                            else:
                                tensor.matmul(pS[hh][:, 0:512], identb[:, 0:128],
                                              masksb[:, diag * 512:(diag + 1) * 512],
                                              start=False, stop=True).then_inc(s_pe.sem, 1)
                    elif step["op"] == "A":
                        ktg, ping = step["ktg"], step["ping"]
                        beta, kt, tile = step["beta"], step["kt"], step["tile"]
                        kc = beta * 32 + kt
                        if step["first"] and tile >= 1:
                            tensor.wait_ge(s_ve.sem, fin_ve_of[tile - 1])
                        for hh in range(2):
                            tensor.wait_ge(s_se.sem, exp_of[ktg][hh])
                            pO = (p2, p3)[hh]
                            tensor.matmul(pO[0:65, 0:512],
                                          A2[:, kc * 130 + hh * 65: kc * 130 + hh * 65 + 65],
                                          ptb[:, (ping * 2 + hh) * 512:(ping * 2 + hh) * 512 + 512],
                                          start=step["first"], stop=step["last"]
                                          ).then_inc(s_pe.sem, 1)
            em("tensor", b3_t)
            s_pe.v = pe_b3 + sum(4 for s in sched if s["op"] == "S")

            fstage_toks = []
            f_sd_tokens = {}
            for step in sched:
                if step["op"] == "E":
                    def b3_e(scalar, step=step):
                        ktg, ping, kt = step["ktg"], step["ping"], step["kt"]
                        for hh in range(2):
                            scalar.wait_ge(s_pe.sem, step["sgrp"][hh])
                            if ktg >= 2:
                                scalar.wait_ge(s_pe.sem, av_of[ktg - 2][hh])
                            pS = ((p0, p1), (p4, p5))[ping]
                            scalar.activation(
                                out=ptb[:, (ping * 2 + hh) * 512:(ping * 2 + hh) * 512 + 512],
                                in_=pS[hh][:, 0:512], func=AF.Exp,
                                scale=1.0 / 32.0,
                                bias=mjc[:, hh * 32 + kt:hh * 32 + kt + 1])
                            scalar.drain()
                            scalar.nop().then_inc(s_se.sem, 1)
                    em("scalar", b3_e)
                    s_se.inc(2)
                elif step["op"] == "F":
                    t = step["tile"]

                    def fev(vector, step=step, t=t):
                        vector.wait_ge(s_pe.sem, step["av"][1])
                        if t >= 2:
                            for hh2 in range(2):
                                swait(vector, st_stream, f_sd_tokens[(t - 2, hh2, "o")])
                                swait(vector, ld_stream, f_sd_tokens[(t - 2, hh2, "n")])
                        for hh in range(2):
                            pO = (p2, p3)[hh]
                            vector.tensor_copy(
                                out=FB[0:64, (t % 2) * 1024 + hh * 512:(t % 2) * 1024 + hh * 512 + 512],
                                in_=pO[0:64, 0:512])
                            vector.tensor_copy(
                                out=FB[64:65, 2048 + ((t % 2) * 2 + hh) * 512:
                                       2048 + ((t % 2) * 2 + hh) * 512 + 512],
                                in_=pO[64:65, 0:512])
                        vector.drain()
                        vector.nop().then_inc(s_ve.sem, 1)
                    em("vector", fev)
                    fv = s_ve.inc()
                    assert fv == fin_ve_of[t], (fv, fin_ve_of[t], t)

                    for hh in range(2):
                        def mk_sto(scalar, t=t, hh=hh):
                            return scalar.dma_start(
                                out=o_scr[2 * t + hh, :, :],
                                in_=FB[0:64, (t % 2) * 1024 + hh * 512:(t % 2) * 1024 + hh * 512 + 512])
                        wrap, tok = stream_dma(mk_sto, st_stream)
                        f_sd_tokens[(t, hh, "o")] = tok

                        def sto(scalar, wrap=wrap, fv=fv):
                            scalar.wait_ge(s_ve.sem, fv)
                            wrap(scalar)
                        em("scalar", sto)

                        def mk_stn(sync, t=t, hh=hh):
                            return sync.dma_start(
                                out=nrows[2 * t + hh:2 * t + hh + 1, :],
                                in_=FB[64:65, 2048 + ((t % 2) * 2 + hh) * 512:
                                       2048 + ((t % 2) * 2 + hh) * 512 + 512])
                        wrapn, tokn = stream_dma(mk_stn, ld_stream)
                        f_sd_tokens[(t, hh, "n")] = tokn

                        def stn(sync, wrapn=wrapn, fv=fv):
                            sync.wait_ge(s_ve.sem, fv)
                            wrapn(sync)
                        em("sync", stn)
                    fstage_toks.append(fv)
            attn_done_ve = s_ve.v
            last_o_toks = [f_sd_tokens[(31, 0, "o")], f_sd_tokens[(31, 1, "o")]]
            last_n_toks = [f_sd_tokens[(31, 0, "n")], f_sd_tokens[(31, 1, "n")]]

            def nrecip(vector, attn_done_ve=attn_done_ve, last_n_toks=last_n_toks):
                vector.wait_ge(s_ve.sem, attn_done_ve)
                for t_ in last_n_toks:
                    swait(vector, ld_stream, t_)
                vector.reciprocal(out=nrows[:, :], in_=nrows[:, :])
            nrec_done = ve(nrecip)

            norm_ve = {}
            for t64 in range(64):
                def mk_ntmp(sync, t64=t64):
                    return sync.dma_start(out=ROWF[0:1, 2048:2560],
                                          in_=nrows[t64:t64 + 1, :])
                wrapn, tokn = stream_dma(mk_ntmp, ld_stream)

                def ldn(sync, wrapn=wrapn, t64=t64, nrec_done=nrec_done):
                    if t64 == 0:
                        sync.wait_ge(s_ve.sem, nrec_done)
                    else:
                        sync.wait_ge(s_pe.sem, nb_pe[t64 - 1])
                    wrapn(sync)
                em("sync", ldn)

                def mk_ldo(sync, t64=t64):
                    return sync.dma_start(
                        out=FB[0:64, (t64 % 2) * 512:(t64 % 2) * 512 + 512],
                        in_=o_scr[t64, :, :])
                wrapo, toko = stream_dma(mk_ldo, ld_stream)

                def ldo(sync, wrapo=wrapo, t64=t64, last_o_toks=last_o_toks):
                    if t64 == 0:
                        for t_ in last_o_toks:
                            swait(sync, st_stream, t_)
                    if t64 >= 2:
                        sync.wait_ge(s_ve.sem, norm_ve[t64 - 2])
                    wrapo(sync)
                em("sync", ldo)

                nprev = norm_ve.get(t64 - 1) if t64 >= 1 else None

                def nb(tensor, tokn=tokn, t64=t64, nprev=nprev):
                    swait(tensor, ld_stream, tokn)
                    if nprev is not None:
                        tensor.wait_ge(s_ve.sem, nprev)
                    tensor.matmul(p6[0:64, 0:512], onesff[0:1, 0:64],
                                  ROWF[0:1, 2048:2560], start=True, stop=True
                                  ).then_inc(s_pe.sem, 1)
                em("tensor", nb)
                nb_pe_t = s_pe.inc()
                if t64 == 0:
                    nb_pe = {}
                nb_pe[t64] = nb_pe_t

                def nmul(vector, t64=t64, toko=toko, nb_pe_t=nb_pe_t):
                    swait(vector, ld_stream, toko)
                    vector.wait_ge(s_pe.sem, nb_pe_t)
                    qtglob = t64 // 2
                    hh = t64 % 2
                    beta, qi = qtglob // 8, qtglob % 8
                    slot = 2 * beta + (1 if qi >= 4 else 0)
                    col = slot * 2048 + (qi % 4) * 512
                    vector.tensor_tensor(out=A3[hh * 64:hh * 64 + 64, col:col + 512],
                                         in0=FB[0:64, (t64 % 2) * 512:(t64 % 2) * 512 + 512],
                                         in1=p6[0:64, 0:512], op=ALU.mult)
                norm_ve[t64] = ve(nmul)
            e3_stage_ve = s_ve.v

            gd1 = s_gd.v
            cc_e3 = s_cc.v + 1

            def e3(gpsimd, e3_stage_ve=e3_stage_ve, gd1=gd1, cc_e3=cc_e3):
                gpsimd.wait_ge(s_ve.sem, e3_stage_ve)
                for slot in range(8):
                    gpsimd.dma_start(out=oa_in[slot, :, :],
                                     in_=A3[:, slot * 2048:(slot + 1) * 2048]).then_inc(s_gd.sem, 16)
                gpsimd.wait_ge(s_gd.sem, gd1 + 8 * 16)
                with allow_ici_all_to_all():
                    gpsimd.collective_compute(
                        "AllToAll", ALU.bypass,
                        replica_groups=[list(range(8))],
                        ins=[oa_in.ap().opt()], outs=[oa_out.ap().opt()],
                    ).then_inc(s_cc.sem, 1)
                gpsimd.wait_ge(s_cc.sem, cc_e3)
                for i in range(8):
                    gpsimd.dma_start(out=A0[:, i * 2048:(i + 1) * 2048],
                                     in_=oa_out[i, :, :]).then_inc(s_gd.sem, 16)
            em("gpsimd", e3)
            s_gd.v += 8 * 16
            s_cc.v += 1
            s_gd.v += 8 * 16
            e3_recv = s_gd.v

            def e3t_gate(sync, e3_recv=e3_recv):
                sync.wait_ge(s_gd.sem, e3_recv)
            em("sync", e3t_gate)
            for si in range(8):
                def trs(sync, si=si):
                    sync.dma_start_transpose(
                        A1[:].rearrange("p (tc s f) -> p tc s f", s=8, f=128)[:, :, si, :],
                        A0[:, si * 2048:(si + 1) * 2048],
                    ).then_inc(s_tr.sem, 16)
                em("sync", trs)
                s_tr.inc(16)
            e3T_done = s_tr.v

            fin_toks = []
            for tc in range(16):
                def mk_ldr(sync, tc=tc):
                    return sync.dma_start(
                        out=FB[:, (tc % 2) * 1024:(tc % 2) * 1024 + 1024],
                        in_=resid_h[tc * 128:(tc + 1) * 128, :])
                wrap, tok = stream_dma(mk_ldr, ld_stream)
                prev = fin_toks[-2][1] if len(fin_toks) >= 2 else None

                def ldr(sync, wrap=wrap, e3T_done=e3T_done, prev=prev, tc=tc):
                    if tc == 0:
                        sync.wait_ge(s_tr.sem, e3T_done)
                    if prev is not None:
                        swait(sync, st_stream, prev)
                    wrap(sync)
                em("sync", ldr)

                def fadd(vector, tc=tc, tok=tok):
                    swait(vector, ld_stream, tok)
                    vector.tensor_tensor(out=FB[:, (tc % 2) * 1024:(tc % 2) * 1024 + 1024],
                                         in0=FB[:, (tc % 2) * 1024:(tc % 2) * 1024 + 1024],
                                         in1=A1[:, tc * 1024:(tc + 1) * 1024], op=ALU.add)
                fa = ve(fadd)

                def mk_str(scalar, tc=tc):
                    return scalar.dma_start(
                        out=resid2_h[tc * 128:(tc + 1) * 128, :],
                        in_=FB[:, (tc % 2) * 1024:(tc % 2) * 1024 + 1024])
                wrap2, tok2 = stream_dma(mk_str, st_stream)
                fin_toks.append((tc, tok2))

                def strx(scalar, wrap2=wrap2, fa=fa):
                    scalar.wait_ge(s_ve.sem, fa)
                    wrap2(scalar)
                em("scalar", strx)
            return fin_toks

        if n_blocks >= 2:
            b1_toks, p1fins = block1(mid_emit=ffn_wprep_pass1)
            mean_fin = ffn_wprep_pass1_fin(p1fins)
            entry_ffn1 = (s_ve.v, s_se.v)
            means_done, kqvb_tok, wd_last, wgwu_last, psum_free_se = \
                ffn_wprep_rest(mean_fin)
            b2_toks = ffn_block(resid_h, resid_h if n_blocks >= 3 else resid2_h,
                                means_done, [t for _, t in b1_toks][-2:],
                                entry=entry_ffn1, wgwu_toks=wgwu_last,
                                wd_toks=wd_last, psum_free_se=psum_free_se)
        else:
            b1_toks, _ = block1()
        if n_blocks >= 3:
            b3_toks = block3(kqvb_tok, b2_toks)
        if n_blocks >= 4:
            b4_toks = ffn_block(resid2_h, y_out, means_done,
                                [t for _, t in b3_toks][-2:])

        if n_blocks == 1:
            def yc(gpsimd, toks=b1_toks):
                for _, t in toks:
                    swait(gpsimd, st_stream, t)
                gpsimd.dma_start(out=y_out[:, :], in_=resid_h[:, :]).then_inc(s_out.sem, 16)
                gpsimd.wait_ge(s_out.sem, 16)
            em("gpsimd", yc)
            s_out.v += 16
        elif n_blocks == 2:
            def yc(gpsimd, toks=b2_toks):
                for _, t in toks:
                    swait(gpsimd, st_stream, t)
                gpsimd.dma_start(out=y_out[:, :], in_=resid2_h[:, :]).then_inc(s_out.sem, 16)
                gpsimd.wait_ge(s_out.sem, 16)
            em("gpsimd", yc)
            s_out.v += 16
        elif n_blocks == 3:
            def yc(gpsimd, toks=b3_toks):
                for _, t in toks:
                    swait(gpsimd, st_stream, t)
                gpsimd.dma_start(out=y_out[:, :], in_=resid2_h[:, :]).then_inc(s_out.sem, 16)
                gpsimd.wait_ge(s_out.sem, 16)
            em("gpsimd", yc)
            s_out.v += 16
        else:
            def yc(gpsimd, toks=b4_toks):
                for _, t in toks:
                    swait(gpsimd, st_stream, t)
            em("gpsimd", yc)

        with nc.Block() as block:
            @block.sync
            def _(sync):
                for fn in prog["sync"]:
                    fn(sync)

            @block.vector
            def _(vector):
                for fn in prog["vector"]:
                    fn(vector)

            @block.scalar
            def _(scalar):
                for fn in prog["scalar"]:
                    fn(scalar)

            @block.tensor
            def _(tensor):
                for fn in prog["tensor"]:
                    fn(tensor)

            @block.gpsimd
            def _(gpsimd):
                for fn in prog["gpsimd"]:
                    fn(gpsimd)

    return nc


def make_host_constants():
    ident = np.eye(128, dtype=ml_dtypes.bfloat16)
    identf = np.eye(128, dtype=np.float32)
    iota_k = (np.arange(32)[None, :] * 128 + np.arange(128)[:, None]).astype(np.float32)
    iota_q = np.arange(S, dtype=np.float32)[None, :]
    masks = np.zeros((4, 128, 512), dtype=np.float32)
    for a in range(4):
        kpos = a * 128 + np.arange(128)[:, None]
        qpos = np.arange(512)[None, :]
        masks[a][kpos > qpos] = NEG
    return ident, identf, iota_k, iota_q, masks.astype(ml_dtypes.bfloat16)


def kernel(L0_attn_a_x, L0_attn_a_Wq, L0_attn_a_Wk, L0_attn_a_Wv, L0_attn_a_Wp,
           L0_attn_a_bp, L0_attn_b_W_kqv, L0_attn_b_m, L0_ffn_w_gate,
           L0_ffn_w_up, L0_ffn_w_down):
    x = np.ascontiguousarray(np.asarray(L0_attn_a_x, dtype=np.float32))
    ident, identf, iota_k, iota_q, masks = make_host_constants()
    nc = build_nc(N_BLOCKS)
    common = {
        "wq": np.ascontiguousarray(L0_attn_a_Wq), "wk": np.ascontiguousarray(L0_attn_a_Wk),
        "wv": np.ascontiguousarray(L0_attn_a_Wv), "wp": np.ascontiguousarray(L0_attn_a_Wp),
        "bp": np.ascontiguousarray(np.asarray(L0_attn_a_bp).reshape(1, D)),
        "wkqv": np.ascontiguousarray(L0_attn_b_W_kqv),
        "m": np.ascontiguousarray(np.asarray(L0_attn_b_m).reshape(1, H)),
        "wg": np.ascontiguousarray(L0_ffn_w_gate), "wu": np.ascontiguousarray(L0_ffn_w_up),
        "wd": np.ascontiguousarray(L0_ffn_w_down),
        "ident": ident, "identf_in": identf, "iota_q": iota_q, "iota_k": iota_k,
        "masks": masks,
    }
    in_maps = []
    for c in range(8):
        bb, hh = c >> 1, c & 1
        mm = dict(common)
        mm["x"] = np.ascontiguousarray(x[bb, hh * TOK:(hh + 1) * TOK, :])
        mfull = np.asarray(L0_attn_b_m, dtype=np.float32).reshape(H)
        mm["m_my"] = np.ascontiguousarray(mfull[2 * c:2 * c + 2].reshape(1, 2))
        in_maps.append(mm)
    res = run_bass_kernel_spmd(nc, in_maps, core_ids=list(range(8)))
    out = np.empty((B, S, D), dtype=np.float32)
    for c in range(8):
        bb, hh = c >> 1, c & 1
        out[bb, hh * TOK:(hh + 1) * TOK, :] = res.results[c]["y"].reshape(TOK, D)
    return out


# version 07 (final)
# speedup vs baseline: 1.1697x; 1.0613x over previous
import numpy as np
import ml_dtypes

import concourse.bass as bass
import concourse.mybir as mybir
from concourse.bass_utils import run_bass_kernel_spmd
from concourse.aot_env import allow_ici_all_to_all

F32 = mybir.dt.float32
BF16 = mybir.dt.bfloat16
I32 = mybir.dt.int32
AF = mybir.ActivationFunctionType
ALU = mybir.AluOpType
AX = mybir.AxisListType

B, S, D = 4, 4096, 1024
H, HD = 16, 64
INNER = 4096
TOK = 2048
NEG = -1.0e30
N_BLOCKS = 4


class Ctr:
    def __init__(self, sem):
        self.sem = sem
        self.v = 0

    def inc(self, n=1):
        self.v += n
        return self.v


class DmaStream:

    def __init__(self, a, b):
        self.sems = [a, b]
        self.k = 0

    def issue(self, dma):
        s = self.sems[self.k % 2]
        dma.then_inc(s.sem, 16)
        s.v += 16
        tok = (self.k % 2, s.v)
        self.k += 1
        return tok

    def wait(self, eng, tok):
        par, val = tok
        eng.wait_ge(self.sems[par].sem, val)


def build_nc(n_blocks=N_BLOCKS):
    nc = bass.Bass("TRN2", target_bir_lowering=False, debug=False)

    x_in = nc.dram_tensor("x", [TOK, D], F32, kind="ExternalInput").ap()
    wq_h = nc.dram_tensor("wq", [D, D], F32, kind="ExternalInput").ap()
    wk_h = nc.dram_tensor("wk", [D, D], F32, kind="ExternalInput").ap()
    wv_h = nc.dram_tensor("wv", [D, D], F32, kind="ExternalInput").ap()
    wp_h = nc.dram_tensor("wp", [D, D], F32, kind="ExternalInput").ap()
    bp_h = nc.dram_tensor("bp", [1, D], F32, kind="ExternalInput").ap()
    wkqv_h = nc.dram_tensor("wkqv", [D, 3 * D], F32, kind="ExternalInput").ap()
    m_h = nc.dram_tensor("m", [1, H], F32, kind="ExternalInput").ap()
    mmy_h = nc.dram_tensor("m_my", [1, 2], F32, kind="ExternalInput").ap()
    wg_h = nc.dram_tensor("wg", [INNER, D], F32, kind="ExternalInput").ap()
    wu_h = nc.dram_tensor("wu", [INNER, D], F32, kind="ExternalInput").ap()
    wd_h = nc.dram_tensor("wd", [D, INNER], F32, kind="ExternalInput").ap()
    ident_h = nc.dram_tensor("ident", [128, 128], BF16, kind="ExternalInput").ap()
    identf_h = nc.dram_tensor("identf_in", [128, 128], F32, kind="ExternalInput").ap()
    iotaq_h = nc.dram_tensor("iota_q", [1, S], F32, kind="ExternalInput").ap()
    iotak_h = nc.dram_tensor("iota_k", [128, 32], F32, kind="ExternalInput").ap()
    masks_h = nc.dram_tensor("masks", [4, 128, 512], BF16, kind="ExternalInput").ap()
    y_out = nc.dram_tensor("y", [TOK, D], F32, kind="ExternalOutput").ap()

    resid_h = nc.dram_tensor("resid", [TOK, D], F32)
    resid2_h = nc.dram_tensor("resid2", [TOK, D], F32)
    kv_ar_in = nc.dram_tensor("kv_ar_in", [D, D], F32)
    kv_ar_out = nc.dram_tensor("kv_ar_out", [D, D], F32)
    wgT_hbm = nc.dram_tensor("wgT_hbm", [32, 128, D], BF16)
    wuT_hbm = nc.dram_tensor("wuT_hbm", [32, 128, D], BF16)
    wdT_hbm = nc.dram_tensor("wdT_hbm", [32, 128, D], BF16)
    gu_hbm = nc.dram_tensor("gu_hbm", [32, 128, TOK], F32)
    kqvb_hbm = nc.dram_tensor("kqvb_hbm", [D, 3 * D], BF16)
    a2a_in = nc.dram_tensor("a2a_in", [8, 128, 6144], BF16)
    a2a_out = nc.dram_tensor("a2a_out", [8, 128, 6144], BF16)
    oa_in = nc.dram_tensor("oa_in", [8, 128, 2048], BF16)
    oa_out = nc.dram_tensor("oa_out", [8, 128, 2048], BF16)
    o_scr = nc.dram_tensor("o_scr", [64, 64, 512], F32)
    row_hbm = nc.dram_tensor("row_hbm", [16, 128], F32)

    from contextlib import ExitStack
    with ExitStack() as stx:
        sbt = lambda name, shape, dt: stx.enter_context(nc.sbuf_tensor(name, shape, dt))
        pst = lambda name, shape, dt=F32: stx.enter_context(nc.psum_tensor(name, shape, dt))
        mksem = lambda name: Ctr(stx.enter_context(nc.semaphore(name)))

        A0 = sbt("A0", [128, 16384], BF16)
        A1 = sbt("A1", [128, 16384], BF16)
        A2 = sbt("A2", [128, 16640], BF16)
        A3 = sbt("A3", [128, 16384], BF16)
        FB = sbt("FB", [128, 8192], F32)
        WKB = sbt("WKB", [128, 2048], BF16)
        ROWF = sbt("ROWF", [128, 2560], F32)
        ROWB = sbt("ROWB", [128, 8192], BF16)
        masksb = sbt("masksb", [128, 2048], BF16)
        ptb = sbt("ptb", [128, 2048], BF16)
        nrows = sbt("nrows", [64, 512], F32)
        rtmp = sbt("rtmp", [16, 128], F32)
        stat = sbt("stat", [128, 96], F32)
        identb = sbt("identb", [128, 128], BF16)
        identf = sbt("identf", [128, 128], F32)
        iotak = sbt("iotak", [128, 32], F32)
        mjc = sbt("mjc", [128, 64], F32)
        onesbf = sbt("onesbf", [1, 128], BF16)
        onesff = sbt("onesff", [1, 128], F32)
        mrow = sbt("mrow", [1, 32], F32)
        qi32 = sbt("qi32", [128, 1024], I32)

        p0 = pst("p0", [128, 512])
        p1 = pst("p1", [128, 512])
        p2 = pst("p2", [128, 512])
        p3 = pst("p3", [128, 512])
        p4 = pst("p4", [128, 512])
        p5 = pst("p5", [128, 512])
        p6 = pst("p6", [128, 512])
        p7 = pst("p7", [128, 512])
        PSUMS = [p0, p1, p2, p3, p4, p5, p6, p7]

        s_ld = mksem("s_ld")
        s_la = mksem("s_la")
        s_lb = mksem("s_lb")
        s_tr = mksem("s_tr")
        s_pe = mksem("s_pe")
        s_ve = mksem("s_ve")
        s_se = mksem("s_se")
        s_sa = mksem("s_sa")
        s_sb = mksem("s_sb")
        s_sd = mksem("s_sd")
        s_gd = mksem("s_gd")
        s_ga = mksem("s_ga")
        s_gb = mksem("s_gb")
        s_cc = mksem("s_cc")
        s_out = mksem("s_out")
        s_va = mksem("s_va")
        s_vb = mksem("s_vb")

        ld_stream = DmaStream(s_la, s_lb)
        st_stream = DmaStream(s_sa, s_sb)
        gd_stream = DmaStream(s_ga, s_gb)
        wl_stream = DmaStream(s_va, s_vb)

        prog = {k: [] for k in ("sync", "vector", "scalar", "tensor", "gpsimd")}

        def em(eng, fn):
            prog[eng].append(fn)

        def ve(fn):
            def wrap(vector, fn=fn):
                fn(vector)
                vector.drain()
                vector.nop().then_inc(s_ve.sem, 1)
            em("vector", wrap)
            return s_ve.inc()

        def se(fn):
            def wrap(scalar, fn=fn):
                fn(scalar)
                scalar.drain()
                scalar.nop().then_inc(s_se.sem, 1)
            em("scalar", wrap)
            return s_se.inc()

        def mm_phase(groups, pool, pe_waits=(), width=512):
            npool = len(pool)
            pe_base = s_pe.v
            ev_base = s_ve.v
            waits = list(pe_waits)

            def t_ops(tensor, groups=groups, pool=pool, pe_base=pe_base,
                      ev_base=ev_base, waits=waits, npool=npool, width=width):
                for semh, val in waits:
                    tensor.wait_ge(semh, val)
                for g, (pieces, _e) in enumerate(groups):
                    pp = pool[g % npool]
                    if g >= npool:
                        tensor.wait_ge(s_ve.sem, ev_base + g - npool + 1)
                    n = len(pieces)
                    for i, (lt, rr) in enumerate(pieces):
                        mm = tensor.matmul(pp[:, 0:width], lt, rr,
                                           start=(i == 0), stop=(i == n - 1))
                        if i == n - 1:
                            mm.then_inc(s_pe.sem, 1)
            em("tensor", t_ops)
            s_pe.v += len(groups)

            def v_ops(vector, groups=groups, pool=pool, pe_base=pe_base, npool=npool):
                for g, (_p, evf) in enumerate(groups):
                    pp = pool[g % npool]
                    vector.wait_ge(s_pe.sem, pe_base + g + 1)
                    evf(vector, pp)
                    vector.drain()
                    vector.nop().then_inc(s_ve.sem, 1)
            em("vector", v_ops)
            s_ve.v += len(groups)
            return s_ve.v

        def copy_evac(dst, width=512):
            def evf(vector, pp, dst=dst, width=width):
                vector.tensor_copy(out=dst, in_=pp[:, 0:width])
            return evf

        def phase_gate(toks=()):
            vev, sev = s_ve.v, s_se.v
            tks = list(toks)

            def g(sync, vev=vev, sev=sev, tks=tks):
                sync.wait_ge(s_ve.sem, vev)
                sync.wait_ge(s_se.sem, sev)
                for t in tks:
                    swait(sync, st_stream, t)
            em("sync", g)

        def c_ld(sync):
            sync.dma_start(out=identb[:], in_=ident_h[:, :]).then_inc(s_ld.sem, 16)
            sync.dma_start(out=identf[:], in_=identf_h[:, :]).then_inc(s_ld.sem, 16)
            sync.dma_start(out=iotak[:], in_=iotak_h[:, :]).then_inc(s_ld.sem, 16)
            sync.dma_start(out=masksb[:].rearrange("p (a q) -> p a q", a=4), in_=masks_h.rearrange("a p q -> p a q")).then_inc(s_ld.sem, 16)
            sync.dma_start(out=mrow[0:1, 0:H], in_=m_h[:, :]).then_inc(s_ld.sem, 16)
            sync.dma_start(out=mrow[0:1, 24:26], in_=mmy_h[:, :]).then_inc(s_ld.sem, 16)
            sync.dma_start(out=ROWF[0:1, 0:D], in_=bp_h[:, :]).then_inc(s_ld.sem, 16)
        em("sync", c_ld)
        s_ld.v += 112
        consts_ld = s_ld.v

        def c_ones(vector):
            vector.wait_ge(s_ld.sem, consts_ld)
            vector.memset(onesbf[:], 1.0)
            vector.memset(onesff[:], 1.0)
            vector.drain()
            vector.tensor_copy(out=ROWB[0:1, 0:D], in_=ROWF[0:1, 0:D])
        consts_done = ve(c_ones)

        def stream_dma(emit_fn, stream):
            par = stream.k % 2
            sem = stream.sems[par]
            val = sem.v + 16

            def wrap(eng, emit_fn=emit_fn, sem=sem, val=val):
                if val > 16:
                    eng.wait_ge(sem.sem, val - 16)
                emit_fn(eng).then_inc(sem.sem, 16)
            stream.k += 1
            sem.v = val
            return wrap, (par, val)

        def swait(eng, stream, tok):
            par, val = tok
            eng.wait_ge(stream.sems[par].sem, val)

        def ln_half2(src, th, dst_ap_fn, quant=False, entry=None, entry_toks=()):
            entry_ve, entry_se = entry if entry is not None else (s_ve.v, s_se.v)
            entry_toks = list(entry_toks)
            for tc8 in range(8):
                tc = th * 8 + tc8
                buf = (tc8 % 2) * 1024
                done_prev = (s_ve.v if quant else s_se.v)

                def mk_ld(sync, tc=tc, buf=buf):
                    return sync.dma_start(out=FB[:, buf:buf + 1024],
                                          in_=src[tc * 128:(tc + 1) * 128, :])
                wrap, tok = stream_dma(mk_ld, ld_stream)

                def ldx(sync, wrap=wrap, tc8=tc8, done_prev=done_prev, quant=quant,
                        entry_ve=entry_ve, entry_se=entry_se, entry_toks=entry_toks):
                    if tc8 < 2:
                        sync.wait_ge(s_ve.sem, entry_ve)
                        sync.wait_ge(s_se.sem, entry_se)
                        for t_ in entry_toks:
                            swait(sync, st_stream, t_)
                    else:
                        if quant:
                            sync.wait_ge(s_ve.sem, done_prev - 1)
                        else:
                            sync.wait_ge(s_se.sem, done_prev - 1)
                    wrap(sync)
                em("sync", ldx)

                def lsq(scalar, tok=tok, buf=buf):
                    swait(scalar, ld_stream, tok)
                    scalar.activation(out=FB[:, 2048:3072], in_=FB[:, buf:buf + 1024],
                                      func=AF.Square, accum_out=stat[:, 1:2])
                sqd = se(lsq)

                def lstats(vector, tok=tok, buf=buf, sqd=sqd):
                    swait(vector, ld_stream, tok)
                    xt = FB[:, buf:buf + 1024]
                    vector.reduce_sum(out=stat[:, 0:1], in_=xt, axis=AX.X)
                    vector.drain()
                    vector.wait_ge(s_se.sem, sqd)
                    vector.tensor_scalar(out=stat[:, 2:3], in0=stat[:, 0:1],
                                         scalar1=1.0 / D, scalar2=None, op0=ALU.mult)
                    vector.drain()
                    vector.scalar_tensor_tensor(out=stat[:, 3:4], in0=stat[:, 2:3],
                                                scalar=-1.0, in1=stat[:, 2:3],
                                                op0=ALU.mult, op1=ALU.mult)
                    vector.drain()
                    vector.scalar_tensor_tensor(out=stat[:, 3:4], in0=stat[:, 1:2],
                                                scalar=1.0 / D, in1=stat[:, 3:4],
                                                op0=ALU.mult, op1=ALU.add)
                    vector.drain()
                    vector.tensor_scalar(out=stat[:, 3:4], in0=stat[:, 3:4],
                                         scalar1=1e-5, scalar2=None, op0=ALU.add)
                    vector.drain()
                    vector.reciprocal(out=stat[:, 3:4], in_=stat[:, 3:4])
                vst = ve(lstats)

                def lsqrt(scalar, vst=vst):
                    scalar.wait_ge(s_ve.sem, vst)
                    scalar.activation(out=stat[:, 4:5], in_=stat[:, 3:4], func=AF.Sqrt)
                sst = se(lsqrt)

                def lbias(vector, sst=sst):
                    vector.wait_ge(s_se.sem, sst)
                    vector.scalar_tensor_tensor(out=stat[:, 5:6], in0=stat[:, 2:3],
                                                scalar=-1.0, in1=stat[:, 4:5],
                                                op0=ALU.mult, op1=ALU.mult)
                vbs = ve(lbias)

                if not quant:
                    def lap(scalar, tc8=tc8, buf=buf, vbs=vbs):
                        scalar.wait_ge(s_ve.sem, vbs)
                        scalar.activation(out=dst_ap_fn(tc8), in_=FB[:, buf:buf + 1024],
                                          func=AF.Identity,
                                          scale=stat[:, 4:5], bias=stat[:, 5:6])
                    se(lap)
                else:
                    def lap(scalar, tc8=tc8, buf=buf, vbs=vbs):
                        scalar.wait_ge(s_ve.sem, vbs)
                        scalar.activation(out=FB[:, 3072:4096], in_=FB[:, buf:buf + 1024],
                                          func=AF.Identity,
                                          scale=stat[:, 4:5], bias=stat[:, 5:6])
                    sap = se(lap)

                    def qsteps(vector, tc8=tc8, sap=sap):
                        vector.wait_ge(s_se.sem, sap)
                        xln = FB[:, 3072:4096]
                        vector.reduce_max(out=stat[:, 6:7], in_=xln, axis=AX.X,
                                          apply_absolute_value=True)
                        vector.drain()
                        vector.tensor_scalar(out=stat[:, 6:7], in0=stat[:, 6:7],
                                             scalar1=1e-5, scalar2=None, op0=ALU.max)
                        vector.drain()
                        vector.tensor_scalar(out=stat[:, 80 + tc8:81 + tc8],
                                             in0=stat[:, 6:7], scalar1=1.0 / 127.0,
                                             scalar2=None, op0=ALU.mult)
                        vector.drain()
                        vector.reciprocal(out=stat[:, 7:8], in_=stat[:, 6:7])
                        vector.drain()
                        vector.tensor_scalar(out=stat[:, 7:8], in0=stat[:, 7:8],
                                             scalar1=127.0, scalar2=None, op0=ALU.mult)
                        vector.drain()
                        vector.tensor_scalar(out=qi32[:, 0:1024], in0=xln,
                                             scalar1=stat[:, 7:8], scalar2=128.5,
                                             op0=ALU.mult, op1=ALU.add)
                        vector.drain()
                        vector.tensor_scalar(out=dst_ap_fn(tc8), in0=qi32[:, 0:1024],
                                             scalar1=-128.0, scalar2=None, op0=ALU.add)
                    ve(qsteps)
            return s_se.v, s_ve.v

        def prep_wT(whbm, slot_ap_fn):
            for slab in range(4):
                r0 = slab * 256
                trb = s_tr.v

                def mk_ld(gpsimd, whbm=whbm, r0=r0):
                    return gpsimd.dma_start(
                        out=WKB[:, 0:2048].rearrange("p (s d) -> p s d", s=2),
                        in_=whbm[r0:r0 + 256, :].rearrange("(s p) d -> p s d", p=128))
                wrap, tok = stream_dma(mk_ld, gd_stream)

                def ldw(gpsimd, wrap=wrap, trb=trb):
                    gpsimd.wait_ge(s_tr.sem, trb)
                    wrap(gpsimd)
                em("gpsimd", ldw)

                for ss in range(2):
                    e0 = r0 + ss * 128

                    def trw(sync, ss=ss, e0=e0, tok=tok):
                        if ss == 0:
                            swait(sync, gd_stream, tok)
                        sync.dma_start_transpose(
                            slot_ap_fn().rearrange("p (c e) -> p c e", c=8)[:, :, e0:e0 + 128],
                            WKB[:, ss * D:(ss + 1) * D],
                        ).then_inc(s_tr.sem, 16)
                    em("sync", trw)
                    s_tr.inc(16)
            return s_tr.v

        def transpose16(src_ap_fn, dst, gates=()):
            def trg(sync, gates=gates):
                for g in gates:
                    g(sync)
            em("sync", trg)
            for tc in range(16):
                def trx(sync, tc=tc):
                    sync.dma_start_transpose(
                        dst[:].rearrange("p (c t) -> p c t", c=8)[:, :, tc * 128:(tc + 1) * 128],
                        src_ap_fn(tc),
                    ).then_inc(s_tr.sem, 16)
                em("sync", trx)
                s_tr.inc(16)
            return s_tr.v

        def block1(mid_emit=None):
            A3lo = lambda: A3[:, 0:8192]
            A3hi = lambda: A3[:, 8192:16384]
            prep_wT(wq_h, A3hi)
            wkT_done = prep_wT(wk_h, A3lo)

            lnse0, _ = ln_half2(x_in, 0, lambda tc8: A2[:, tc8 * D:(tc8 + 1) * D])
            lnse1, _ = ln_half2(x_in, 1, lambda tc8: A2[:, 8192 + tc8 * D: 8192 + (tc8 + 1) * D])

            kv_last = None
            for th in range(2):
                kvprev = kv_last
                lnse = (lnse0, lnse1)[th]
                a2base = th * 8192

                def trg(sync, lnse=lnse, wk=wkT_done, th=th, kvprev=kvprev):
                    sync.wait_ge(s_se.sem, lnse)
                    if th == 0:
                        sync.wait_ge(s_tr.sem, wk)
                    else:
                        sync.wait_ge(s_ve.sem, kvprev)
                em("sync", trg)
                for tc8 in range(8):
                    def trx(sync, tc8=tc8, a2base=a2base):
                        sync.dma_start_transpose(
                            A0[:, 0:8192].rearrange("p (c t) -> p c t", c=8)[:, :, tc8 * 128:(tc8 + 1) * 128],
                            A2[:, a2base + tc8 * D: a2base + (tc8 + 1) * D],
                        ).then_inc(s_tr.sem, 16)
                    em("sync", trx)
                    s_tr.inc(16)
                xT_done = s_tr.v

                groups = []
                for ec in range(8):
                    for tq in range(2):
                        pieces = [(A3[:, 8192 + c * D + ec * 128: 8192 + c * D + ec * 128 + 128],
                                   A0[:, c * 1024 + tq * 512: c * 1024 + tq * 512 + 512])
                                  for c in range(8)]
                        dst = A1[:, ec * 2048 + th * 1024 + tq * 512:
                                 ec * 2048 + th * 1024 + tq * 512 + 512]
                        groups.append((pieces, copy_evac(dst)))
                q_done = mm_phase(groups, [p0, p1],
                                  pe_waits=[(s_tr.sem, xT_done), (s_ve.sem, consts_done)])

                groups = []
                for tc8 in range(8):
                    for half in range(2):
                        pieces = [(A0[:, c * 1024 + tc8 * 128: c * 1024 + tc8 * 128 + 128],
                                   A3[:, c * D + half * 512: c * D + half * 512 + 512])
                                  for c in range(8)]
                        dst = A0[:, 8192 + tc8 * 1024 + half * 512:
                                 8192 + tc8 * 1024 + half * 512 + 512]
                        groups.append((pieces, copy_evac(dst)))
                k_done = mm_phase(groups, [p2, p3])

                def wv_gate(sync, q_done=q_done):
                    sync.wait_ge(s_ve.sem, q_done)
                em("sync", wv_gate)
                wvT_done = prep_wT(wv_h, A3hi)

                groups = []
                for tc8 in range(8):
                    for half in range(2):
                        pieces = [(A0[:, c * 1024 + tc8 * 128: c * 1024 + tc8 * 128 + 128],
                                   A3[:, 8192 + c * D + half * 512: 8192 + c * D + half * 512 + 512])
                                  for c in range(8)]
                        dst = A2[:, tc8 * 1024 + half * 512: tc8 * 1024 + half * 512 + 512]
                        groups.append((pieces, copy_evac(dst)))
                v_done = mm_phase(groups, [p4, p5], pe_waits=[(s_tr.sem, wvT_done)])

                groups = []
                for ec in range(8):
                    for fh in range(2):
                        pieces = [(A0[:, 8192 + tc8 * 1024 + ec * 128:
                                      8192 + tc8 * 1024 + ec * 128 + 128],
                                   A2[:, tc8 * 1024 + fh * 512: tc8 * 1024 + fh * 512 + 512])
                                  for tc8 in range(8)]
                        dst = FB[:, ec * 1024 + fh * 512: ec * 1024 + fh * 512 + 512]
                        if th == 0:
                            evf = copy_evac(dst)
                        else:
                            def evf(vector, pp, dst=dst):
                                vector.tensor_tensor(out=dst, in0=pp[:, 0:512],
                                                     in1=dst, op=ALU.add)
                        groups.append((pieces, evf))
                kvw = [(s_ve.sem, v_done)]
                if th == 0:
                    kvw.append((s_se.sem, lnse1))
                kv_last = mm_phase(groups, [p6, p7], pe_waits=kvw)

                if th == 0:
                    def wq_gate(sync, kv_last=kv_last):
                        sync.wait_ge(s_ve.sem, kv_last)
                    em("sync", wq_gate)
                    prep_wT(wq_h, A3hi)

            gd0, cc0 = s_gd.v, s_cc.v

            def e1(gpsimd, gd0=gd0, cc0=cc0, kvdone=kv_last):
                gpsimd.wait_ge(s_ve.sem, kvdone)
                gpsimd.dma_start(out=kv_ar_in.rearrange("(c p) f -> p c f", p=128),
                                 in_=FB[:, 0:8192].rearrange("p (c f) -> p c f", c=8)
                                 ).then_inc(s_gd.sem, 16)
                gpsimd.wait_ge(s_gd.sem, gd0 + 16)
                gpsimd.collective_compute(
                    "AllReduce", ALU.add,
                    replica_groups=[[0, 1], [2, 3], [4, 5], [6, 7]],
                    ins=[kv_ar_in.ap().opt()], outs=[kv_ar_out.ap().opt()],
                ).then_inc(s_cc.sem, 1)
                gpsimd.wait_ge(s_cc.sem, cc0 + 1)
                gpsimd.dma_start(out=A2[:, 8192:16384].rearrange("p (c f) -> p c f", c=8),
                                 in_=kv_ar_out.rearrange("(c p) f -> p c f", p=128)
                                 ).then_inc(s_gd.sem, 16)
            em("gpsimd", e1)
            s_gd.v += 32
            s_cc.v += 1
            kvfull_done = s_gd.v

            mid_out = mid_emit(gd0 + 16) if mid_emit is not None else None

            wpT_done = prep_wT(wp_h, A3hi)

            groups = []
            for fc in range(8):
                for tt in range(4):
                    pieces = [(A2[:, 8192 + ec * 1024 + fc * 128:
                                  8192 + ec * 1024 + fc * 128 + 128],
                               A1[:, ec * 2048 + tt * 512: ec * 2048 + tt * 512 + 512])
                              for ec in range(8)]
                    dst = A0[:, fc * 2048 + tt * 512: fc * 2048 + tt * 512 + 512]
                    groups.append((pieces, copy_evac(dst)))
            z_done = mm_phase(groups, [p0, p1, p2, p3],
                              pe_waits=[(s_gd.sem, kvfull_done)])

            pe_base = s_pe.v
            ev_base = s_ve.v
            pool = [p4, p5, p6, p7]

            def prj_t(tensor, pe_base=pe_base, ev_base=ev_base,
                      z_done=z_done, wpT_done=wpT_done):
                tensor.wait_ge(s_ve.sem, z_done)
                tensor.wait_ge(s_tr.sem, wpT_done)
                g = 0
                for tc in range(16):
                    for gh in range(2):
                        pp = pool[g % 4]
                        if g >= 4:
                            tensor.wait_ge(s_ve.sem, ev_base + g - 3)
                        for fc in range(8):
                            tensor.matmul(pp[:, 0:512],
                                          A0[:, fc * 2048 + tc * 128: fc * 2048 + tc * 128 + 128],
                                          A3[:, 8192 + fc * D + gh * 512:
                                             8192 + fc * D + gh * 512 + 512],
                                          start=(fc == 0), stop=False)
                        tensor.matmul(pp[:, 0:512], onesbf[0:1, 0:128],
                                      ROWB[0:1, gh * 512: gh * 512 + 512],
                                      start=False, stop=True).then_inc(s_pe.sem, 1)
                        g += 1
            em("tensor", prj_t)
            s_pe.v += 32

            st_wraps = []
            rtoks = []
            for tc in range(16):
                def mk_st(scalar, tc=tc):
                    return scalar.dma_start(out=resid_h[tc * 128:(tc + 1) * 128, :],
                                            in_=FB[:, (tc % 2) * 1024:(tc % 2) * 1024 + 1024])
                wrap, tok = stream_dma(mk_st, st_stream)
                st_wraps.append(wrap)
                rtoks.append((tc, tok))

            xtoks = []
            for tc in range(16):
                def mk_ld(sync, tc=tc):
                    return sync.dma_start(out=FB[:, (tc % 2) * 1024:(tc % 2) * 1024 + 1024],
                                          in_=x_in[tc * 128:(tc + 1) * 128, :])
                wrap, tok = stream_dma(mk_ld, ld_stream)
                xtoks.append(tok)
                pst_tok = rtoks[tc - 2][1] if tc >= 2 else None

                def ldx2(sync, wrap=wrap, tc=tc, ev_base=ev_base, pst_tok=pst_tok,
                         kv_read_done=gd0 + 16):
                    if tc < 2:
                        sync.wait_ge(s_gd.sem, kv_read_done)
                    else:
                        sync.wait_ge(s_ve.sem, ev_base + (tc - 2) * 2 + 2)
                        swait(sync, st_stream, pst_tok)
                    wrap(sync)
                em("sync", ldx2)

            def prj_v(vector, pe_base=pe_base, xtoks=xtoks):
                for tc in range(16):
                    swait(vector, ld_stream, xtoks[tc])
                    for gh in range(2):
                        g = tc * 2 + gh
                        pp = pool[g % 4]
                        vector.wait_ge(s_pe.sem, pe_base + g + 1)
                        sl = slice((tc % 2) * 1024 + gh * 512, (tc % 2) * 1024 + gh * 512 + 512)
                        vector.tensor_tensor(out=FB[:, sl], in0=pp[:, 0:512],
                                             in1=FB[:, sl], op=ALU.add)
                        vector.drain()
                        vector.nop().then_inc(s_ve.sem, 1)
            em("vector", prj_v)
            s_ve.v += 32

            for tc in range(16):
                tgt = ev_base + tc * 2 + 2

                def stx(scalar, wrap=st_wraps[tc], tgt=tgt):
                    scalar.wait_ge(s_ve.sem, tgt)
                    wrap(scalar)
                em("scalar", stx)
            return rtoks, mid_out

        def ffn_wprep_pass1(e1_gate_gd):
            slabs = []
            for wi, (whbm, cols) in enumerate(((wg_h, D), (wu_h, D), (wd_h, INNER))):
                for sl in range(16):
                    if cols == D:
                        src = whbm[sl * 256:(sl + 1) * 256, :].rearrange(
                            "(s p) d -> p s d", p=128)
                        is3d = True
                    else:
                        src = whbm[(sl // 2) * 128:(sl // 2) * 128 + 128,
                                   (sl % 2) * 2048:(sl % 2) * 2048 + 2048]
                        is3d = False
                    slabs.append((wi, sl, src, is3d))
            wraps, toks = [], []
            for i, (wi, sl, src, is3d) in enumerate(slabs):
                def mk_ld(scalar, src=src, is3d=is3d, i=i):
                    dst = FB[:, 2048 + (i % 2) * 2048: 2048 + (i % 2) * 2048 + 2048]
                    if is3d:
                        dst = dst.rearrange("p (s d) -> p s d", s=2)
                    return scalar.dma_start(out=dst, in_=src)
                wrap, tok = stream_dma(mk_ld, wl_stream)
                wraps.append(wrap)
                toks.append(tok)

            fins = []
            redv = []
            for i, (wi, sl, src, is3d) in enumerate(slabs):
                def red(vector, i=i, sl=sl, tok=toks[i]):
                    swait(vector, wl_stream, tok)
                    vector.reduce_sum(
                        out=stat[:, 40 + sl:41 + sl],
                        in_=FB[:, 2048 + (i % 2) * 2048: 2048 + (i % 2) * 2048 + 2048],
                        axis=AX.X, apply_absolute_value=True)
                redv.append(ve(red))
                if sl == 15:
                    def fin1(vector, wi=wi):
                        vector.reduce_sum(out=stat[:, 20 + wi:21 + wi],
                                          in_=stat[:, 40:56], axis=AX.X)
                    fins.append(ve(fin1))

            for i in range(len(slabs)):
                def ld(scalar, wrap=wraps[i], i=i, g=e1_gate_gd,
                       rprev=(redv[i - 2] if i >= 2 else None)):
                    if i == 0:
                        scalar.wait_ge(s_gd.sem, g)
                    if rprev is not None:
                        scalar.wait_ge(s_ve.sem, rprev)
                    wrap(scalar)
                em("scalar", ld)
            return fins

        def ffn_wprep_pass1_fin(fins):
            prev = None
            for wi in range(3):
                n = INNER * D

                def tsum(tensor, wi=wi, f1=fins[wi], prev=prev):
                    tensor.wait_ge(s_ve.sem, f1)
                    if prev is not None:
                        tensor.wait_ge(s_ve.sem, prev)
                    tensor.transpose(p0[0:1, 0:128], stat[:, 20 + wi:21 + wi],
                                     identf[:]).then_inc(s_pe.sem, 1)
                em("tensor", tsum)
                pe_t = s_pe.inc()

                def fin2(vector, pe_t=pe_t, wi=wi, n=n):
                    vector.wait_ge(s_pe.sem, pe_t)
                    vector.reduce_sum(out=stat[0:1, 10:11], in_=p0[0:1, 0:128],
                                      axis=AX.X)
                    vector.drain()
                    vector.tensor_scalar(out=stat[0:1, 13 + wi:14 + wi],
                                         in0=stat[0:1, 10:11],
                                         scalar1=1.0 / n, scalar2=1e-5,
                                         op0=ALU.mult, op1=ALU.max)
                prev = ve(fin2)
            return prev

        def ffn_wprep_rest(mean_fin):
            for wi in range(3):
                def bch(tensor, wi=wi, mean_fin=mean_fin):
                    tensor.wait_ge(s_ve.sem, mean_fin)
                    tensor.matmul(p1[:, 1 + wi:2 + wi], onesff[0:1, 0:128],
                                  stat[0:1, 13 + wi:14 + wi], start=True, stop=True
                                  ).then_inc(s_pe.sem, 1)
                em("tensor", bch)
                pe_b = s_pe.inc()

                def bce(vector, pe_b=pe_b, wi=wi):
                    vector.wait_ge(s_pe.sem, pe_b)
                    vector.tensor_copy(out=stat[:, 17 + wi:18 + wi],
                                       in_=p1[:, 1 + wi:2 + wi])
                means_done = ve(bce)

            def emit_scale(wi):
                def swb(vector, wi=wi):
                    vector.reciprocal(out=stat[0:1, 10:11],
                                      in_=stat[0:1, 13 + wi:14 + wi])
                sw_done = ve(swb)

                def bcast(tensor, sw_done=sw_done):
                    tensor.wait_ge(s_ve.sem, sw_done)
                    tensor.matmul(p1[:, 0:1], onesff[0:1, 0:128],
                                  stat[0:1, 10:11], start=True, stop=True
                                  ).then_inc(s_pe.sem, 1)
                em("tensor", bcast)
                pe_b = s_pe.inc()

                def bc_ev(vector, pe_b=pe_b):
                    vector.wait_ge(s_pe.sem, pe_b)
                    vector.tensor_copy(out=stat[:, 12:13], in_=p1[:, 0:1])
                return ve(bc_ev)

            def quant3(vector, src, dst):
                vector.tensor_scalar(out=qi32[:, 0:1024], in0=src,
                                     scalar1=stat[:, 12:13], scalar2=128.5,
                                     op0=ALU.mult, op1=ALU.add)
                vector.drain()
                vector.tensor_scalar(out=qi32[:, 0:1024], in0=qi32[:, 0:1024],
                                     scalar1=129.0, scalar2=127.0,
                                     op0=ALU.min, op1=ALU.max)
                vector.drain()
                vector.tensor_scalar(out=dst, in0=qi32[:, 0:1024],
                                     scalar1=-128.0, scalar2=None, op0=ALU.add)
                vector.drain()

            carry_ev = {}
            carry_tr = {}

            emit_scale(2)
            wraps, toks = [], []
            for ph in range(32):
                def mk_ld(gpsimd, ph=ph):
                    ds, iq = ph // 4, ph % 4
                    return gpsimd.dma_start(
                        out=FB[:, 2048 + (ph % 2) * 1024: 2048 + (ph % 2) * 1024 + 1024],
                        in_=wd_h[ds * 128:(ds + 1) * 128, iq * 1024:(iq + 1) * 1024])
                wrap, tok = stream_dma(mk_ld, gd_stream)
                wraps.append(wrap)
                toks.append(tok)

            def kickd(gpsimd, w0=wraps[0], w1=wraps[1], mf=mean_fin):
                gpsimd.wait_ge(s_ve.sem, mf)
                w0(gpsimd)
                w1(gpsimd)
            em("gpsimd", kickd)

            tr_pe, ev_se = {}, {}
            for ph in range(32):
                ds, iq = ph // 4, ph % 4

                def qzd(vector, ph=ph, tok=toks[ph], tprev=tr_pe.get(ph - 2)):
                    swait(vector, gd_stream, tok)
                    if tprev is not None:
                        vector.wait_ge(s_pe.sem, tprev)
                    quant3(vector,
                           FB[:, 2048 + (ph % 2) * 1024: 2048 + (ph % 2) * 1024 + 1024],
                           FB[:, 6144 + (ph % 2) * 1024: 6144 + (ph % 2) * 1024 + 1024])
                qv = ve(qzd)
                if ph + 2 < 32:
                    def ldn(gpsimd, wrap=wraps[ph + 2], qv=qv):
                        gpsimd.wait_ge(s_ve.sem, qv)
                        wrap(gpsimd)
                    em("gpsimd", ldn)

                def trd(tensor, ph=ph, qv=qv, eprev=ev_se.get(ph - 2)):
                    tensor.wait_ge(s_ve.sem, qv)
                    if eprev is not None:
                        tensor.wait_ge(s_se.sem, eprev)
                    bA, bB = ((p4, p5), (p6, p7))[ph % 2]
                    for j in range(8):
                        bk = bA if j < 4 else bB
                        mm = tensor.transpose(
                            bk[:, (j % 4) * 128:(j % 4) * 128 + 128],
                            FB[:, 6144 + (ph % 2) * 1024 + j * 128:
                               6144 + (ph % 2) * 1024 + j * 128 + 128],
                            identf[:])
                        if j == 7:
                            mm.then_inc(s_pe.sem, 1)
                em("tensor", trd)
                tr_pe[ph] = s_pe.inc()

                def evd(scalar, ph=ph, ds=ds, iq=iq, pe_t=tr_pe[ph]):
                    scalar.wait_ge(s_pe.sem, pe_t)
                    bA, bB = ((p4, p5), (p6, p7))[ph % 2]
                    for g in range(2):
                        oc0 = iq * 8 + g * 4
                        base = A2 if oc0 < 16 else A1
                        b0 = (oc0 if oc0 < 16 else oc0 - 16) * 1024
                        scalar.activation(
                            out=base[:, b0:b0 + 4096].rearrange(
                                "p (s e) -> p s e", s=4)[:, :, ds * 128:(ds + 1) * 128],
                            in_=(bA, bB)[g][:, 0:512].rearrange(
                                "p (s e) -> p s e", s=4),
                            func=AF.Identity, scale=1.0)
                        scalar.drain()
                ev_se[ph] = se(evd)
            carry_tr = {0: tr_pe[30], 1: tr_pe[31]}
            carry_ev = {0: ev_se[30], 1: ev_se[31]}

            wd_toks = []
            for oc in range(32):
                def mk_st(scalar, oc=oc):
                    base = A2 if oc < 16 else A1
                    b0 = (oc if oc < 16 else oc - 16) * 1024
                    return scalar.dma_start(out=wdT_hbm[oc, :, :],
                                            in_=base[:, b0:b0 + 1024])
                wrap2, tok2 = stream_dma(mk_st, st_stream)
                wd_toks.append(tok2)

                def std(scalar, wrap2=wrap2):
                    wrap2(scalar)
                em("scalar", std)
            wd_last = [wd_toks[30], wd_toks[31]]

            wgwu_last = None
            prev_qv = None
            for wi, (whbm, thbm) in enumerate(((wg_h, wgT_hbm), (wu_h, wuT_hbm))):
                emit_scale(wi)
                wraps, toks = [], []
                for pc in range(32):
                    def mk_ld(gpsimd, whbm=whbm, pc=pc):
                        return gpsimd.dma_start(
                            out=FB[:, 4096 + (pc % 2) * 1024: 4096 + (pc % 2) * 1024 + 1024],
                            in_=whbm[pc * 128:(pc + 1) * 128, :])
                    wrap, tok = stream_dma(mk_ld, gd_stream)
                    wraps.append(wrap)
                    toks.append(tok)

                def kickg(gpsimd, w0=wraps[0], w1=wraps[1], prev_qv=prev_qv):
                    if prev_qv is not None:
                        gpsimd.wait_ge(s_ve.sem, prev_qv)
                    w0(gpsimd)
                    w1(gpsimd)
                em("gpsimd", kickg)

                tr_pe, ev_se, st_toks = {-2: carry_tr[0], -1: carry_tr[1]}, \
                    {-2: carry_ev[0], -1: carry_ev[1]}, {}
                for pc in range(32):
                    def qz(vector, pc=pc, tok=toks[pc], tprev=tr_pe.get(pc - 2)):
                        swait(vector, gd_stream, tok)
                        vector.wait_ge(s_pe.sem, tprev)
                        quant3(vector,
                               FB[:, 4096 + (pc % 2) * 1024: 4096 + (pc % 2) * 1024 + 1024],
                               FB[:, 6144 + (pc % 2) * 1024: 6144 + (pc % 2) * 1024 + 1024])
                    qv = ve(qz)
                    last_qv = qv
                    if pc + 2 < 32:
                        def ldn(gpsimd, wrap=wraps[pc + 2], qv=qv):
                            gpsimd.wait_ge(s_ve.sem, qv)
                            wrap(gpsimd)
                        em("gpsimd", ldn)

                    def trq(tensor, pc=pc, qv=qv, eprev=ev_se.get(pc - 2)):
                        tensor.wait_ge(s_ve.sem, qv)
                        tensor.wait_ge(s_se.sem, eprev)
                        bA, bB = ((p4, p5), (p6, p7))[pc % 2]
                        for j in range(8):
                            bk = bA if j < 4 else bB
                            mm = tensor.transpose(
                                bk[:, (j % 4) * 128:(j % 4) * 128 + 128],
                                FB[:, 6144 + (pc % 2) * 1024 + j * 128:
                                   6144 + (pc % 2) * 1024 + j * 128 + 128],
                                identf[:])
                            if j == 7:
                                mm.then_inc(s_pe.sem, 1)
                    em("tensor", trq)
                    tr_pe[pc] = s_pe.inc()

                    def evq(scalar, pc=pc, pe_t=tr_pe[pc],
                            sprev=st_toks.get(pc - 4)):
                        scalar.wait_ge(s_pe.sem, pe_t)
                        if sprev is not None:
                            swait(scalar, st_stream, sprev)
                        bA, bB = ((p4, p5), (p6, p7))[pc % 2]
                        slot = 8192 + (pc % 4) * 1024
                        scalar.activation(out=A3[:, slot:slot + 512],
                                          in_=bA[:, 0:512],
                                          func=AF.Identity, scale=1.0)
                        scalar.drain()
                        scalar.activation(out=A3[:, slot + 512:slot + 1024],
                                          in_=bB[:, 0:512],
                                          func=AF.Identity, scale=1.0)
                    ev_se[pc] = se(evq)

                    def mk_st(scalar, thbm=thbm, pc=pc):
                        slot = 8192 + (pc % 4) * 1024
                        return scalar.dma_start(out=thbm[pc, :, :],
                                                in_=A3[:, slot:slot + 1024])
                    wrap2, tok2 = stream_dma(mk_st, st_stream)
                    st_toks[pc] = tok2

                    def stq(scalar, wrap2=wrap2):
                        wrap2(scalar)
                    em("scalar", stq)
                carry_tr = {0: tr_pe[30], 1: tr_pe[31]}
                carry_ev = {0: ev_se[30], 1: ev_se[31]}
                wgwu_last = [st_toks[30], st_toks[31]]
                prev_qv = last_qv
            psum_free_se = s_se.v
            a3_free_pe = carry_tr[1]

            ck_prev = [None, None]
            for sl in range(8):
                def mk_ld(gpsimd, sl=sl):
                    return gpsimd.dma_start(
                        out=A3[:, (sl % 2) * 3072:(sl % 2) * 3072 + 3072],
                        in_=wkqv_h[sl * 128:(sl + 1) * 128, :])
                wrap, tok = stream_dma(mk_ld, gd_stream)

                def ldk(gpsimd, wrap=wrap, sl=sl, pv=ck_prev[sl % 2],
                        a3_free_pe=a3_free_pe):
                    if sl < 2:
                        gpsimd.wait_ge(s_pe.sem, a3_free_pe)
                    if pv is not None:
                        swait(gpsimd, st_stream, pv)
                    wrap(gpsimd)
                em("gpsimd", ldk)

                def mk_st(scalar, sl=sl):
                    return scalar.dma_start(out=kqvb_hbm[sl * 128:(sl + 1) * 128, :],
                                            in_=A3[:, (sl % 2) * 3072:(sl % 2) * 3072 + 3072])
                wrap2, tok2 = stream_dma(mk_st, st_stream)

                def stk(scalar, wrap2=wrap2, tok=tok):
                    swait(scalar, gd_stream, tok)
                    wrap2(scalar)
                em("scalar", stk)
                ck_prev[sl % 2] = tok2
            kqvb_tok = tok2
            return means_done, kqvb_tok, wd_last, wgwu_last, psum_free_se

        def ffn_block(src_resid, dst_resid, means_done, entry_toks,
                      entry=None, wgwu_toks=(), wd_toks=(), psum_free_se=None):
            if entry is None:
                entry = (s_ve.v, s_se.v)
            if psum_free_se is None:
                psum_free_se = s_se.v
            srow_last = None
            for th in range(2):
                _, vq = ln_half2(src_resid, th,
                                 lambda tc8, th=th: A0[:, (th * 8 + tc8) * 1024:(th * 8 + tc8) * 1024 + 1024],
                                 quant=True, entry=entry, entry_toks=entry_toks)

                def strow(tensor, vq=vq, psum_free_se=psum_free_se):
                    tensor.wait_ge(s_ve.sem, vq)
                    tensor.wait_ge(s_se.sem, psum_free_se)
                    tensor.transpose(p7[0:8, 0:128], stat[:, 80:88], identf[:]
                                     ).then_inc(s_pe.sem, 1)
                em("tensor", strow)
                pe1 = s_pe.inc()

                def srow_ev(vector, pe1=pe1):
                    vector.wait_ge(s_pe.sem, pe1)
                    vector.tensor_copy(out=rtmp[0:8, 0:128], in_=p7[0:8, 0:128])
                srow_cp = ve(srow_ev)

                def mk_srd1(sync):
                    return sync.dma_start(out=row_hbm[0:8, :], in_=rtmp[0:8, 0:128])
                srw1, srt1 = stream_dma(mk_srd1, ld_stream)

                def mk_srdma(sync, th=th):
                    return sync.dma_start(
                        out=ROWF[0:1, th * 1024: th * 1024 + 1024],
                        in_=row_hbm[0:8, :])
                srwrap, srtok = stream_dma(mk_srdma, ld_stream)

                def srdma(sync, srw1=srw1, srwrap=srwrap, srow_cp=srow_cp, srt1=srt1):
                    sync.wait_ge(s_ve.sem, srow_cp)
                    srw1(sync)
                    swait(sync, ld_stream, srt1)
                    srwrap(sync)
                em("sync", srdma)
                srow_last = srtok
            xq_ve = s_ve.v

            def tr_gate(sync, v=xq_ve, wd_toks=list(wd_toks)):
                sync.wait_ge(s_ve.sem, v)
                for t_ in wd_toks:
                    swait(sync, st_stream, t_)
            xqT_done = transpose16(lambda tc: A0[:, tc * 1024:(tc + 1) * 1024], A1,
                                   gates=[tr_gate])

            groups = []
            for tt in range(4):
                pieces = [(onesff[0:1, 0:128], ROWF[0:1, tt * 512:(tt + 1) * 512])]
                groups.append((pieces, copy_evac(FB[:, tt * 512:(tt + 1) * 512])))
            bcast0_done = mm_phase(groups, [p4, p5],
                                   pe_waits=[(ld_stream.sems[srow_last[0]].sem, srow_last[1]),
                                             (s_se.sem, psum_free_se)])

            def macc0(vector, bcast0_done=bcast0_done, wgwu_toks=list(wgwu_toks)):
                vector.wait_ge(s_ve.sem, bcast0_done)
                for t_ in wgwu_toks:
                    swait(vector, st_stream, t_)
                vector.memset(FB[:, 4096:6144], 0.0)
            macc_z = ve(macc0)

            peA = s_pe.v
            guev_last = {}
            gst_toks = {}
            for oc in range(32):
                bsl = (oc % 2) * 1024
                prev_pe = peA + (oc - 1) * 8

                def mk_ldg(sync, oc=oc, bsl=bsl):
                    return sync.dma_start(out=WKB[:, bsl:bsl + 1024],
                                          in_=wgT_hbm[oc, :, :])
                wrapg, tokg = stream_dma(mk_ldg, ld_stream)

                def mk_ldu(sync, oc=oc, bsl=bsl):
                    return sync.dma_start(out=A3[:, 14336 + bsl:14336 + bsl + 1024],
                                          in_=wuT_hbm[oc, :, :])
                wrapu, toku = stream_dma(mk_ldu, ld_stream)

                def ldw(sync, wrapg=wrapg, wrapu=wrapu, oc=oc, prev_pe=prev_pe,
                        xqT_done=xqT_done, macc_z=macc_z, wgwu_toks=list(wgwu_toks)):
                    if oc == 0:
                        sync.wait_ge(s_tr.sem, xqT_done)
                        sync.wait_ge(s_ve.sem, macc_z)
                        for t in wgwu_toks:
                            swait(sync, st_stream, t)
                    if oc >= 2:
                        sync.wait_ge(s_pe.sem, prev_pe)
                    wrapg(sync)
                    wrapu(sync)
                em("sync", ldw)

                def gumm(tensor, oc=oc, bsl=bsl, tokg=tokg, toku=toku, peA=peA):
                    swait(tensor, ld_stream, tokg)
                    swait(tensor, ld_stream, toku)
                    for tt in range(4):
                        gidx = oc * 4 + tt
                        if gidx >= 2:
                            tensor.wait_ge(s_ve.sem, guev_last[gidx - 2])
                        pg = (p0, p2)[tt % 2]
                        pu = (p1, p3)[tt % 2]
                        for c in range(8):
                            mm = tensor.matmul(pg[:, 0:512],
                                               WKB[:, bsl + c * 128: bsl + c * 128 + 128],
                                               A1[:, c * 2048 + tt * 512: c * 2048 + tt * 512 + 512],
                                               start=(c == 0), stop=(c == 7))
                            if c == 7:
                                mm.then_inc(s_pe.sem, 1)
                        for c in range(8):
                            mm = tensor.matmul(pu[:, 0:512],
                                               A3[:, 14336 + bsl + c * 128: 14336 + bsl + c * 128 + 128],
                                               A1[:, c * 2048 + tt * 512: c * 2048 + tt * 512 + 512],
                                               start=(c == 0), stop=(c == 7))
                            if c == 7:
                                mm.then_inc(s_pe.sem, 1)
                em("tensor", gumm)
                s_pe.v += 8

                for tt in range(4):
                    gidx = oc * 4 + tt
                    pe_u = peA + oc * 8 + tt * 2 + 2

                    def guev1(vector, tt=tt, pe_u=pe_u):
                        pg = (p0, p2)[tt % 2]
                        vector.wait_ge(s_pe.sem, pe_u - 1)
                        vector.tensor_tensor(
                            out=FB[:, 6144 + (tt % 2) * 512: 6144 + (tt % 2) * 512 + 512],
                            in0=pg[:, 0:512],
                            in1=FB[:, tt * 512:(tt + 1) * 512], op=ALU.mult)
                    v1 = ve(guev1)

                    def gusil(scalar, tt=tt, v1=v1):
                        scalar.wait_ge(s_ve.sem, v1)
                        scalar.activation(
                            out=FB[:, 7168 + (tt % 2) * 512: 7168 + (tt % 2) * 512 + 512],
                            in_=FB[:, 6144 + (tt % 2) * 512: 6144 + (tt % 2) * 512 + 512],
                            func=AF.Silu, scale=stat[:, 17:18])
                    sl1 = se(gusil)

                    gprev = gst_toks.get(gidx - 2)

                    def guev2(vector, tt=tt, sl1=sl1, pe_u=pe_u, gprev=gprev):
                        pu = (p1, p3)[tt % 2]
                        vector.wait_ge(s_se.sem, sl1)
                        vector.wait_ge(s_pe.sem, pe_u)
                        if gprev is not None:
                            swait(vector, st_stream, gprev)
                        vector.tensor_tensor(
                            out=FB[:, 6144 + (tt % 2) * 512: 6144 + (tt % 2) * 512 + 512],
                            in0=pu[:, 0:512],
                            in1=FB[:, tt * 512:(tt + 1) * 512], op=ALU.mult)
                        vector.drain()
                        vector.tensor_tensor(
                            out=FB[:, 2048 + (tt % 2) * 512: 2048 + (tt % 2) * 512 + 512],
                            in0=FB[:, 7168 + (tt % 2) * 512: 7168 + (tt % 2) * 512 + 512],
                            in1=FB[:, 6144 + (tt % 2) * 512: 6144 + (tt % 2) * 512 + 512],
                            op=ALU.mult)
                    v2a = ve(guev2)

                    def gusq(scalar, tt=tt, v2a=v2a):
                        scalar.wait_ge(s_ve.sem, v2a)
                        scalar.activation(
                            out=FB[:, 7168 + (tt % 2) * 512: 7168 + (tt % 2) * 512 + 512],
                            in_=FB[:, 2048 + (tt % 2) * 512: 2048 + (tt % 2) * 512 + 512],
                            func=AF.Square)
                    sq1 = se(gusq)

                    def guevm(vector, tt=tt, sq1=sq1):
                        vector.wait_ge(s_se.sem, sq1)
                        vector.tensor_tensor(
                            out=FB[:, 4096 + tt * 512: 4096 + tt * 512 + 512],
                            in0=FB[:, 4096 + tt * 512: 4096 + tt * 512 + 512],
                            in1=FB[:, 7168 + (tt % 2) * 512: 7168 + (tt % 2) * 512 + 512],
                            op=ALU.max)
                    v2 = ve(guevm)
                    guev_last[gidx] = v2

                    def mk_stg(scalar, oc=oc, tt=tt):
                        return scalar.dma_start(
                            out=gu_hbm[oc, :, tt * 512:(tt + 1) * 512],
                            in_=FB[:, 2048 + (tt % 2) * 512: 2048 + (tt % 2) * 512 + 512])
                    wraps, toks = stream_dma(mk_stg, st_stream)
                    gst_toks[gidx] = toks

                    def gust(scalar, wraps=wraps, v2=v2):
                        scalar.wait_ge(s_ve.sem, v2)
                        wraps(scalar)
                    em("scalar", gust)
            phaseA_ve = s_ve.v
            gu_last_toks = [gst_toks[126], gst_toks[127]]

            peT = s_pe.v
            s2c_base = s_ve.v

            def maccT(tensor, phaseA_ve=phaseA_ve, s2c_base=s2c_base):
                tensor.wait_ge(s_ve.sem, phaseA_ve)
                for j in range(16):
                    if j >= 2:
                        tensor.wait_ge(s_ve.sem, s2c_base + j - 1)
                    tensor.transpose((p4, p5)[j % 2][:, 0:128],
                                     FB[:, 4096 + j * 128: 4096 + j * 128 + 128],
                                     identf[:]).then_inc(s_pe.sem, 1)
            em("tensor", maccT)
            s_pe.v += 16

            def s2cols(vector, peT=peT):
                for j in range(16):
                    vector.wait_ge(s_pe.sem, peT + j + 1)
                    vector.reduce_max(out=stat[:, 64 + j:65 + j],
                                      in_=(p4, p5)[j % 2][:, 0:128], axis=AX.X)
                    vector.drain()
                    vector.nop().then_inc(s_ve.sem, 1)
            em("vector", s2cols)
            s_ve.v += 16
            s2c_done = s_ve.v

            def s2sqrt(scalar, s2c_done=s2c_done):
                scalar.wait_ge(s_ve.sem, s2c_done)
                scalar.activation(out=stat[:, 64:80], in_=stat[:, 64:80], func=AF.Sqrt)
            s2sq = se(s2sqrt)

            def s2fin(vector, s2sq=s2sq):
                vector.wait_ge(s_se.sem, s2sq)
                vector.tensor_scalar(out=stat[:, 64:80], in0=stat[:, 64:80],
                                     scalar1=1e-5, scalar2=None, op0=ALU.max)
                vector.drain()
                vector.tensor_scalar(out=stat[:, 32:48], in0=stat[:, 64:80],
                                     scalar1=stat[:, 18:19], scalar2=None, op0=ALU.mult)
                vector.drain()
                vector.tensor_scalar(out=stat[:, 32:48], in0=stat[:, 32:48],
                                     scalar1=stat[:, 19:20], scalar2=1.0 / 127.0,
                                     op0=ALU.mult, op1=ALU.mult)
                vector.drain()
                vector.reciprocal(out=stat[:, 64:80], in_=stat[:, 64:80])
                vector.drain()
                vector.tensor_scalar(out=stat[:, 64:80], in0=stat[:, 64:80],
                                     scalar1=127.0, scalar2=None, op0=ALU.mult)
            s2_done = ve(s2fin)

            def s2row_t(tensor, s2_done=s2_done):
                tensor.wait_ge(s_ve.sem, s2_done)
                tensor.transpose(p7[0:16, 0:128], stat[:, 64:80], identf[:]
                                 ).then_inc(s_pe.sem, 1)
            em("tensor", s2row_t)
            peR = s_pe.inc()

            def s2row_ev(vector, peR=peR):
                vector.wait_ge(s_pe.sem, peR)
                vector.tensor_copy(out=rtmp[0:16, 0:128], in_=p7[0:16, 0:128])
            s2row_cp = ve(s2row_ev)

            def mk_s2d1(sync):
                return sync.dma_start(out=row_hbm[:, :], in_=rtmp[0:16, 0:128])
            s2w1, s2t1 = stream_dma(mk_s2d1, ld_stream)

            def mk_s2dma(sync):
                return sync.dma_start(
                    out=ROWF[0:1, 0:2048],
                    in_=row_hbm[:, :])
            s2wrap, s2tok = stream_dma(mk_s2dma, ld_stream)

            def s2dma(sync, s2w1=s2w1, s2wrap=s2wrap, s2row_cp=s2row_cp, s2t1=s2t1):
                sync.wait_ge(s_ve.sem, s2row_cp)
                s2w1(sync)
                swait(sync, ld_stream, s2t1)
                s2wrap(sync)
            em("sync", s2dma)

            groups = []
            for tt in range(4):
                pieces = [(onesff[0:1, 0:128], ROWF[0:1, tt * 512:(tt + 1) * 512])]
                groups.append((pieces, copy_evac(FB[:, tt * 512:(tt + 1) * 512])))
            bcast2_done = mm_phase(groups, [p4, p5],
                                   pe_waits=[(ld_stream.sems[s2tok[0]].sem, s2tok[1])])

            res_toks = []
            dev_last_tt = None
            dpe_prev = None
            qv_all = {}
            dpe_all = {}
            for tt in range(4):
                dpe = {}
                for oc in range(32):
                    def mk_ldgu(sync, oc=oc, tt=tt):
                        return sync.dma_start(
                            out=FB[:, 2048 + (oc % 4) * 512: 2048 + (oc % 4) * 512 + 512],
                            in_=gu_hbm[oc, :, tt * 512:(tt + 1) * 512])
                    wrap, tok = stream_dma(mk_ldgu, ld_stream)

                    def ldgu(sync, wrap=wrap, oc=oc, tt=tt,
                             pv=qv_all.get(tt * 32 + oc - 4),
                             bcast2_done=bcast2_done, gu_last_toks=gu_last_toks):
                        if oc == 0 and tt == 0:
                            for t_ in gu_last_toks:
                                swait(sync, st_stream, t_)
                            sync.wait_ge(s_ve.sem, bcast2_done)
                        if pv is not None:
                            sync.wait_ge(s_ve.sem, pv)
                        wrap(sync)
                    em("sync", ldgu)

                    def mk_ldd(gpsimd, oc=oc):
                        return gpsimd.dma_start(
                            out=WKB[:, (oc % 2) * 1024:(oc % 2) * 1024 + 1024],
                            in_=wdT_hbm[oc, :, :])
                    wrapd, tokd = stream_dma(mk_ldd, gd_stream)

                    def ldd(gpsimd, wrapd=wrapd, oc=oc, tt=tt,
                            pv=(dpe_all.get(tt * 32 + oc - 2)), bcast2_done=bcast2_done):
                        if oc < 2 and tt == 0:
                            gpsimd.wait_ge(s_ve.sem, bcast2_done)
                        if pv is not None:
                            gpsimd.wait_ge(s_pe.sem, pv)
                        wrapd(gpsimd)
                    em("gpsimd", ldd)

                    def qgu(vector, oc=oc, tt=tt, tok=tok,
                            afree=(dpe_prev[oc] if dpe_prev is not None else None)):
                        swait(vector, ld_stream, tok)
                        if afree is not None:
                            vector.wait_ge(s_pe.sem, afree)
                        src = FB[:, 2048 + (oc % 4) * 512: 2048 + (oc % 4) * 512 + 512]
                        vector.tensor_tensor(out=FB[:, 6144:6656], in0=src,
                                             in1=FB[:, tt * 512:(tt + 1) * 512],
                                             op=ALU.mult)
                        vector.drain()
                        vector.tensor_scalar(out=qi32[:, 0:512], in0=FB[:, 6144:6656],
                                             scalar1=128.5, scalar2=None, op0=ALU.add)
                        vector.drain()
                        vector.tensor_scalar(out=A0[:, oc * 512:(oc + 1) * 512],
                                             in0=qi32[:, 0:512],
                                             scalar1=-128.0, scalar2=None, op0=ALU.add)
                    qv = ve(qgu)
                    qv_all[tt * 32 + oc] = qv

                    def dmm(tensor, oc=oc, tokd=tokd, qv=qv, dprev=dev_last_tt):
                        swait(tensor, gd_stream, tokd)
                        tensor.wait_ge(s_ve.sem, qv)
                        if oc == 0 and dprev is not None:
                            tensor.wait_ge(s_ve.sem, dprev)
                        for tc2 in range(4):
                            for hh in range(2):
                                pp = PSUMS[tc2 * 2 + hh]
                                mm = tensor.matmul(
                                    pp[:, 0:512],
                                    A0[:, oc * 512 + tc2 * 128: oc * 512 + tc2 * 128 + 128],
                                    WKB[:, (oc % 2) * 1024 + hh * 512:(oc % 2) * 1024 + hh * 512 + 512],
                                    start=(oc == 0), stop=(oc == 31))
                                if tc2 == 3 and hh == 1:
                                    mm.then_inc(s_pe.sem, 1)
                    em("tensor", dmm)
                    dpe[oc] = s_pe.inc()
                    dpe_all[tt * 32 + oc] = dpe[oc]
                down_pe = s_pe.v
                dpe_prev = dpe

                for tc2 in range(4):
                    tc = tt * 4 + tc2

                    def mk_ldr(sync, tc=tc, tc2=tc2):
                        return sync.dma_start(
                            out=FB[:, 4096 + (tc2 % 2) * 1024: 4096 + (tc2 % 2) * 1024 + 1024],
                            in_=src_resid[tc * 128:(tc + 1) * 128, :])
                    wrap, tok = stream_dma(mk_ldr, ld_stream)
                    prev_res_tok = res_toks[-2][1] if len(res_toks) >= 2 else None

                    def ldr(sync, wrap=wrap, down_pe=down_pe, prev=prev_res_tok):
                        sync.wait_ge(s_pe.sem, down_pe)
                        if prev is not None:
                            swait(sync, st_stream, prev)
                        wrap(sync)
                    em("sync", ldr)

                    def dev(vector, tc2=tc2, tt=tt, tok=tok, down_pe=down_pe):
                        swait(vector, ld_stream, tok)
                        vector.wait_ge(s_pe.sem, down_pe)
                        for hh in range(2):
                            pp = PSUMS[tc2 * 2 + hh]
                            sl = slice(4096 + (tc2 % 2) * 1024 + hh * 512,
                                       4096 + (tc2 % 2) * 1024 + hh * 512 + 512)
                            vector.scalar_tensor_tensor(
                                out=FB[:, sl], in0=pp[:, 0:512],
                                scalar=stat[:, 32 + tt * 4 + tc2:33 + tt * 4 + tc2],
                                in1=FB[:, sl], op0=ALU.mult, op1=ALU.add)
                            vector.drain()
                        vector.nop().then_inc(s_ve.sem, 1)
                    em("vector", dev)
                    dv = s_ve.inc()

                    def mk_str(scalar, tc=tc, tc2=tc2):
                        return scalar.dma_start(
                            out=dst_resid[tc * 128:(tc + 1) * 128, :],
                            in_=FB[:, 4096 + (tc2 % 2) * 1024: 4096 + (tc2 % 2) * 1024 + 1024])
                    wrap2, tok2 = stream_dma(mk_str, st_stream)
                    res_toks.append((tc, tok2))

                    def strx(scalar, wrap2=wrap2, dv=dv):
                        scalar.wait_ge(s_ve.sem, dv)
                        wrap2(scalar)
                    em("scalar", strx)
                dev_last_tt = s_ve.v
            return res_toks

        def block3(kqvb_tok, b2_res_toks):
            phase_gate([t for _, t in b2_res_toks])
            def mk_liq(sync):
                return sync.dma_start(out=FB[0:1, 0:4096], in_=iotaq_h[:, :])
            wrap, liq_tok = stream_dma(mk_liq, ld_stream)

            def ldiq(sync, wrap=wrap):
                wrap(sync)
            em("sync", ldiq)

            def mvals(vector, liq_tok=liq_tok):
                swait(vector, ld_stream, liq_tok)
                for hh in range(2):
                    vector.tensor_copy(out=stat[0:1, 20 + hh:21 + hh],
                                       in_=mrow[0:1, 24 + hh:25 + hh])
                    vector.drain()
                    vector.tensor_scalar(out=stat[0:1, 22 + hh:23 + hh],
                                         in0=stat[0:1, 20 + hh:21 + hh],
                                         scalar1=0.0, scalar2=-32.0,
                                         op0=ALU.max, op1=ALU.mult)
                    vector.drain()
                    vector.tensor_scalar(out=ROWB[0:1, hh * 4096:(hh + 1) * 4096],
                                         in0=FB[0:1, 0:4096],
                                         scalar1=stat[0:1, 22 + hh:23 + hh],
                                         scalar2=None, op0=ALU.mult)
                    vector.drain()
            mv_done = ve(mvals)

            mjc_done = mv_done
            for hh in range(2):
                def mbc(tensor, hh=hh, mv_done=mv_done):
                    tensor.wait_ge(s_ve.sem, mv_done)
                    tensor.matmul(p1[:, hh:hh + 1], onesff[0:1, 0:128],
                                  stat[0:1, 20 + hh:21 + hh], start=True, stop=True
                                  ).then_inc(s_pe.sem, 1)
                em("tensor", mbc)
                pe_b = s_pe.inc()

                def mje(vector, hh=hh, pe_b=pe_b):
                    vector.wait_ge(s_pe.sem, pe_b)
                    vector.tensor_copy(out=stat[:, 24 + hh:25 + hh], in_=p1[:, hh:hh + 1])
                    vector.drain()
                    vector.tensor_scalar(out=mjc[:, hh * 32:(hh + 1) * 32], in0=iotak[:],
                                         scalar1=stat[:, 24 + hh:25 + hh],
                                         scalar2=None, op0=ALU.mult)
                mjc_done = ve(mje)

            for th in range(2):
                lnse, _ = ln_half2(
                    resid_h, th,
                    lambda tc8, th=th: A0[:, (th * 8 + tc8) * 1024:(th * 8 + tc8) * 1024 + 1024])
            ln3_se = s_se.v
            ln3T_done = transpose16(lambda tc: A0[:, tc * 1024:(tc + 1) * 1024], A3,
                                    gates=[lambda sync, v=ln3_se: sync.wait_ge(s_se.sem, v)])

            peALL = None

            def kq_part2(colbase, dstbuf, first, part_off, mjc_done=mjc_done):
                nonlocal peALL
                if peALL is None:
                    peALL = s_pe.v
                peK0 = peALL - part_off
                for tt in range(4):
                    peK = s_pe.v
                    for c in range(8):
                        def mk_ldc(sync, c=c, colbase=colbase):
                            return sync.dma_start(
                                out=WKB[:, (c % 2) * 1024:(c % 2) * 1024 + 1024],
                                in_=kqvb_hbm[c * 128:(c + 1) * 128, colbase:colbase + 1024])
                        wrap, tok = stream_dma(mk_ldc, ld_stream)

                        def ldc(sync, wrap=wrap, c=c, tt=tt, first=first,
                                peALLv=peALL, part_off=part_off,
                                ln3T_done=ln3T_done, kqvb_tok=kqvb_tok):
                            if c == 0 and tt == 0 and first:
                                sync.wait_ge(s_tr.sem, ln3T_done)
                                swait(sync, st_stream, kqvb_tok)
                            gi = part_off + tt * 8 + c
                            if gi >= 2:
                                sync.wait_ge(s_pe.sem, peALLv + gi - 1)
                            wrap(sync)
                        em("sync", ldc)

                        def kmm(tensor, c=c, tt=tt, tok=tok, first=first,
                                mjc_done=mjc_done):
                            if c == 0 and tt == 0 and first:
                                tensor.wait_ge(s_ve.sem, mjc_done)
                            swait(tensor, ld_stream, tok)
                            for slot in range(8):
                                pp = PSUMS[slot]
                                mm = tensor.matmul(
                                    pp[:, 0:512],
                                    WKB[:, (c % 2) * 1024 + slot * 128:(c % 2) * 1024 + slot * 128 + 128],
                                    A3[:, c * 2048 + tt * 512: c * 2048 + tt * 512 + 512],
                                    start=(c == 0), stop=(c == 7))
                                if slot == 7:
                                    mm.then_inc(s_pe.sem, 1)
                        em("tensor", kmm)
                        s_pe.inc()

                    def kev(vector, tt=tt, dstbuf=dstbuf, peK=peK):
                        vector.wait_ge(s_pe.sem, peK + 8)
                        for slot in range(8):
                            pp = PSUMS[slot]
                            vector.tensor_copy(
                                out=dstbuf[:, slot * 2048 + tt * 512: slot * 2048 + tt * 512 + 512],
                                in_=pp[:, 0:512])
                        vector.drain()
                        vector.nop().then_inc(s_ve.sem, 1)
                    em("vector", kev)
                    s_ve.inc()
                    kev_v = s_ve.v

                    def tt_guard(tensor, kev_v=kev_v):
                        tensor.wait_ge(s_ve.sem, kev_v)
                    em("tensor", tt_guard)

            kq_part2(0, A0, True, 0)
            kq_part2(D, A1, False, 32)
            for tcg in range(8):
                peV = s_pe.v
                for c in range(8):
                    def mk_ldc(sync, c=c):
                        return sync.dma_start(
                            out=WKB[:, (c % 2) * 1024:(c % 2) * 1024 + 1024],
                            in_=kqvb_hbm[c * 128:(c + 1) * 128, 2 * D:2 * D + 1024])
                    wrap, tok = stream_dma(mk_ldc, ld_stream)

                    def ldc(sync, wrap=wrap, c=c, tcg=tcg, peALLv=peALL):
                        gi = 64 + tcg * 8 + c
                        if gi >= 2:
                            sync.wait_ge(s_pe.sem, peALLv + gi - 1)
                        wrap(sync)
                    em("sync", ldc)

                    def vmm(tensor, c=c, tcg=tcg, tok=tok):
                        swait(tensor, ld_stream, tok)
                        for tc2 in range(2):
                            for vh in range(2):
                                pp = PSUMS[tc2 * 2 + vh]
                                tc = tcg * 2 + tc2
                                mm = tensor.matmul(
                                    pp[:, 0:512],
                                    A3[:, c * 2048 + tc * 128: c * 2048 + tc * 128 + 128],
                                    WKB[:, (c % 2) * 1024 + vh * 512:(c % 2) * 1024 + vh * 512 + 512],
                                    start=(c == 0), stop=(c == 7))
                                if tc2 == 1 and vh == 1:
                                    mm.then_inc(s_pe.sem, 1)
                    em("tensor", vmm)
                    s_pe.inc()

                def vev(vector, tcg=tcg, peV=peV):
                    vector.wait_ge(s_pe.sem, peV + 8)
                    for tc2 in range(2):
                        for vh in range(2):
                            pp = PSUMS[tc2 * 2 + vh]
                            tc = tcg * 2 + tc2
                            for sl4 in range(4):
                                slot = vh * 4 + sl4
                                vector.tensor_copy(
                                    out=A2[:, slot * 2048 + tc * 128: slot * 2048 + tc * 128 + 128],
                                    in_=pp[:, sl4 * 128:(sl4 + 1) * 128])
                    vector.drain()
                    vector.nop().then_inc(s_ve.sem, 1)
                em("vector", vev)
                s_ve.inc()
                vev_v = s_ve.v

                def v_guard(tensor, vev_v=vev_v):
                    tensor.wait_ge(s_ve.sem, vev_v)
                em("tensor", v_guard)
            kqv_ve = s_ve.v

            gd0 = s_gd.v
            cc_e2 = s_cc.v + 1

            def e2(gpsimd, kqv_ve=kqv_ve, gd0=gd0, cc_e2=cc_e2):
                gpsimd.wait_ge(s_ve.sem, kqv_ve)
                for slot in range(8):
                    gpsimd.dma_start(out=a2a_in[slot, :, 0:2048],
                                     in_=A0[:, slot * 2048:(slot + 1) * 2048]).then_inc(s_gd.sem, 16)
                    gpsimd.dma_start(out=a2a_in[slot, :, 2048:4096],
                                     in_=A1[:, slot * 2048:(slot + 1) * 2048]).then_inc(s_gd.sem, 16)
                    gpsimd.dma_start(out=a2a_in[slot, :, 4096:6144],
                                     in_=A2[:, slot * 2048:(slot + 1) * 2048]).then_inc(s_gd.sem, 16)
                gpsimd.wait_ge(s_gd.sem, gd0 + 24 * 16)
                with allow_ici_all_to_all():
                    gpsimd.collective_compute(
                        "AllToAll", ALU.bypass,
                        replica_groups=[list(range(8))],
                        ins=[a2a_in.ap().opt()], outs=[a2a_out.ap().opt()],
                    ).then_inc(s_cc.sem, 1)
                gpsimd.wait_ge(s_cc.sem, cc_e2)
                for i in range(8):
                    gpsimd.dma_start(out=A1[:, i * 2048:(i + 1) * 2048],
                                     in_=a2a_out[i, :, 0:2048]).then_inc(s_gd.sem, 16)
                    gpsimd.dma_start(out=A0[:, i * 2048:(i + 1) * 2048],
                                     in_=a2a_out[i, :, 2048:4096]).then_inc(s_gd.sem, 16)
                    gpsimd.dma_start(
                        out=A2[:].rearrange("p (k w) -> p k w", w=130)[:, i * 16:(i + 1) * 16, 0:64],
                        in_=a2a_out[i, :, 4096:6144].rearrange("p (k v) -> p k v", v=128)[:, :, 0:64]
                    ).then_inc(s_gd.sem, 16)
                    gpsimd.dma_start(
                        out=A2[:].rearrange("p (k w) -> p k w", w=130)[:, i * 16:(i + 1) * 16, 65:129],
                        in_=a2a_out[i, :, 4096:6144].rearrange("p (k v) -> p k v", v=128)[:, :, 64:128]
                    ).then_inc(s_gd.sem, 16)
            em("gpsimd", e2)
            s_gd.v += 24 * 16
            s_cc.v += 1
            s_gd.v += 32 * 16
            e2_recv = s_gd.v

            def vones(vector, e2_recv=e2_recv, mjc_done=mjc_done):
                vector.wait_ge(s_gd.sem, e2_recv)
                vector.wait_ge(s_ve.sem, mjc_done)
                vector.memset(A2[:].rearrange("p (k w) -> p k w", w=130)[:, :, 64:65], 1.0)
                vector.memset(A2[:].rearrange("p (k w) -> p k w", w=130)[:, :, 129:130], 1.0)
            vpad_done = ve(vones)

            sched = []
            exp_of = {}
            av_of = {}
            fin_ve_of = {}
            pe_v = s_pe.v
            se_v = s_se.v
            ve_v = s_ve.v
            ktg = 0
            tile = 0
            for beta in range(4):
                for qi in range(8):
                    nkt = 4 * qi + 4
                    for kt in range(nkt):
                        ping = ktg % 2
                        sched.append(dict(op="S", beta=beta, qi=qi, kt=kt, ktg=ktg,
                                          ping=ping,
                                          first=(kt == 0), last=(kt == nkt - 1)))
                        pe_v += 2
                        sgrp = (pe_v - 1, pe_v)
                        sched[-1]["sgrp"] = sgrp
                        sched.append(dict(op="E", ktg=ktg, ping=ping, beta=beta,
                                          qi=qi, kt=kt, sgrp=sgrp))
                        se_v += 2
                        exp_of[ktg] = (se_v - 1, se_v)
                        sched.append(dict(op="A", ktg=ktg, ping=ping, beta=beta,
                                          qi=qi, kt=kt, first=(kt == 0),
                                          last=(kt == nkt - 1), tile=tile))
                        pe_v += 2
                        av_of[ktg] = (pe_v - 1, pe_v)
                        ktg += 1
                    sched.append(dict(op="F", beta=beta, qi=qi, tile=tile,
                                      av=av_of[ktg - 1]))
                    ve_v += 1
                    fin_ve_of[tile] = ve_v
                    tile += 1

            pe_b3 = s_pe.v
            se_b3 = s_se.v
            ve_b3 = s_ve.v

            f_toks = {}
            for step in [s for s in sched if s["op"] == "F"]:
                t = step["tile"]
            def b3_t(tensor, sched=sched, vpad_done=vpad_done):
                for step in sched:
                    if step["op"] == "S":
                        beta, qi, kt, ktg = step["beta"], step["qi"], step["kt"], step["ktg"]
                        ping = step["ping"]
                        pS = ((p0, p1), (p4, p5))[ping]
                        if ktg == 0:
                            tensor.wait_ge(s_ve.sem, vpad_done)
                        if ktg >= 2:
                            tensor.wait_ge(s_se.sem, exp_of[ktg - 2][1])
                        kcol = beta * 4096 + kt * 128
                        qcol = beta * 4096 + qi * 512
                        diag = kt - 4 * qi
                        for hh in range(2):
                            hb = hh * 64
                            tensor.matmul(pS[hh][:, 0:512],
                                          A1[hb:hb + 64, kcol:kcol + 128],
                                          A0[hb:hb + 64, qcol:qcol + 512],
                                          start=True, stop=False)
                            mm = tensor.matmul(
                                pS[hh][:, 0:512], onesbf[0:1, 0:128],
                                ROWB[0:1, hh * 4096 + qi * 512: hh * 4096 + qi * 512 + 512],
                                start=False, stop=(diag < 0))
                            if diag < 0:
                                mm.then_inc(s_pe.sem, 1)
                            else:
                                tensor.matmul(pS[hh][:, 0:512], identb[:, 0:128],
                                              masksb[:, diag * 512:(diag + 1) * 512],
                                              start=False, stop=True).then_inc(s_pe.sem, 1)
                    elif step["op"] == "A":
                        ktg, ping = step["ktg"], step["ping"]
                        beta, kt, tile = step["beta"], step["kt"], step["tile"]
                        kc = beta * 32 + kt
                        if step["first"] and tile >= 1:
                            tensor.wait_ge(s_ve.sem, fin_ve_of[tile - 1])
                        for hh in range(2):
                            tensor.wait_ge(s_se.sem, exp_of[ktg][hh])
                            pO = (p2, p3)[hh]
                            tensor.matmul(pO[0:65, 0:512],
                                          A2[:, kc * 130 + hh * 65: kc * 130 + hh * 65 + 65],
                                          ptb[:, (ping * 2 + hh) * 512:(ping * 2 + hh) * 512 + 512],
                                          start=step["first"], stop=step["last"]
                                          ).then_inc(s_pe.sem, 1)
            em("tensor", b3_t)
            s_pe.v = pe_b3 + sum(4 for s in sched if s["op"] == "S")

            fstage_toks = []
            f_sd_tokens = {}
            for step in sched:
                if step["op"] == "E":
                    def b3_e(scalar, step=step):
                        ktg, ping, kt = step["ktg"], step["ping"], step["kt"]
                        for hh in range(2):
                            scalar.wait_ge(s_pe.sem, step["sgrp"][hh])
                            if ktg >= 2:
                                scalar.wait_ge(s_pe.sem, av_of[ktg - 2][hh])
                            pS = ((p0, p1), (p4, p5))[ping]
                            scalar.activation(
                                out=ptb[:, (ping * 2 + hh) * 512:(ping * 2 + hh) * 512 + 512],
                                in_=pS[hh][:, 0:512], func=AF.Exp,
                                scale=1.0 / 32.0,
                                bias=mjc[:, hh * 32 + kt:hh * 32 + kt + 1])
                            scalar.drain()
                            scalar.nop().then_inc(s_se.sem, 1)
                    em("scalar", b3_e)
                    s_se.inc(2)
                elif step["op"] == "F":
                    t = step["tile"]

                    def fev(vector, step=step, t=t):
                        vector.wait_ge(s_pe.sem, step["av"][1])
                        if t >= 2:
                            for hh2 in range(2):
                                swait(vector, st_stream, f_sd_tokens[(t - 2, hh2, "o")])
                                swait(vector, ld_stream, f_sd_tokens[(t - 2, hh2, "n")])
                        for hh in range(2):
                            pO = (p2, p3)[hh]
                            vector.tensor_copy(
                                out=FB[0:64, (t % 2) * 1024 + hh * 512:(t % 2) * 1024 + hh * 512 + 512],
                                in_=pO[0:64, 0:512])
                            vector.tensor_copy(
                                out=FB[64:65, 2048 + ((t % 2) * 2 + hh) * 512:
                                       2048 + ((t % 2) * 2 + hh) * 512 + 512],
                                in_=pO[64:65, 0:512])
                        vector.drain()
                        vector.nop().then_inc(s_ve.sem, 1)
                    em("vector", fev)
                    fv = s_ve.inc()
                    assert fv == fin_ve_of[t], (fv, fin_ve_of[t], t)

                    for hh in range(2):
                        def mk_sto(scalar, t=t, hh=hh):
                            return scalar.dma_start(
                                out=o_scr[2 * t + hh, :, :],
                                in_=FB[0:64, (t % 2) * 1024 + hh * 512:(t % 2) * 1024 + hh * 512 + 512])
                        wrap, tok = stream_dma(mk_sto, st_stream)
                        f_sd_tokens[(t, hh, "o")] = tok

                        def sto(scalar, wrap=wrap, fv=fv):
                            scalar.wait_ge(s_ve.sem, fv)
                            wrap(scalar)
                        em("scalar", sto)

                        def mk_stn(sync, t=t, hh=hh):
                            return sync.dma_start(
                                out=nrows[2 * t + hh:2 * t + hh + 1, :],
                                in_=FB[64:65, 2048 + ((t % 2) * 2 + hh) * 512:
                                       2048 + ((t % 2) * 2 + hh) * 512 + 512])
                        wrapn, tokn = stream_dma(mk_stn, ld_stream)
                        f_sd_tokens[(t, hh, "n")] = tokn

                        def stn(sync, wrapn=wrapn, fv=fv):
                            sync.wait_ge(s_ve.sem, fv)
                            wrapn(sync)
                        em("sync", stn)
                    fstage_toks.append(fv)
            attn_done_ve = s_ve.v
            last_o_toks = [f_sd_tokens[(31, 0, "o")], f_sd_tokens[(31, 1, "o")]]
            last_n_toks = [f_sd_tokens[(31, 0, "n")], f_sd_tokens[(31, 1, "n")]]

            def nrecip(vector, attn_done_ve=attn_done_ve, last_n_toks=last_n_toks):
                vector.wait_ge(s_ve.sem, attn_done_ve)
                for t_ in last_n_toks:
                    swait(vector, ld_stream, t_)
                vector.reciprocal(out=nrows[:, :], in_=nrows[:, :])
            nrec_done = ve(nrecip)

            norm_ve = {}
            for t64 in range(64):
                def mk_ntmp(sync, t64=t64):
                    return sync.dma_start(out=ROWF[0:1, 2048:2560],
                                          in_=nrows[t64:t64 + 1, :])
                wrapn, tokn = stream_dma(mk_ntmp, ld_stream)

                def ldn(sync, wrapn=wrapn, t64=t64, nrec_done=nrec_done):
                    if t64 == 0:
                        sync.wait_ge(s_ve.sem, nrec_done)
                    else:
                        sync.wait_ge(s_pe.sem, nb_pe[t64 - 1])
                    wrapn(sync)
                em("sync", ldn)

                def mk_ldo(sync, t64=t64):
                    return sync.dma_start(
                        out=FB[0:64, (t64 % 2) * 512:(t64 % 2) * 512 + 512],
                        in_=o_scr[t64, :, :])
                wrapo, toko = stream_dma(mk_ldo, ld_stream)

                def ldo(sync, wrapo=wrapo, t64=t64, last_o_toks=last_o_toks):
                    if t64 == 0:
                        for t_ in last_o_toks:
                            swait(sync, st_stream, t_)
                    if t64 >= 2:
                        sync.wait_ge(s_ve.sem, norm_ve[t64 - 2])
                    wrapo(sync)
                em("sync", ldo)

                nprev = norm_ve.get(t64 - 1) if t64 >= 1 else None

                def nb(tensor, tokn=tokn, t64=t64, nprev=nprev):
                    swait(tensor, ld_stream, tokn)
                    if nprev is not None:
                        tensor.wait_ge(s_ve.sem, nprev)
                    tensor.matmul(p6[0:64, 0:512], onesff[0:1, 0:64],
                                  ROWF[0:1, 2048:2560], start=True, stop=True
                                  ).then_inc(s_pe.sem, 1)
                em("tensor", nb)
                nb_pe_t = s_pe.inc()
                if t64 == 0:
                    nb_pe = {}
                nb_pe[t64] = nb_pe_t

                def nmul(vector, t64=t64, toko=toko, nb_pe_t=nb_pe_t):
                    swait(vector, ld_stream, toko)
                    vector.wait_ge(s_pe.sem, nb_pe_t)
                    qtglob = t64 // 2
                    hh = t64 % 2
                    beta, qi = qtglob // 8, qtglob % 8
                    slot = 2 * beta + (1 if qi >= 4 else 0)
                    col = slot * 2048 + (qi % 4) * 512
                    vector.tensor_tensor(out=A3[hh * 64:hh * 64 + 64, col:col + 512],
                                         in0=FB[0:64, (t64 % 2) * 512:(t64 % 2) * 512 + 512],
                                         in1=p6[0:64, 0:512], op=ALU.mult)
                norm_ve[t64] = ve(nmul)
            e3_stage_ve = s_ve.v

            gd1 = s_gd.v
            cc_e3 = s_cc.v + 1

            def e3(gpsimd, e3_stage_ve=e3_stage_ve, gd1=gd1, cc_e3=cc_e3):
                gpsimd.wait_ge(s_ve.sem, e3_stage_ve)
                for slot in range(8):
                    gpsimd.dma_start(out=oa_in[slot, :, :],
                                     in_=A3[:, slot * 2048:(slot + 1) * 2048]).then_inc(s_gd.sem, 16)
                gpsimd.wait_ge(s_gd.sem, gd1 + 8 * 16)
                with allow_ici_all_to_all():
                    gpsimd.collective_compute(
                        "AllToAll", ALU.bypass,
                        replica_groups=[list(range(8))],
                        ins=[oa_in.ap().opt()], outs=[oa_out.ap().opt()],
                    ).then_inc(s_cc.sem, 1)
                gpsimd.wait_ge(s_cc.sem, cc_e3)
                for i in range(8):
                    gpsimd.dma_start(out=A0[:, i * 2048:(i + 1) * 2048],
                                     in_=oa_out[i, :, :]).then_inc(s_gd.sem, 16)
            em("gpsimd", e3)
            s_gd.v += 8 * 16
            s_cc.v += 1
            s_gd.v += 8 * 16
            e3_recv = s_gd.v

            def e3t_gate(sync, e3_recv=e3_recv):
                sync.wait_ge(s_gd.sem, e3_recv)
            em("sync", e3t_gate)
            for si in range(8):
                def trs(sync, si=si):
                    sync.dma_start_transpose(
                        A1[:].rearrange("p (tc s f) -> p tc s f", s=8, f=128)[:, :, si, :],
                        A0[:, si * 2048:(si + 1) * 2048],
                    ).then_inc(s_tr.sem, 16)
                em("sync", trs)
                s_tr.inc(16)
            e3T_done = s_tr.v

            fin_toks = []
            for tc in range(16):
                def mk_ldr(sync, tc=tc):
                    return sync.dma_start(
                        out=FB[:, (tc % 2) * 1024:(tc % 2) * 1024 + 1024],
                        in_=resid_h[tc * 128:(tc + 1) * 128, :])
                wrap, tok = stream_dma(mk_ldr, ld_stream)
                prev = fin_toks[-2][1] if len(fin_toks) >= 2 else None

                def ldr(sync, wrap=wrap, e3T_done=e3T_done, prev=prev, tc=tc):
                    if tc == 0:
                        sync.wait_ge(s_tr.sem, e3T_done)
                    if prev is not None:
                        swait(sync, st_stream, prev)
                    wrap(sync)
                em("sync", ldr)

                def fadd(vector, tc=tc, tok=tok):
                    swait(vector, ld_stream, tok)
                    vector.tensor_tensor(out=FB[:, (tc % 2) * 1024:(tc % 2) * 1024 + 1024],
                                         in0=FB[:, (tc % 2) * 1024:(tc % 2) * 1024 + 1024],
                                         in1=A1[:, tc * 1024:(tc + 1) * 1024], op=ALU.add)
                fa = ve(fadd)

                def mk_str(scalar, tc=tc):
                    return scalar.dma_start(
                        out=resid2_h[tc * 128:(tc + 1) * 128, :],
                        in_=FB[:, (tc % 2) * 1024:(tc % 2) * 1024 + 1024])
                wrap2, tok2 = stream_dma(mk_str, st_stream)
                fin_toks.append((tc, tok2))

                def strx(scalar, wrap2=wrap2, fa=fa):
                    scalar.wait_ge(s_ve.sem, fa)
                    wrap2(scalar)
                em("scalar", strx)
            return fin_toks

        if n_blocks >= 2:
            b1_toks, p1fins = block1(mid_emit=ffn_wprep_pass1)
            mean_fin = ffn_wprep_pass1_fin(p1fins)
            entry_ffn1 = (s_ve.v, s_se.v)
            means_done, kqvb_tok, wd_last, wgwu_last, psum_free_se = \
                ffn_wprep_rest(mean_fin)
            b2_toks = ffn_block(resid_h, resid_h if n_blocks >= 3 else resid2_h,
                                means_done, [t for _, t in b1_toks][-2:],
                                entry=entry_ffn1, wgwu_toks=wgwu_last,
                                wd_toks=wd_last, psum_free_se=psum_free_se)
        else:
            b1_toks, _ = block1()
        if n_blocks >= 3:
            b3_toks = block3(kqvb_tok, b2_toks)
        if n_blocks >= 4:
            b4_toks = ffn_block(resid2_h, y_out, means_done,
                                [t for _, t in b3_toks][-2:])

        if n_blocks == 1:
            def yc(gpsimd, toks=b1_toks):
                for _, t in toks:
                    swait(gpsimd, st_stream, t)
                gpsimd.dma_start(out=y_out[:, :], in_=resid_h[:, :]).then_inc(s_out.sem, 16)
                gpsimd.wait_ge(s_out.sem, 16)
            em("gpsimd", yc)
            s_out.v += 16
        elif n_blocks == 2:
            def yc(gpsimd, toks=b2_toks):
                for _, t in toks:
                    swait(gpsimd, st_stream, t)
                gpsimd.dma_start(out=y_out[:, :], in_=resid2_h[:, :]).then_inc(s_out.sem, 16)
                gpsimd.wait_ge(s_out.sem, 16)
            em("gpsimd", yc)
            s_out.v += 16
        elif n_blocks == 3:
            def yc(gpsimd, toks=b3_toks):
                for _, t in toks:
                    swait(gpsimd, st_stream, t)
                gpsimd.dma_start(out=y_out[:, :], in_=resid2_h[:, :]).then_inc(s_out.sem, 16)
                gpsimd.wait_ge(s_out.sem, 16)
            em("gpsimd", yc)
            s_out.v += 16
        else:
            def yc(gpsimd, toks=b4_toks):
                for _, t in toks:
                    swait(gpsimd, st_stream, t)
            em("gpsimd", yc)

        with nc.Block() as block:
            @block.sync
            def _(sync):
                for fn in prog["sync"]:
                    fn(sync)

            @block.vector
            def _(vector):
                for fn in prog["vector"]:
                    fn(vector)

            @block.scalar
            def _(scalar):
                for fn in prog["scalar"]:
                    fn(scalar)

            @block.tensor
            def _(tensor):
                for fn in prog["tensor"]:
                    fn(tensor)

            @block.gpsimd
            def _(gpsimd):
                for fn in prog["gpsimd"]:
                    fn(gpsimd)

    return nc


def make_host_constants():
    ident = np.eye(128, dtype=ml_dtypes.bfloat16)
    identf = np.eye(128, dtype=np.float32)
    iota_k = (np.arange(32)[None, :] * 128 + np.arange(128)[:, None]).astype(np.float32)
    iota_q = np.arange(S, dtype=np.float32)[None, :]
    masks = np.zeros((4, 128, 512), dtype=np.float32)
    for a in range(4):
        kpos = a * 128 + np.arange(128)[:, None]
        qpos = np.arange(512)[None, :]
        masks[a][kpos > qpos] = NEG
    return ident, identf, iota_k, iota_q, masks.astype(ml_dtypes.bfloat16)


def kernel(L0_attn_a_x, L0_attn_a_Wq, L0_attn_a_Wk, L0_attn_a_Wv, L0_attn_a_Wp,
           L0_attn_a_bp, L0_attn_b_W_kqv, L0_attn_b_m, L0_ffn_w_gate,
           L0_ffn_w_up, L0_ffn_w_down):
    x = np.ascontiguousarray(np.asarray(L0_attn_a_x, dtype=np.float32))
    ident, identf, iota_k, iota_q, masks = make_host_constants()
    nc = build_nc(N_BLOCKS)
    common = {
        "wq": np.ascontiguousarray(L0_attn_a_Wq), "wk": np.ascontiguousarray(L0_attn_a_Wk),
        "wv": np.ascontiguousarray(L0_attn_a_Wv), "wp": np.ascontiguousarray(L0_attn_a_Wp),
        "bp": np.ascontiguousarray(np.asarray(L0_attn_a_bp).reshape(1, D)),
        "wkqv": np.ascontiguousarray(L0_attn_b_W_kqv),
        "m": np.ascontiguousarray(np.asarray(L0_attn_b_m).reshape(1, H)),
        "wg": np.ascontiguousarray(L0_ffn_w_gate), "wu": np.ascontiguousarray(L0_ffn_w_up),
        "wd": np.ascontiguousarray(L0_ffn_w_down),
        "ident": ident, "identf_in": identf, "iota_q": iota_q, "iota_k": iota_k,
        "masks": masks,
    }
    in_maps = []
    for c in range(8):
        bb, hh = c >> 1, c & 1
        mm = dict(common)
        mm["x"] = np.ascontiguousarray(x[bb, hh * TOK:(hh + 1) * TOK, :])
        mfull = np.asarray(L0_attn_b_m, dtype=np.float32).reshape(H)
        mm["m_my"] = np.ascontiguousarray(mfull[2 * c:2 * c + 2].reshape(1, 2))
        in_maps.append(mm)
    res = run_bass_kernel_spmd(nc, in_maps, core_ids=list(range(8)))
    out = np.empty((B, S, D), dtype=np.float32)
    for c in range(8):
        bb, hh = c >> 1, c & 1
        out[bb, hh * TOK:(hh + 1) * TOK, :] = res.results[c]["y"].reshape(TOK, D)
    return out
